# version 02
# baseline (speedup 1.0000x reference)
import contextlib
import numpy as np
import concourse.bass as bass
import concourse.mybir as mybir
from concourse.bass_utils import run_bass_kernel_spmd

F32 = mybir.dt.float32
BF16 = mybir.dt.bfloat16
AF = mybir.ActivationFunctionType
OP = mybir.AluOpType
AX = mybir.AxisListType.X

D = 1024
TOK = 2048
SEG = 1024
H, HD = 16, 64
CH = 32
NCHUNK = SEG // CH
NCORE = 8
NEGBIG = -1.0e30
SIM_SKIP_CC = False
DEBUG_NCH = 8


def _prep_consts():
    c = {}
    c["iden"] = np.eye(128, dtype=np.float32)
    m = np.triu(np.ones((32, 32), np.float32))
    c["mask64"] = np.ascontiguousarray(np.tile(np.tile(m, (1, 16)), (2, 1)))
    c["ivec"] = np.ascontiguousarray(
        np.tile(np.arange(1, CH + 1, dtype=np.float32)[None, :], (128, 1)))
    c["revseg"] = (SEG - 1.0 - np.arange(SEG, dtype=np.float32))[None, :]
    c["w31col2"] = (31.0 - (np.arange(64) % 32)).astype(np.float32)[None, :]
    c["hsel"] = np.ascontiguousarray(np.repeat(np.eye(16, dtype=np.float32), HD, axis=1))
    c["iota_pj"] = np.ascontiguousarray(
        np.arange(128, dtype=np.float32)[:, None] - np.arange(512, dtype=np.float32)[None, :])
    c["ones_1x"] = np.ones((1, 128), np.float32)
    return c


def _prep_core_consts(core):
    qA, qB = core, 15 - core
    segs = [(qA // 8, qA % 8), (qB // 8, qB % 8)]
    d = {}
    expo = np.full((2, 16), 1.0e6, np.float32)
    for s, (b, kpos) in enumerate(segs):
        for rank in range(8):
            for sp in range(2):
                q = rank if sp == 0 else 15 - rank
                bp, kp = q // 8, q % 8
                if bp == b and kp < kpos:
                    expo[s, 2 * rank + sp] = SEG * (kpos - 1 - kp)
    d["sin_nexpo"] = np.ascontiguousarray(np.tile(-expo.reshape(1, 32), (128, 1)))
    th = np.zeros((2, 8, 16), np.float32)
    for s, (b, kpos) in enumerate(segs):
        for k in range(8):
            g0 = kpos * SEG + k * 128
            for t in range(16):
                th[s, k, t] = t * 512 - g0
    d["attn_thresh"] = np.ascontiguousarray(np.tile(th.reshape(1, 256), (128, 1)))
    return d


def _prep_shared(inputs):
    g = lambda k: np.asarray(inputs[k], dtype=np.float32)
    s = {}
    s["wk"] = np.ascontiguousarray(g("L0_attn_Wk").T)
    s["wv"] = np.ascontiguousarray(g("L0_attn_Wv").T)
    s["wr"] = np.ascontiguousarray(g("L0_attn_Wr").T)
    s["wgm"] = np.ascontiguousarray(g("L0_attn_Wg").T)
    s["wo"] = np.ascontiguousarray(g("L0_attn_Wo").T)
    tm = g("L0_attn_time_mix").reshape(D)
    s["tm_col"] = np.ascontiguousarray(tm.reshape(8, 128).T)
    td = g("L0_attn_time_decay")
    s["td_col"] = np.ascontiguousarray(td.reshape(D).reshape(8, 128).T)
    s["td_row"] = np.ascontiguousarray(td.reshape(1, D))
    s["td_nat"] = np.ascontiguousarray(td)
    tf = g("L0_attn_time_first")
    s["tf_col"] = np.ascontiguousarray(tf.reshape(D).reshape(8, 128).T)
    s["m0_w1t"] = np.ascontiguousarray(g("L0_ffn_W1").transpose(0, 2, 1))
    s["m0_w2t"] = np.ascontiguousarray(g("L0_ffn_W2").transpose(0, 2, 1))
    s["m0_wg"] = np.ascontiguousarray(g("L0_ffn_Wg").T)
    s["m0_bg"] = np.ascontiguousarray(g("L0_ffn_bg").reshape(1, 8))
    s["m0_b1"] = np.ascontiguousarray(g("L0_ffn_b1").reshape(8, 16, 128).transpose(0, 2, 1))
    s["m0_b2"] = np.ascontiguousarray(g("L0_ffn_b2"))
    s["wqkv"] = np.ascontiguousarray(g("L1_attn_W_qkv"))
    s["rw"] = np.ascontiguousarray(g("L1_ffn_router_w"))
    s["m1_w1"] = np.ascontiguousarray(g("L1_ffn_w1"))
    s["m1_w2"] = np.ascontiguousarray(g("L1_ffn_w2"))
    s["m1_b1"] = np.ascontiguousarray(g("L1_ffn_b1").reshape(16, 16, 128).transpose(0, 2, 1))
    s["m1_b2"] = np.ascontiguousarray(g("L1_ffn_b2"))
    return s


def prep_in_maps(inputs):
    x = np.asarray(inputs["L0_attn_x"], dtype=np.float32)
    shared = _prep_shared(inputs)
    consts = _prep_consts()
    in_maps = []
    for c in range(NCORE):
        qA, qB = c, 15 - c
        shard = np.concatenate(
            [x[qA // 8, (qA % 8) * SEG:(qA % 8 + 1) * SEG],
             x[qB // 8, (qB % 8) * SEG:(qB % 8 + 1) * SEG]], axis=0)
        m = {"x": np.ascontiguousarray(shard)}
        m.update(shared)
        m.update(consts)
        m.update(_prep_core_consts(c))
        in_maps.append(m)
    return in_maps


def scatter_out(results):
    out = np.empty((2, 8192, D), dtype=np.float32)
    for c in range(NCORE):
        y = np.asarray(results[c]["out"])
        qA, qB = c, 15 - c
        out[qA // 8, (qA % 8) * SEG:(qA % 8 + 1) * SEG] = y[:SEG]
        out[qB // 8, (qB % 8) * SEG:(qB % 8 + 1) * SEG] = y[SEG:]
    return out



class SemCt:

    def __init__(self, es, nc, name):
        self.h = es.enter_context(nc.semaphore(name))
        self.n = 0

    def bump(self, k=1):
        self.n += k
        return self.h


class Ctx:
    def __init__(self, nc, es, taps=()):
        self.nc = nc
        self.es = es
        self.dram = {}
        self.kinds = {}
        self.taps = set(taps)
        self.uid = 0
        self.glob = {}
        self.sems = {}

    def name(self, stem):
        self.uid += 1
        return f"{stem}{self.uid}"

    def sem(self, key):
        if key not in self.sems:
            self.sems[key] = SemCt(self.es, self.nc, "s_" + key)
        return self.sems[key]

    def din(self, key, shape, dtype=F32):
        if key not in self.dram:
            self.dram[key] = self.nc.dram_tensor(key, shape, dtype, kind="ExternalInput")
            self.kinds[key] = "in"
        return self.dram[key]

    def dout(self, key, shape, dtype=F32):
        if key not in self.dram:
            self.dram[key] = self.nc.dram_tensor(key, shape, dtype, kind="ExternalOutput")
            self.kinds[key] = "out"
        return self.dram[key]

    def dtmp(self, key, shape, dtype):
        if key not in self.dram:
            if key in self.taps:
                self.dram[key] = self.nc.dram_tensor(key, shape, dtype, kind="ExternalOutput")
                self.kinds[key] = "out"
            else:
                self.dram[key] = self.nc.dram_tensor(key, shape, dtype)
                self.kinds[key] = "tmp"
        return self.dram[key]



def emit_setup(ctx):
    nc = ctx.nc
    g = ctx.glob
    mk = lambda n, sh, dt: ctx.es.enter_context(nc.sbuf_tensor(n, sh, dt))
    g["idb"] = mk("g_idb", [128, 128], BF16)
    g["td_col"] = mk("g_tdcol", [128, 8], F32)
    g["tm_col"] = mk("g_tmcol", [128, 8], F32)
    g["tf_col"] = mk("g_tfcol", [128, 8], F32)
    g["td_nat"] = mk("g_tdnat", [16, 64], F32)
    g["dhat"] = mk("g_dhat", [128, 8, CH], BF16)
    g["dneg"] = mk("g_dneg", [128, 8, CH], BF16)
    g["tfmap"] = mk("g_tfmap", [128, 8, CH], F32)
    g["dc32"] = mk("g_dc32", [128, D], F32)
    g["dbar32"] = mk("g_dbar32", [64, D], BF16)
    g["mask64"] = mk("g_mask", [64, 512], BF16)
    g["iota_pj"] = mk("g_iota", [128, 512], F32)
    g["thresh"] = mk("g_thr", [128, 256], F32)
    g["nexpo"] = mk("g_nexpo", [128, 32], F32)
    g["hsel"] = mk("g_hsel", [16, D], F32)
    g["ones1b"] = mk("g_ones1b", [1, 128], BF16)
    g["zrow"] = mk("g_zrow", [1, 512], BF16)

    with contextlib.ExitStack() as _es:
        idf = _es.enter_context(nc.sbuf_tensor("su_idf", [128, 128], F32))
        ivec = _es.enter_context(nc.sbuf_tensor("su_ivec", [128, CH], F32))
        tmp = _es.enter_context(nc.sbuf_tensor("su_t", [128, D], F32))
        mskf = _es.enter_context(nc.sbuf_tensor("su_m", [64, 512], F32))
        w31 = _es.enter_context(nc.sbuf_tensor("su_w31", [1, 64], F32))
        tdrow_l = _es.enter_context(nc.sbuf_tensor("su_tdrow", [1, D], F32))
        ones1 = _es.enter_context(nc.sbuf_tensor("su_ones", [1, 128], F32))
        ps = _es.enter_context(nc.psum_tensor("su_ps", [128, D], F32))
        pb = _es.enter_context(nc.psum_tensor("su_pb", [128, D], F32))
        s_ld = _es.enter_context(nc.semaphore("su_ld"))
        w31_host = ctx.din("w31col2", [1, 64])
        with nc.Block() as blk:
            @blk.sync
            def _(sy):
                n = [0]
                def ld(dst, key, shape):
                    sy.dma_start(dst, ctx.din(key, shape).ap()).then_inc(s_ld, 16)
                    n[0] += 16
                ld(idf[:], "iden", [128, 128])
                ld(g["td_col"][:], "td_col", [128, 8])
                ld(g["tm_col"][:], "tm_col", [128, 8])
                ld(g["tf_col"][:], "tf_col", [128, 8])
                ld(g["td_nat"][:], "td_nat", [16, 64])
                ld(tdrow_l[:], "td_row", [1, D])
                ld(g["iota_pj"][:], "iota_pj", [128, 512])
                ld(g["thresh"][:], "attn_thresh", [128, 256])
                ld(g["nexpo"][:], "sin_nexpo", [128, 32])
                ld(g["hsel"][:], "hsel", [16, D])
                ld(ivec[:], "ivec", [128, CH])
                ld(mskf[:], "mask64", [64, 512])
                ld(w31[:], "w31col2", [1, 64])
                ld(ones1[:], "ones_1x", [1, 128])
                sy.wait_ge(s_ld, n[0])
        with nc.Block() as blk:
            @blk.vector
            def _(v):
                v.memset(g["zrow"][:], 0.0)
                v.tensor_copy(g["idb"][:], idf[:])
                v.tensor_copy(g["ones1b"][:], ones1[:])
                v.tensor_copy(g["mask64"][:], mskf[:])
                for c in range(8):
                    v.tensor_scalar(tmp[:, c * CH:(c + 1) * CH], ivec[:],
                                    g["td_col"][:, c:c + 1], None, OP.mult)
                    v.tensor_scalar(g["tfmap"][:, c, :], ivec[:],
                                    0.0, g["tf_col"][:, c:c + 1], OP.mult, OP.add)
        with nc.Block() as blk:
            @blk.scalar
            def _(sc):
                for c in range(8):
                    sc.activation(g["dhat"][:, c, :], tmp[:, c * CH:(c + 1) * CH],
                                  AF.Exp, scale=-1.0)
                    sc.activation(g["dneg"][:, c, :], tmp[:, c * CH:(c + 1) * CH],
                                  AF.Exp, scale=1.0)
                sc.drain()
                sc.activation(tmp[0:16, 0:64], g["td_nat"][:], AF.Exp, scale=-32.0)
        with nc.Block() as blk:
            @blk.tensor
            def _(t):
                for hh in (0, 64):
                    for cc in (0, 512):
                        t.matmul(ps[hh:hh + 64, cc:cc + 512], tmp[0:16, 0:64],
                                 g["hsel"][:, cc:cc + 512], start=True, stop=True)
                for cc in (0, 512):
                    t.matmul(pb[0:64, cc:cc + 512], w31[:], tdrow_l[:, cc:cc + 512],
                             start=True, stop=True)
        with nc.Block() as blk:
            @blk.vector
            def _(v):
                v.tensor_copy(g["dc32"][:], ps[:])
            @blk.scalar
            def _(sc):
                sc.activation(g["dbar32"][:], pb[0:64, :], AF.Exp, scale=-1.0)



def emit_ln(ctx, src_hdl, dst_hdl):
    nc = ctx.nc
    idb = ctx.glob["idb"]
    with contextlib.ExitStack() as _es:
        xs = _es.enter_context(nc.sbuf_tensor(ctx.name("lx"), [128, 16, D], F32))
        sq = _es.enter_context(nc.sbuf_tensor(ctx.name("lq"), [128, D], F32))
        s1 = _es.enter_context(nc.sbuf_tensor(ctx.name("ls1"), [128, 16], F32))
        s2 = _es.enter_context(nc.sbuf_tensor(ctx.name("ls2"), [128, 16], F32))
        nmu = _es.enter_context(nc.sbuf_tensor(ctx.name("lmu"), [128, 16], F32))
        var = _es.enter_context(nc.sbuf_tensor(ctx.name("lva"), [128, 16], F32))
        lnv = _es.enter_context(nc.sbuf_tensor(ctx.name("llv"), [128, 16], F32))
        epsb = _es.enter_context(nc.sbuf_tensor(ctx.name("lep"), [128, 1], F32))
        rstd = _es.enter_context(nc.sbuf_tensor(ctx.name("lrs"), [128, 16], F32))
        xn = _es.enter_context(nc.sbuf_tensor(ctx.name("lxn"), [128, D], BF16))
        xnT = _es.enter_context(nc.sbuf_tensor(ctx.name("lxT"), [128, 2, 8, 128], BF16))
        psT = _es.enter_context(nc.psum_tensor(ctx.name("lp"), [128, 8, 128], BF16))
        s_ld = ctx.sem("ln_ld"); s_v = ctx.sem("ln_v")
        s_pe = ctx.sem("ln_pe"); s_ev = ctx.sem("ln_ev")
        s_o2 = [ctx.sem("ln_oA"), ctx.sem("ln_oB")]
        L, V, P, E = s_ld.n, s_v.n, s_pe.n, s_ev.n
        O2 = [x.n for x in s_o2]
        src = src_hdl.ap()
        dst = dst_hdl.ap()
        with nc.Block() as blk:
            @blk.sync
            def _(sy):
                for t in range(16):
                    sy.dma_start(xs[:, t, :], src[t * 128:(t + 1) * 128, :]).then_inc(s_ld.bump(16), 16)
            @blk.vector
            def _(v):
                v.memset(epsb[:], 1e-5)
                v.wait_ge(s_ld.h, L + 256)
                for t in range(16):
                    v.tensor_tensor(sq[:], xs[:, t, :], xs[:, t, :], OP.mult)
                    v.drain()
                    v.reduce_sum(s1[:, t:t + 1], xs[:, t, :], axis=AX)
                    v.reduce_sum(s2[:, t:t + 1], sq[:], axis=AX)
                    v.drain()
                v.tensor_scalar(nmu[:], s1[:], -1.0 / D, None, OP.mult)
                v.tensor_scalar(var[:], s2[:], 1.0 / D, None, OP.mult)
                v.drain()
                v.tensor_tensor(s2[:], nmu[:], nmu[:], OP.mult)
                v.drain()
                v.tensor_tensor(var[:], var[:], s2[:], OP.subtract).then_inc(s_v.bump(), 1)
            @blk.scalar
            def _(sc):
                sc.wait_ge(s_v.h, V + 1)
                sc.activation(lnv[:], var[:], AF.Ln, bias=epsb[:])
                sc.drain()
                sc.activation(rstd[:], lnv[:], AF.Exp, scale=-0.5).then_inc(s_v.bump(), 1)
        with nc.Block() as blk:
            @blk.vector
            def _(v):
                for t in range(16):
                    v.tensor_scalar(xn[:], xs[:, t, :], nmu[:, t:t + 1],
                                    rstd[:, t:t + 1], OP.add, OP.mult).then_inc(s_v.bump(), 1)
                    v.wait_ge(s_pe.h, P + 8 * (t + 1))
                    if t >= 2:
                        v.wait_ge(s_o2[t % 2].h, O2[t % 2] + 16 * (t // 2))
                    v.tensor_copy(xnT[:, t % 2, :, :], psT[:]).then_inc(s_ev.bump(), 1)
            @blk.tensor
            def _(t_):
                for t in range(16):
                    t_.wait_ge(s_v.h, V + 2 + t + 1)
                    if t > 0:
                        t_.wait_ge(s_ev.h, E + t)
                    for cc in range(8):
                        t_.transpose(psT[:, cc, :], xn[:, cc * 128:(cc + 1) * 128],
                                     idb[:]).then_inc(s_pe.bump(), 1)
            @blk.sync
            def _(sy):
                for t in range(16):
                    sy.wait_ge(s_ev.h, E + t + 1)
                    sy.dma_start(dst[:, :, t * 128:(t + 1) * 128].rearrange("c p j -> p c j"),
                                 xnT[:, t % 2, :, :]).then_inc(s_o2[t % 2].bump(16), 16)
                sy.wait_ge(s_o2[0].h, O2[0] + 128)
                sy.wait_ge(s_o2[1].h, O2[1] + 128)



def emit_mixer_a(ctx, parts="all"):
    nc = ctx.nc
    g = ctx.glob
    xn_fm = ctx.dtmp("xn_fm", [8, 128, TOK], BF16)
    ktm_d = ctx.dtmp("ktm", [2, 64, 16, D], BF16)
    vtm_d = ctx.dtmp("vtm", [2, 64, 16, D], BF16)
    cc0s = ctx.dtmp("cc0src", [128, D], F32)
    cc0d = ctx.dtmp("cc0dst", [NCORE * 128, D], F32)
    with contextlib.ExitStack() as _es:
        wsb = _es.enter_context(nc.sbuf_tensor("ma_w", [128, 8, 2048], BF16))
        ktm = _es.enter_context(nc.sbuf_tensor("ma_ktm", [64, 16, D], BF16))
        vtm = _es.enter_context(nc.sbuf_tensor("ma_vtm", [64, 16, D], BF16))
        xnt = _es.enter_context(nc.sbuf_tensor("ma_xn", [128, 2, 8, 128], BF16))
        dbu = _es.enter_context(nc.sbuf_tensor("ma_dbu", [64, D], BF16))
        tdrow_a = _es.enter_context(nc.sbuf_tensor("ma_tdrow", [1, D], F32))
        rev_a = _es.enter_context(nc.sbuf_tensor("ma_rev", [1, SEG], F32))
        kbu = _es.enter_context(nc.sbuf_tensor("ma_kbu", [64, D], BF16))
        usb = _es.enter_context(nc.sbuf_tensor("ma_u", [64, D], F32))
        pj0 = _es.enter_context(nc.psum_tensor("ma_p0", [128, D], F32))
        pj1 = _es.enter_context(nc.psum_tensor("ma_p1", [128, D], F32))
        pdb = _es.enter_context(nc.psum_tensor("ma_pd", [128, D], F32))
        pu = _es.enter_context(nc.psum_tensor("ma_pu", [128, D], F32))
        wk = ctx.din("wk", [D, D]); wv = ctx.din("wv", [D, D])
        s_lw = ctx.sem("ma_lw")
        with nc.Block() as blk:
            LW = s_lw.n
            @blk.gpsimd
            def _(gp):
                gp.dma_start(tdrow_a[:], ctx.din("td_row", [1, D]).ap()).then_inc(s_lw.bump(16), 16)
                gp.dma_start(rev_a[:], ctx.din("revseg", [1, SEG]).ap()).then_inc(s_lw.bump(16), 16)
                for dc in range(8):
                    gp.dma_start(wsb[:, dc, 0:D], wk.ap()[dc * 128:(dc + 1) * 128, :]).then_inc(s_lw.bump(16), 16)
                    gp.dma_start(wsb[:, dc, D:2 * D], wv.ap()[dc * 128:(dc + 1) * 128, :]).then_inc(s_lw.bump(16), 16)
            @blk.vector
            def _(v):
                v.wait_ge(s_lw.h, LW + 288)
                for dc in range(8):
                    v.tensor_scalar(wsb[:, dc, :], wsb[:, dc, :],
                                    g["tm_col"][:, dc:dc + 1], None, OP.mult)
        for s in range(2):
            s_lx = [ctx.sem("ma_lxA"), ctx.sem("ma_lxB")]
            s_pe = ctx.sem("ma_pe"); s_ev = ctx.sem("ma_ev")
            with nc.Block() as blk:
                LX = [x.n for x in s_lx]
                PE, EV = s_pe.n, s_ev.n
                @blk.sync
                def _(sy):
                    for tc in range(8):
                        if tc >= 2:
                            sy.wait_ge(s_pe.h, PE + 4 * (tc - 1))
                        sy.dma_start(
                            xnt[:, tc % 2, :, :],
                            xn_fm.ap()[:, :, s * SEG + tc * 128: s * SEG + (tc + 1) * 128]
                            .rearrange("c p j -> p c j")).then_inc(s_lx[tc % 2].bump(16), 16)
                @blk.tensor
                def _(t_):
                    i = 0
                    for tc in range(8):
                        t_.wait_ge(s_lx[tc % 2].h, LX[tc % 2] + 16 * (tc // 2 + 1))
                        for w in range(2):
                            for jg in range(2):
                                ps = (pj0, pj1)[i % 2]
                                if i >= 2:
                                    t_.wait_ge(s_ev.h, EV + i - 1)
                                last = None
                                for cc2 in range(2):
                                    for dc in range(8):
                                        last = t_.matmul(ps[0:64, cc2 * 512:(cc2 + 1) * 512],
                                                  xnt[:, tc % 2, dc, jg * 64:(jg + 1) * 64],
                                                  wsb[:, dc, w * D + cc2 * 512:w * D + (cc2 + 1) * 512],
                                                  start=(dc == 0), stop=(dc == 7))
                                last.then_inc(s_pe.bump(), 1)
                                i += 1
                @blk.vector
                def _(v):
                    i = 0
                    for tc in range(8):
                        for w in range(2):
                            dstb = (ktm, vtm)[w]
                            for jg in range(2):
                                ps = (pj0, pj1)[i % 2]
                                v.wait_ge(s_pe.h, PE + i + 1)
                                v.tensor_copy(dstb[:, 2 * tc + jg, :], ps[0:64, :]).then_inc(s_ev.bump(), 1)
                                i += 1
            if parts == "prod":
                continue
            s_db = ctx.sem("ma_db"); s_ex = ctx.sem("ma_ex"); s_kb = ctx.sem("ma_kb")
            s_up = ctx.sem("ma_up"); s_sp = ctx.sem("ma_sp")
            with nc.Block() as blk:
                DB, EX, KB, UP, SP = s_db.n, s_ex.n, s_kb.n, s_up.n, s_sp.n
                @blk.tensor
                def _(t_):
                    for gr in range(16):
                        if gr > 0:
                            t_.wait_ge(s_ex.h, EX + gr)
                        last = None
                        for cc2 in range(2):
                            last = t_.matmul(pdb[0:64, cc2 * 512:(cc2 + 1) * 512],
                                      rev_a[:, 64 * gr:64 * gr + 64],
                                      tdrow_a[:, cc2 * 512:(cc2 + 1) * 512],
                                      start=True, stop=True)
                        last.then_inc(s_db.bump(), 1)
                        t_.wait_ge(s_kb.h, KB + gr + 1)
                        last = None
                        for n2 in range(2):
                            n = 2 * gr + n2
                            base = 32 * n2
                            for h in range(H):
                                last = t_.matmul(pu[0:64, h * 64:(h + 1) * 64],
                                          kbu[base:base + 32, h * 64:(h + 1) * 64],
                                          vtm[base:base + 32, gr, h * 64:(h + 1) * 64],
                                          start=(n == 0 and h in (0, 8)),
                                          stop=(n == NCHUNK - 1 and h in (7, 15)))
                        last.then_inc(s_up.bump(), 1)
                @blk.scalar
                def _(sc):
                    for gr in range(16):
                        sc.wait_ge(s_db.h, DB + gr + 1)
                        if gr > 0:
                            sc.wait_ge(s_kb.h, KB + gr)
                        sc.activation(dbu[:], pdb[0:64, :], AF.Exp, scale=-1.0).then_inc(s_ex.bump(), 1)
                @blk.vector
                def _(v):
                    for gr in range(16):
                        v.wait_ge(s_ex.h, EX + gr + 1)
                        if gr > 0:
                            v.wait_ge(s_up.h, UP + gr)
                        v.tensor_tensor(kbu[:], ktm[:, gr, :], dbu[:], OP.mult).then_inc(s_kb.bump(), 1)
                    v.wait_ge(s_up.h, UP + 16)
                    v.tensor_copy(usb[:], pu[0:64, :]).then_inc(s_kb.bump(), 1)
                @blk.sync
                def _(sy):
                    sy.wait_ge(s_kb.h, KB + 17)
                    sy.dma_start(cc0s.ap()[s * 64:(s + 1) * 64, :], usb[:]).then_inc(s_sp.bump(16), 16)
                    if "cc0dbg" in ctx.taps:
                        dbg = ctx.dtmp("cc0dbg", [128, D], F32)
                        sy.dma_start(dbg.ap()[s * 64:(s + 1) * 64, :], usb[:]).then_inc(s_sp.bump(16), 16)
                    sy.dma_start(ktm_d.ap()[s], ktm[:]).then_inc(s_sp.bump(16), 16)
                    sy.dma_start(vtm_d.ap()[s], vtm[:]).then_inc(s_sp.bump(16), 16)
                    sy.wait_ge(s_sp.h, s_sp.n)
        if parts == "nocc" or SIM_SKIP_CC:
            ctx.sem("cc0").bump()
            return
        s_cc0 = ctx.sem("cc0")
        with nc.Block() as blk:
            @blk.gpsimd
            def _(gp):
                gp.collective_compute(
                    "AllGather", OP.bypass, replica_groups=[list(range(NCORE))],
                    ins=[ctx.dram["cc0src"].ap().opt()],
                    outs=[ctx.dram["cc0dst"].ap().opt()]).then_inc(s_cc0.bump(), 1)



def emit_moe(ctx, layer):
    nc = ctx.nc
    g = ctx.glob
    E = 8 if layer == 0 else 16
    act = AF.Relu if layer == 0 else AF.Gelu
    xn_fm = ctx.dtmp("xn_fm", [8, 128, TOK], BF16)
    g_d = ctx.dtmp(f"g{layer}", [128, 16, D], BF16)
    if layer == 0:
        w1 = ctx.din("m0_w1t", [E, D, 2048]); w2 = ctx.din("m0_w2t", [E, 2048, D])
        b1 = ctx.din("m0_b1", [E, 128, 16]); b2 = ctx.din("m0_b2", [E, D])
        wg_h = ctx.din("m0_wg", [D, E])
    else:
        w1 = ctx.din("m1_w1", [E, D, 2048]); w2 = ctx.din("m1_w2", [E, 2048, D])
        b1 = ctx.din("m1_b1", [E, 128, 16]); b2 = ctx.din("m1_b2", [E, D])
        wg_h = ctx.din("rw", [D, E])
    with contextlib.ExitStack() as _es:
        comb = _es.enter_context(nc.sbuf_tensor(ctx.name("mcb"), [128, 16, E], F32))
        lgall = _es.enter_context(nc.sbuf_tensor(ctx.name("mlg"), [128, 16, E], F32))
        m1n = _es.enter_context(nc.sbuf_tensor(ctx.name("mm1"), [128, 16], F32))
        m2a = _es.enter_context(nc.sbuf_tensor(ctx.name("mm2"), [128, 16], F32))
        em2 = _es.enter_context(nc.sbuf_tensor(ctx.name("mem"), [128, 16], F32))
        aux = _es.enter_context(nc.sbuf_tensor(ctx.name("mau"), [128, 16], F32))
        bgb = _es.enter_context(nc.sbuf_tensor(ctx.name("mgb"), [128, E], F32))
        xng = _es.enter_context(nc.sbuf_tensor(ctx.name("mxg"), [128, 2, 8, 128], BF16))
        bgsb = _es.enter_context(nc.sbuf_tensor(ctx.name("mbg16"), [1, 8], BF16))
        wgsb = _es.enter_context(nc.sbuf_tensor(ctx.name("mwg"), [128, 8, E], BF16))
        gacc = _es.enter_context(nc.sbuf_tensor(ctx.name("mga"), [128, 16, D], F32))
        pga = _es.enter_context(nc.psum_tensor(ctx.name("mpga"), [128, 512], F32))
        pgb = _es.enter_context(nc.psum_tensor(ctx.name("mpgb"), [128, 512], F32))
        s_l = ctx.sem("mo_l"); s_gp = ctx.sem("mo_gp"); s_gv = ctx.sem("mo_gv")
        s_gx = [ctx.sem("mo_gxA"), ctx.sem("mo_gxB")]
        with nc.Block() as blk:
            L0_, GP, GV = s_l.n, s_gp.n, s_gv.n
            GX = [x.n for x in s_gx]
            @blk.gpsimd
            def _(gp_):
                for dc in range(8):
                    gp_.dma_start(wgsb[:, dc, :], wg_h.ap()[dc * 128:(dc + 1) * 128, :]).then_inc(s_l.bump(16), 16)
                if layer == 0:
                    gp_.dma_start(bgsb[:], ctx.din("m0_bg", [1, 8]).ap()).then_inc(s_l.bump(16), 16)
            @blk.sync
            def _(sy):
                for tc in range(16):
                    if tc >= 2:
                        sy.wait_ge(s_gp.h, GP + tc - 1)
                    sy.dma_start(
                        xng[:, tc % 2, :, :],
                        xn_fm.ap()[:, :, tc * 128:(tc + 1) * 128].rearrange("c p j -> p c j")
                    ).then_inc(s_gx[tc % 2].bump(16), 16)
            @blk.tensor
            def _(t_):
                t_.wait_ge(s_l.h, L0_ + 16 * (9 if layer == 0 else 8))
                for tc in range(16):
                    t_.wait_ge(s_gx[tc % 2].h, GX[tc % 2] + 16 * (tc // 2 + 1))
                    if tc >= 2:
                        t_.wait_ge(s_gv.h, GV + tc - 1)
                    pgx = (pga, pgb)[tc % 2]
                    last = None
                    for dc in range(8):
                        last = t_.matmul(pgx[:, 0:E], xng[:, tc % 2, dc, :], wgsb[:, dc, :],
                                  start=(dc == 0), stop=(dc == 7))
                    last.then_inc(s_gp.bump(), 1)
            @blk.vector
            def _(v):
                for tc in range(16):
                    v.wait_ge(s_gp.h, GP + tc + 1)
                    v.tensor_copy(lgall[:, tc, :], (pga, pgb)[tc % 2][:, 0:E]).then_inc(s_gv.bump(), 1)
                for t in range(16):
                    v.memset(gacc[:, t, :], 0.0)
        if layer == 0:
            with nc.Block() as blk:
                @blk.tensor
                def _(t_):
                    t_.matmul(pga[:, 0:E], g["ones1b"][:], bgsb[:], start=True, stop=True)
            with nc.Block() as blk:
                @blk.vector
                def _(v):
                    v.tensor_copy(bgb[:], pga[:, 0:E])
                    v.drain()
                    for tc in range(16):
                        v.tensor_tensor(lgall[:, tc, :], lgall[:, tc, :], bgb[:], OP.add)
        with nc.Block() as blk:
            @blk.vector
            def _(v):
                for tc in range(16):
                    v.reduce_max(m1n[:, tc:tc + 1], lgall[:, tc, :], axis=AX, negate=True)
                v.drain()
                if layer == 0:
                    for tc in range(16):
                        v.tensor_scalar(aux[:, tc:tc + 1], m1n[:, tc:tc + 1], -1.0, None, OP.mult)
                    v.drain()
                    for tc in range(16):
                        v.tensor_scalar(comb[:, tc, :], lgall[:, tc, :], aux[:, tc:tc + 1], None, OP.is_ge)
                        v.drain()
                        v.scalar_tensor_tensor(comb[:, tc, :], comb[:, tc, :], NEGBIG,
                                               lgall[:, tc, :], OP.mult, OP.add)
                        v.drain()
                        v.reduce_max(m2a[:, tc:tc + 1], comb[:, tc, :], axis=AX)
                        v.drain()
                        v.tensor_scalar(comb[:, tc, :], lgall[:, tc, :], m2a[:, tc:tc + 1],
                                        None, OP.is_ge)
                else:
                    for tc in range(16):
                        v.tensor_scalar(aux[:, tc:tc + 1], m1n[:, tc:tc + 1], -1.0, None, OP.mult)
                        v.drain()
                        v.tensor_scalar(comb[:, tc, :], lgall[:, tc, :], aux[:, tc:tc + 1],
                                        None, OP.is_ge)
        with nc.Block() as blk:
            @blk.scalar
            def _(sc):
                if layer == 0:
                    for tc in range(16):
                        sc.activation(lgall[:, tc, :], lgall[:, tc, :], AF.Exp,
                                      bias=m1n[:, tc:tc + 1])
                        sc.activation(em2[:, tc:tc + 1], m2a[:, tc:tc + 1], AF.Exp,
                                      bias=m1n[:, tc:tc + 1])
                else:
                    for tc in range(16):
                        sc.activation(em2[:, tc:tc + 1], m1n[:, tc:tc + 1], AF.Exp)
        with nc.Block() as blk:
            @blk.vector
            def _(v):
                v.tensor_scalar(aux[:], em2[:], 1.0, None, OP.add)
                v.drain()
                v.reciprocal(em2[:], aux[:])
                v.drain()
                if layer == 0:
                    for tc in range(16):
                        v.tensor_tensor(comb[:, tc, :], comb[:, tc, :], lgall[:, tc, :], OP.mult)
                        v.drain()
                        v.tensor_scalar(comb[:, tc, :], comb[:, tc, :], em2[:, tc:tc + 1],
                                        None, OP.mult)
                        v.drain()
                else:
                    for tc in range(16):
                        v.tensor_scalar(comb[:, tc, :], comb[:, tc, :], em2[:, tc:tc + 1],
                                        None, OP.mult)
        with contextlib.ExitStack() as _es:
            w1sb = _es.enter_context(nc.sbuf_tensor(ctx.name("mw1"), [128, 8, 2048], BF16))
            w2sb = _es.enter_context(nc.sbuf_tensor(ctx.name("mw2"), [128, 16, D], BF16))
            hsb = _es.enter_context(nc.sbuf_tensor(ctx.name("mh"), [128, 16, 512], BF16))
            xnt = _es.enter_context(nc.sbuf_tensor(ctx.name("mx5"), [128, 2, 8, 512], BF16))
            b1sb = _es.enter_context(nc.sbuf_tensor(ctx.name("mb1"), [128, 2, 16], F32))
            b2sb = _es.enter_context(nc.sbuf_tensor(ctx.name("mb2"), [1, 2, D], BF16))
            ph = _es.enter_context(nc.psum_tensor(ctx.name("mph"), [128, 2, 512], F32))
            po = _es.enter_context(nc.psum_tensor(ctx.name("mpo"), [128, 2, 512], F32))
            s_w1 = ctx.sem("mo_w1"); s_w2 = ctx.sem("mo_w2")
            s_b1s = ctx.sem("mo_b1"); s_b2s = ctx.sem("mo_b2")
            s_xn = [ctx.sem("mo_xnA"), ctx.sem("mo_xnB")]
            s_hp = ctx.sem("mo_hp"); s_ha = ctx.sem("mo_ha")
            s_ht = ctx.sem("mo_ht"); s_op = ctx.sem("mo_op"); s_mac = ctx.sem("mo_mac")
            s_ot = ctx.sem("mo_ot")
            with nc.Block() as blk:
                W1, W2, HP, HA, OPM, MAC = (
                    s_w1.n, s_w2.n, s_hp.n, s_ha.n, s_op.n, s_mac.n)
                B1, B2 = s_b1s.n, s_b2s.n
                XN = [x.n for x in s_xn]
                @blk.gpsimd
                def _(gp_):
                    for e in range(E):
                        if e >= 1:
                            gp_.wait_ge(s_hp.h, HP + 64 * e)
                        for dc in range(8):
                            gp_.dma_start(w1sb[:, dc, :], w1.ap()[e, dc * 128:(dc + 1) * 128, :]
                                          ).then_inc(s_w1.bump(16), 16)
                        if e >= 2:
                            gp_.wait_ge(s_op.h, OPM + 32 * (e - 1))
                        gp_.dma_start(b1sb[:, e % 2, :], b1.ap()[e]).then_inc(s_b1s.bump(16), 16)
                        gp_.dma_start(b2sb[0:1, e % 2, :], b2.ap()[e:e + 1, :]).then_inc(s_b2s.bump(16), 16)
                        if e >= 1:
                            gp_.wait_ge(s_op.h, OPM + 32 * e)
                        for fc in range(16):
                            gp_.dma_start(w2sb[:, fc, :], w2.ap()[e, fc * 128:(fc + 1) * 128, :]
                                          ).then_inc(s_w2.bump(16), 16)
                @blk.sync
                def _(sy):
                    i = 0
                    for e in range(E):
                        for hf in range(2):
                            for t5 in range(2):
                                if i >= 2:
                                    sy.wait_ge(s_hp.h, HP + 16 * (i - 1))
                                sy.dma_start(
                                    xnt[:, i % 2, :, :],
                                    xn_fm.ap()[:, :, hf * 1024 + t5 * 512:hf * 1024 + (t5 + 1) * 512]
                                    .rearrange("c p j -> p c j")).then_inc(s_xn[i % 2].bump(16), 16)
                                i += 1
                @blk.tensor
                def _(t_):
                    i = 0
                    for e in range(E):
                        for hf in range(2):
                            for t5 in range(2):
                                t_.wait_ge(s_w1.h, W1 + 128 * (e + 1))
                                t_.wait_ge(s_xn[i % 2].h, XN[i % 2] + 16 * (i // 2 + 1))
                                for fc in range(16):
                                    gi = (i * 16 + fc)
                                    if gi >= 2:
                                        t_.wait_ge(s_ha.h, HA + gi - 1)
                                    last = None
                                    for dc in range(8):
                                        last = t_.matmul(ph[:, gi % 2, :], w1sb[:, dc, fc * 128:(fc + 1) * 128],
                                                  xnt[:, i % 2, dc, :], start=(dc == 0), stop=(dc == 7))
                                    last.then_inc(s_hp.bump(), 1)
                                t_.wait_ge(s_w2.h, W2 + 256 * (e + 1))
                                t_.wait_ge(s_b2s.h, B2 + 16 * (e + 1))
                                t_.wait_ge(s_ha.h, HA + 16 * (i + 1))
                                for tloc in range(4):
                                    for ot in range(2):
                                        oi = i * 8 + tloc * 2 + ot
                                        if oi >= 2:
                                            t_.wait_ge(s_mac.h, MAC + oi - 1)
                                        for fc in range(16):
                                            t_.matmul(po[:, oi % 2, :], hsb[:, fc, tloc * 128:(tloc + 1) * 128],
                                                      w2sb[:, fc, ot * 512:(ot + 1) * 512],
                                                      start=(fc == 0), stop=False)
                                        t_.matmul(po[:, oi % 2, :], g["ones1b"][:],
                                                  b2sb[0:1, e % 2, ot * 512:(ot + 1) * 512],
                                                  start=False, stop=True).then_inc(s_op.bump(), 1)
                                i += 1
                @blk.scalar
                def _(sc):
                    i = 0
                    for e in range(E):
                        for hf in range(2):
                            for t5 in range(2):
                                for fc in range(16):
                                    gi = i * 16 + fc
                                    sc.wait_ge(s_hp.h, HP + gi + 1)
                                    if hf == 0 and t5 == 0 and fc == 0:
                                        sc.wait_ge(s_b1s.h, B1 + 16 * (e + 1))
                                    if i >= 1 and fc == 0:
                                        sc.wait_ge(s_op.h, OPM + 8 * i)
                                    sc.activation(hsb[:, fc, :],
                                                  ph[:, gi % 2, :], act,
                                                  bias=b1sb[:, e % 2, fc:fc + 1]).then_inc(s_ha.bump(), 1)
                                i += 1
                @blk.vector
                def _(v):
                    i = 0
                    for e in range(E):
                        for hf in range(2):
                            for t5 in range(2):
                                for tloc in range(4):
                                    for ot in range(2):
                                        oi = i * 8 + tloc * 2 + ot
                                        tg = hf * 8 + t5 * 4 + tloc
                                        v.wait_ge(s_op.h, OPM + oi + 1)
                                        v.scalar_tensor_tensor(
                                            gacc[:, tg, ot * 512:(ot + 1) * 512],
                                            po[:, oi % 2, :], comb[:, tg, e:e + 1],
                                            gacc[:, tg, ot * 512:(ot + 1) * 512],
                                            OP.mult, OP.add).then_inc(s_mac.bump(), 1)
                                i += 1
            with (nc.sbuf_tensor(ctx.name("mgs"), [128, 2, D], BF16) as gsb,):
                s_sg = ctx.sem("mo_sg")
                s_go = [ctx.sem("mo_goA"), ctx.sem("mo_goB")]
                with nc.Block() as blk:
                    SG = s_sg.n
                    GO = [x.n for x in s_go]
                    @blk.scalar
                    def _(sc):
                        for t in range(16):
                            if t >= 2:
                                sc.wait_ge(s_go[t % 2].h, GO[t % 2] + 16 * (t // 2))
                            sc.activation(gsb[:, t % 2, :], gacc[:, t, :], AF.Silu).then_inc(s_sg.bump(), 1)
                    @blk.sync
                    def _(sy):
                        for t in range(16):
                            sy.wait_ge(s_sg.h, SG + t + 1)
                            sy.dma_start(g_d.ap()[:, t, :], gsb[:, t % 2, :]).then_inc(s_go[t % 2].bump(16), 16)
                        sy.wait_ge(s_go[0].h, GO[0] + 128)
                        sy.wait_ge(s_go[1].h, GO[1] + 128)


def emit_mixer_b(ctx, parts="all"):
    nc = ctx.nc
    g = ctx.glob
    x_in = ctx.din("x", [TOK, D])
    xn_fm = ctx.dtmp("xn_fm", [8, 128, TOK], BF16)
    ktm_d = ctx.dtmp("ktm", [2, 64, 16, D], BF16)
    vtm_d = ctx.dtmp("vtm", [2, 64, 16, D], BF16)
    cc0d = ctx.dtmp("cc0dst", [NCORE * 128, D], F32)
    g0_d = ctx.dtmp("g0", [128, 16, D], BF16)
    x1_d = ctx.dtmp("x1", [TOK, D], F32)
    wk = ctx.din("wk", [D, D]); wv = ctx.din("wv", [D, D])
    wr = ctx.din("wr", [D, D]); wgm = ctx.din("wgm", [D, D])
    wo = ctx.din("wo", [D, D])
    with contextlib.ExitStack() as mbes:
        ec_ = mbes.enter_context
        ktmq = ec_(nc.sbuf_tensor("mb_ktq", [64, 2, 4, D], BF16))
        vtmq = ec_(nc.sbuf_tensor("mb_vtq", [64, 2, 4, D], BF16))
        kfm = ec_(nc.sbuf_tensor("mb_kfm", [128, 8, SEG], BF16))
        vfm = ec_(nc.sbuf_tensor("mb_vfm", [128, 8, SEG], BF16))
        wbuf = ec_(nc.sbuf_tensor("mb_w", [128, 8, 2048], BF16))
        wosb = ec_(nc.sbuf_tensor("mb_wo", [128, 8, D], BF16))
        rq = ec_(nc.sbuf_tensor("mb_rq", [128, 8, 256], BF16))
        gq = ec_(nc.sbuf_tensor("mb_gq", [128, 8, 256], BF16))
        xnq = ec_(nc.sbuf_tensor("mb_xq", [128, 2, 8, 256], BF16))
        khb = ec_(nc.sbuf_tensor("mb_kh", [128, 2, 8, CH], BF16))
        ktb = ec_(nc.sbuf_tensor("mb_kt", [128, 2, 8, CH], BF16))
        kb = ec_(nc.sbuf_tensor("mb_kb", [64, D], BF16))
        am = ec_(nc.sbuf_tensor("mb_am", [64, 512], BF16))
        S = ec_(nc.sbuf_tensor("mb_S", [128, D], F32))
        S2 = ec_(nc.sbuf_tensor("mb_S2", [128, D], BF16))
        St = ec_(nc.sbuf_tensor("mb_St", [128, D], F32))
        tw = ec_(nc.sbuf_tensor("mb_tw", [128, 8, CH], F32))
        ws = ec_(nc.sbuf_tensor("mb_ws", [128, 8, CH], F32))
        rp = ec_(nc.sbuf_tensor("mb_rp", [128, 8, 256], BF16))
        usl = ec_(nc.sbuf_tensor("mb_us", [128, D], F32))
        xres = ec_(nc.sbuf_tensor("mb_xr", [128, 2, D], F32))
        gres = ec_(nc.sbuf_tensor("mb_gr", [128, 2, D], BF16))
        x1sb = ec_(nc.sbuf_tensor("mb_x1", [128, 2, D], F32))
        cfn = ec_(nc.sbuf_tensor("mb_cn", [16, 2, 64], F32))
        for s in range(2):
            s_lw = ctx.sem("mb_lw")
            with nc.Block() as blk:
                LW = s_lw.n
                @blk.gpsimd
                def _(gp_):
                    for dc in range(8):
                        gp_.dma_start(wbuf[:, dc, 0:D], wk.ap()[dc * 128:(dc + 1) * 128, :]).then_inc(s_lw.bump(16), 16)
                        gp_.dma_start(wbuf[:, dc, D:2 * D], wv.ap()[dc * 128:(dc + 1) * 128, :]).then_inc(s_lw.bump(16), 16)
                        gp_.dma_start(wosb[:, dc, :], wo.ap()[dc * 128:(dc + 1) * 128, :]).then_inc(s_lw.bump(16), 16)
                @blk.vector
                def _(v):
                    v.wait_ge(s_lw.h, LW + 16 * 24)
                    for dc in range(8):
                        v.tensor_scalar(wbuf[:, dc, :], wbuf[:, dc, :],
                                        g["tm_col"][:, dc:dc + 1], None, OP.mult)
            s_xf = [ctx.sem("mb_xfA"), ctx.sem("mb_xfB")]
            s_fp = ctx.sem("mb_fp"); s_fe = ctx.sem("mb_fe")
            with contextlib.ExitStack() as pes1:
                pfa = pes1.enter_context(nc.psum_tensor(ctx.name("mbpfa"), [128, 512], F32))
                pfb = pes1.enter_context(nc.psum_tensor(ctx.name("mbpfb"), [128, 512], F32))
                with nc.Block() as blk:
                    FP, FE = s_fp.n, s_fe.n
                    XF = [x.n for x in s_xf]
                    @blk.sync
                    def _(sy):
                        for qq in range(4):
                            if qq >= 2:
                                sy.wait_ge(s_fp.h, FP + 16 * (qq - 1))
                            sy.dma_start(
                                xnq[:, qq % 2, :, :],
                                xn_fm.ap()[:, :, s * SEG + qq * 256:s * SEG + (qq + 1) * 256]
                                .rearrange("c p j -> p c j")).then_inc(s_xf[qq % 2].bump(16), 16)
                    @blk.tensor
                    def _(t_):
                        i = 0
                        for qq in range(4):
                            t_.wait_ge(s_xf[qq % 2].h, XF[qq % 2] + 16 * (qq // 2 + 1))
                            for w in range(2):
                                for ec in range(8):
                                    if i >= 2:
                                        t_.wait_ge(s_fe.h, FE + i - 1)
                                    last = None
                                    for dc in range(8):
                                        last = t_.matmul((pfa, pfb)[i % 2][:, 0:256],
                                                  wbuf[:, dc, w * D + ec * 128:w * D + (ec + 1) * 128],
                                                  xnq[:, qq % 2, dc, :],
                                                  start=(dc == 0), stop=(dc == 7))
                                    last.then_inc(s_fp.bump(), 1)
                                    i += 1
                    @blk.vector
                    def _(v):
                        i = 0
                        for qq in range(4):
                            for w in range(2):
                                dstb = (kfm, vfm)[w]
                                for ec in range(8):
                                    v.wait_ge(s_fp.h, FP + i + 1)
                                    v.tensor_copy(dstb[:, ec, qq * 256:(qq + 1) * 256],
                                                  (pfa, pfb)[i % 2][:, 0:256]).then_inc(s_fe.bump(), 1)
                                    i += 1
            if parts == "m1":
                continue
            s_cs = [ctx.sem("mb_csA"), ctx.sem("mb_csB")]
            s_cp = ctx.sem("mb_cp"); s_cv = ctx.sem("mb_cv")
            s_ca = ctx.sem("mb_ca")
            with contextlib.ExitStack() as pes2:
                pco = pes2.enter_context(nc.psum_tensor(ctx.name("mbpc"), [128, D], F32))
                with nc.Block() as blk:
                    CP, CV, CA = s_cp.n, s_cv.n, s_ca.n
                    CS = [x.n for x in s_cs]
                    s_cc0 = ctx.sem("cc0")
                    @blk.vector
                    def _(v):
                        v.memset(S[:], 0.0)
                        for sl in range(16):
                            v.wait_ge(s_cs[0].h, CS[0] + 32 * (sl + 1))
                            v.wait_ge(s_cp.h, CP + sl + 1)
                            v.tensor_tensor(St[:], pco[:], usl[:], OP.mult)
                            v.drain()
                            v.tensor_tensor(S[:], S[:], St[:], OP.add).then_inc(s_cv.bump(), 1)
                            v.drain()
                        v.tensor_copy(S2[:], S[:])
                    @blk.sync
                    def _(sy):
                        if not SIM_SKIP_CC:
                            sy.wait_ge(s_cc0.h, 1)
                        for sl in range(16):
                            if sl >= 1:
                                sy.wait_ge(s_cv.h, CV + sl)
                            sy.dma_start(usl[0:64, :],
                                         cc0d.ap()[sl * 64:(sl + 1) * 64, :]).then_inc(s_cs[0].bump(16), 16)
                            sy.dma_start(usl[64:128, :],
                                         cc0d.ap()[sl * 64:(sl + 1) * 64, :]).then_inc(s_cs[0].bump(16), 16)
                    @blk.scalar
                    def _(sc):
                        for sl in range(16):
                            if sl >= 1:
                                sc.wait_ge(s_cp.h, CP + sl)
                            sc.activation(cfn[:, sl % 2, :], g["td_nat"][:], AF.Exp,
                                          scale=g["nexpo"][0:16, s * 16 + sl:s * 16 + sl + 1]
                                          ).then_inc(s_ca.bump(), 1)
                    @blk.tensor
                    def _(t_):
                        for sl in range(16):
                            t_.wait_ge(s_ca.h, CA + sl + 1)
                            if sl >= 1:
                                t_.wait_ge(s_cv.h, CV + sl)
                            last = None
                            for hh in (0, 64):
                                for cc in (0, 512):
                                    last = t_.matmul(pco[hh:hh + 64, cc:cc + 512],
                                              cfn[:, sl % 2, :], g["hsel"][:, cc:cc + 512],
                                              start=True, stop=True)
                            last.then_inc(s_cp.bump(), 1)
            if parts == "m2":
                continue
            m3rg = parts == "m3rg"
            m3ch = parts in ("m3rg", "m3ch", "m3c1", "m3c2")
            c_wkv = parts not in ("m3c1",)
            c_u = parts not in ("m3c1", "m3c2")
            s_fold = ctx.sem("mb_fo")
            s_xq2 = [ctx.sem("mb_x2A"), ctx.sem("mb_x2B")]
            s_kv2 = [ctx.sem("mb_kvA"), ctx.sem("mb_kvB")]
            s_rgp = ctx.sem("mb_rg")
            s_rge = ctx.sem("mb_re"); s_ge = ctx.sem("mb_gse")
            s_pA = ctx.sem("mb_pA"); s_pw = ctx.sem("mb_pw"); s_pU = ctx.sem("mb_pU")
            s_wo2 = ctx.sem("mb_wo"); s_dkh = ctx.sem("mb_d1"); s_dam = ctx.sem("mb_d2")
            s_drg = ctx.sem("mb_d3"); s_dkb = ctx.sem("mb_d4"); s_dsu = ctx.sem("mb_d5")
            s_rs = ctx.sem("mb_rs")
            s_xt = [ctx.sem("mb_xtA"), ctx.sem("mb_xtB")]
            s_gt = [ctx.sem("mb_gtA"), ctx.sem("mb_gtB")]
            s_xo = [ctx.sem("mb_xoA"), ctx.sem("mb_xoB")]
            s_lw2 = ctx.sem("mb_l2")
            with contextlib.ExitStack() as pes3:
                pA = pes3.enter_context(nc.psum_tensor(ctx.name("mbpa"), [128, 512], F32))
                pw = pes3.enter_context(nc.psum_tensor(ctx.name("mbpw"), [128, 16, CH], F32))
                pU = pes3.enter_context(nc.psum_tensor(ctx.name("mbpu"), [128, D], F32))
                pRGa = pes3.enter_context(nc.psum_tensor(ctx.name("mbra"), [128, 512], F32))
                pRGb = pes3.enter_context(nc.psum_tensor(ctx.name("mbrb"), [128, 512], F32))
                ps_s = pes3.enter_context(nc.psum_tensor(ctx.name("mbps"), [128, 2, 512], F32))
                with nc.Block() as blk:
                    FO, RGP, RGE, GE = s_fold.n, s_rgp.n, s_rge.n, s_ge.n
                    PA, PW, PU, WO2 = s_pA.n, s_pw.n, s_pU.n, s_wo2.n
                    KH, AM, RG3, KB4, SU = s_dkh.n, s_dam.n, s_drg.n, s_dkb.n, s_dsu.n
                    RS, LW2 = s_rs.n, s_lw2.n
                    XQ = [x.n for x in s_xq2]
                    KV2 = [x.n for x in s_kv2]
                    XT = [x.n for x in s_xt]
                    GT = [x.n for x in s_gt]
                    XO = [x.n for x in s_xo]
                    @blk.gpsimd
                    def _(gp_):
                        for dc in range(8):
                            gp_.dma_start(wbuf[:, dc, 0:D], wr.ap()[dc * 128:(dc + 1) * 128, :]).then_inc(s_lw2.bump(16), 16)
                            gp_.dma_start(wbuf[:, dc, D:2 * D], wgm.ap()[dc * 128:(dc + 1) * 128, :]).then_inc(s_lw2.bump(16), 16)
                        for tc in range(8):
                            if m3ch:
                                continue
                            if tc >= 2:
                                gp_.wait_ge(s_rs.h, RS + tc - 1)
                            gp_.dma_start(gres[:, tc % 2, :],
                                          g0_d.ap()[:, s * 8 + tc, :]).then_inc(s_gt[tc % 2].bump(16), 16)
                    @blk.sync
                    def _(sy):
                        for q in range(4):
                            if q >= 2:
                                sy.wait_ge(s_rgp.h, RGP + 16 * (q - 1))
                                if not m3rg:
                                    sy.wait_ge(s_pU.h, PU + DEBUG_NCH * (q - 1))
                            sy.dma_start(
                                xnq[:, q % 2, :, :],
                                xn_fm.ap()[:, :, s * SEG + q * 256:s * SEG + (q + 1) * 256]
                                .rearrange("c p j -> p c j")).then_inc(s_xq2[q % 2].bump(16), 16)
                            if not m3rg:
                                sy.dma_start(ktmq[:, q % 2, :, :],
                                             ktm_d.ap()[s, :, 4 * q:4 * q + 4, :]).then_inc(s_kv2[q % 2].bump(16), 16)
                                sy.dma_start(vtmq[:, q % 2, :, :],
                                             vtm_d.ap()[s, :, 4 * q:4 * q + 4, :]).then_inc(s_kv2[q % 2].bump(16), 16)
                            if m3ch:
                                continue
                            for tc in (2 * q, 2 * q + 1):
                                if tc >= 2:
                                    sy.wait_ge(s_rs.h, RS + tc - 1)
                                sy.dma_start(xres[:, tc % 2, :],
                                             x_in.ap()[s * SEG + tc * 128:s * SEG + (tc + 1) * 128, :]
                                             ).then_inc(s_xt[tc % 2].bump(16), 16)
                                if tc >= 2:
                                    to = tc - 2
                                    sy.wait_ge(s_rs.h, RS + to + 1)
                                    sy.dma_start(x1_d.ap()[s * SEG + to * 128:s * SEG + (to + 1) * 128, :],
                                                 x1sb[:, to % 2, :]).then_inc(s_xo[to % 2].bump(16), 16)
                        if not m3ch:
                            for tc in (6, 7):
                                sy.wait_ge(s_rs.h, RS + tc + 1)
                                sy.dma_start(x1_d.ap()[s * SEG + tc * 128:s * SEG + (tc + 1) * 128, :],
                                             x1sb[:, tc % 2, :]).then_inc(s_xo[tc % 2].bump(16), 16)
                            sy.wait_ge(s_xo[0].h, XO[0] + 64)
                            sy.wait_ge(s_xo[1].h, XO[1] + 64)
                    @blk.vector
                    def _(v):
                        v.wait_ge(s_lw2.h, LW2 + 256)
                        last = None
                        for dc in range(8):
                            last = v.tensor_scalar(wbuf[:, dc, :], wbuf[:, dc, :],
                                            g["tm_col"][:, dc:dc + 1], None, OP.mult)
                        last.then_inc(s_fold.bump(), 1)
                        for q in range(4):
                            for ec in range(8):
                                v.wait_ge(s_rgp.h, RGP + 16 * q + ec + 1)
                                v.tensor_copy(rq[:, ec, :], (pRGa, pRGb)[ec % 2][:, 0:256]).then_inc(s_rge.bump(), 1)
                            if m3rg:
                                continue
                            v.wait_ge(s_kv2[q % 2].h, KV2[q % 2] + 32 * (q // 2 + 1))
                            for m in range(8 * q, 8 * q + DEBUG_NCH):
                                sl = slice(m * CH, (m + 1) * CH)
                                par = m % 2
                                if m >= 2 and c_wkv:
                                    v.wait_ge(s_pw.h, PW + m - 1)
                                v.tensor_tensor(khb[:, par, :, :], kfm[:, :, sl], g["dhat"][:], OP.mult)
                                v.tensor_tensor(ktb[:, par, :, :], kfm[:, :, sl], g["dneg"][:], OP.mult).then_inc(s_dkh.bump(), 1)
                                v.wait_ge(s_pA.h, PA + m + 1)
                                v.tensor_tensor(am[32 * par:32 * par + 32, :],
                                                pA[32 * par:32 * par + 32, :],
                                                g["mask64"][32 * par:32 * par + 32, :],
                                                OP.mult).then_inc(s_dam.bump(), 1)
                                if not c_wkv:
                                    v.nop().then_inc(s_drg.bump(), 1)
                                    v.nop().then_inc(s_dkb.bump(), 1)
                                    v.nop().then_inc(s_dsu.bump(), 1)
                                    continue
                                v.tensor_tensor(tw[:], kfm[:, :, sl], vfm[:, :, sl], OP.mult)
                                v.drain()
                                v.tensor_tensor(tw[:], tw[:], g["tfmap"][:], OP.mult)
                                v.wait_ge(s_pw.h, PW + m + 1)
                                v.wait_ge(s_ge.h, GE + 8 * q + 8)
                                v.drain()
                                v.tensor_tensor(ws[:], pw[:, 0:8, :], tw[:], OP.add)
                                v.drain()
                                v.tensor_tensor(ws[:], ws[:], rq[:, :, (m % 8) * CH:(m % 8 + 1) * CH], OP.mult)
                                v.drain()
                                v.tensor_tensor(rp[:, :, (m % 8) * CH:(m % 8 + 1) * CH],
                                                ws[:], gq[:, :, (m % 8) * CH:(m % 8 + 1) * CH],
                                                OP.mult).then_inc(s_drg.bump(), 1)
                                if not c_u:
                                    v.nop().then_inc(s_dkb.bump(), 1)
                                    v.nop().then_inc(s_dsu.bump(), 1)
                                    continue
                                if m >= 2:
                                    v.wait_ge(s_pU.h, PU + m - 1)
                                v.tensor_tensor(kb[32 * par:32 * par + 32, :],
                                                ktmq[32 * par:32 * par + 32, q % 2, m // 2 - 4 * q, :],
                                                g["dbar32"][32 * par:32 * par + 32, :],
                                                OP.mult).then_inc(s_dkb.bump(), 1)
                                v.wait_ge(s_pU.h, PU + m + 1)
                                v.tensor_tensor(St[:], S[:], g["dc32"][:], OP.mult)
                                v.drain()
                                v.tensor_tensor(S[:], St[:], pU[:], OP.add)
                                v.drain()
                                v.tensor_copy(S2[:], S[:]).then_inc(s_dsu.bump(), 1)
                            for hh in range(2):
                                if m3ch:
                                    continue
                                tc = 2 * q + hh
                                v.wait_ge(s_wo2.h, WO2 + tc + 1)
                                v.wait_ge(s_xt[tc % 2].h, XT[tc % 2] + 16 * (tc // 2 + 1))
                                v.wait_ge(s_gt[tc % 2].h, GT[tc % 2] + 16 * (tc // 2 + 1))
                                if tc >= 2:
                                    v.wait_ge(s_xo[tc % 2].h, XO[tc % 2] + 16 * (tc // 2))
                                last = None
                                for ot in range(2):
                                    v.tensor_tensor(St[:, 0:512], ps_s[:, ot, :],
                                                    gres[:, tc % 2, ot * 512:(ot + 1) * 512], OP.mult)
                                    v.drain()
                                    last = v.tensor_tensor(x1sb[:, tc % 2, ot * 512:(ot + 1) * 512],
                                                    St[:, 0:512], xres[:, tc % 2, ot * 512:(ot + 1) * 512],
                                                    OP.add)
                                    v.drain()
                                last.then_inc(s_rs.bump(), 1)
                    @blk.scalar
                    def _(sc):
                        for q in range(4):
                            for ec in range(8):
                                sc.wait_ge(s_rgp.h, RGP + 16 * q + 8 + ec + 1)
                                sc.activation(gq[:, ec, :], (pRGa, pRGb)[ec % 2][:, 0:256],
                                              AF.Silu).then_inc(s_ge.bump(), 1)
                    @blk.tensor
                    def _(t_):
                        prevb = [None]
                        def bmm(base, *a, **kw):
                            if prevb[0] is not None and prevb[0] != base:
                                t_.drain()
                            prevb[0] = base
                            return t_.matmul(*a, **kw)
                        for q in range(4):
                            t_.wait_ge(s_fold.h, FO + 1)
                            t_.wait_ge(s_xq2[q % 2].h, XQ[q % 2] + 16 * (q // 2 + 1))
                            if q >= 1 and not m3ch:
                                t_.wait_ge(s_drg.h, RG3 + DEBUG_NCH * q)
                            for w in range(2):
                                for ec in range(8):
                                    gi = 16 * q + w * 8 + ec
                                    if gi >= 2:
                                        ndone = gi - 1
                                        nfullq, rem = divmod(ndone, 16)
                                        rdone = nfullq * 8 + min(rem, 8)
                                        gdone = nfullq * 8 + max(rem - 8, 0)
                                        if rdone > 0:
                                            t_.wait_ge(s_rge.h, RGE + rdone)
                                        if gdone > 0:
                                            t_.wait_ge(s_ge.h, GE + gdone)
                                    prg = (pRGa, pRGb)[ec % 2]
                                    last = None
                                    for dc in range(8):
                                        last = bmm(0, prg[:, 0:256], wbuf[:, dc, w * D + ec * 128:w * D + (ec + 1) * 128],
                                                  xnq[:, q % 2, dc, :],
                                                  start=(dc == 0), stop=(dc == 7))
                                    last.then_inc(s_rgp.bump(), 1)
                            if m3rg:
                                continue
                            t_.wait_ge(s_kv2[q % 2].h, KV2[q % 2] + 32 * (q // 2 + 1))
                            for m in range(8 * q, 8 * q + DEBUG_NCH):
                                par = m % 2
                                t_.wait_ge(s_dkh.h, KH + m + 1)
                                t_.wait_ge(s_dam.h, AM + m)
                                first = True
                                last = None
                                for h in [hh for hh in range(H) if hh % 2 == 0] + [hh for hh in range(H) if hh % 2 == 1]:
                                    hb = 64 * (h % 2)
                                    last = bmm(hb, pA[32 * par:32 * par + 32, h * 32:(h + 1) * 32],
                                              ktb[hb:hb + 64, par, h // 2, :],
                                              khb[hb:hb + 64, par, h // 2, :],
                                              start=first, stop=(h == H - 1))
                                    first = False
                                last.then_inc(s_pA.bump(), 1)
                                if not c_wkv:
                                    t_.nop().then_inc(s_pw.bump(), 1)
                                    t_.nop().then_inc(s_pU.bump(), 1)
                                    continue
                                t_.wait_ge(s_dam.h, AM + m + 1)
                                t_.wait_ge(s_dsu.h, SU + m)
                                t_.wait_ge(s_drg.h, RG3 + m)
                                pwflat = bass.AP(pw, 0, [[512, 128], [1, 512]])
                                bmm("z", pwflat, g["ones1b"][:], g["zrow"][:],
                                    start=True, stop=False)
                                for h in range(H):
                                    bmm(32 * par, pw[64 * (h % 2):64 * (h % 2) + 64, h // 2, :],
                                        vtmq[32 * par:32 * par + 32, q % 2, m // 2 - 4 * q, h * 64:(h + 1) * 64],
                                        am[32 * par:32 * par + 32, h * 32:(h + 1) * 32],
                                        start=False, stop=False)
                                for h in [hh for hh in range(H) if hh % 2 == 0] + [hh for hh in range(H) if hh % 2 == 1]:
                                    hb = 64 * (h % 2)
                                    bmm(hb, pw[hb:hb + 64, h // 2, :],
                                        S2[hb:hb + 64, h * 64:(h + 1) * 64],
                                        khb[hb:hb + 64, par, h // 2, :],
                                        start=False, stop=False)
                                bmm("z", pwflat, g["ones1b"][:], g["zrow"][:],
                                    start=False, stop=True).then_inc(s_pw.bump(), 1)
                                if not c_u:
                                    t_.nop().then_inc(s_pU.bump(), 1)
                                    if m % 4 == 3 and not m3ch:
                                        pass
                                    continue
                                t_.wait_ge(s_dkb.h, KB4 + m + 1)
                                t_.wait_ge(s_dsu.h, SU + m)
                                for cc in (0, 512):
                                    bmm("z", pU[:, cc:cc + 512], g["ones1b"][:], g["zrow"][:],
                                        start=True, stop=False)
                                for half in range(2):
                                    for h in range(H):
                                        bmm(32 * par, pU[64 * half:64 * half + 64, h * 64:(h + 1) * 64],
                                            kb[32 * par:32 * par + 32, h * 64:(h + 1) * 64],
                                            vtmq[32 * par:32 * par + 32, q % 2, m // 2 - 4 * q, h * 64:(h + 1) * 64],
                                            start=False, stop=False)
                                last = None
                                for cc in (0, 512):
                                    last = bmm("z", pU[:, cc:cc + 512], g["ones1b"][:], g["zrow"][:],
                                               start=False, stop=True)
                                last.then_inc(s_pU.bump(), 1)
                                if m % 4 == 3 and not m3ch:
                                    tc = m // 4
                                    t_.wait_ge(s_drg.h, RG3 + m + 1)
                                    t_.wait_ge(s_rs.h, RS + tc)
                                    last = None
                                    for ot in range(2):
                                        for dc in range(8):
                                            last = bmm(0, ps_s[:, ot, :],
                                                      rp[:, dc, (tc % 2) * 128:(tc % 2) * 128 + 128],
                                                      wosb[:, dc, ot * 512:(ot + 1) * 512],
                                                      start=(dc == 0), stop=(dc == 7))
                                    last.then_inc(s_wo2.bump(), 1)


def emit_qkv(ctx):
    nc = ctx.nc
    xn_fm = ctx.dtmp("xn_fm", [8, 128, TOK], BF16)
    qf_d = ctx.dtmp("qf", [2, 128, 8, SEG], BF16)
    cc1s = ctx.dtmp("cc1src", [2, 2, 128, 8, SEG], BF16)
    cc1d = ctx.dtmp("cc1dst", [NCORE, 2, 2, 128, 8, SEG], BF16)
    wqkv = ctx.din("wqkv", [D, 3 * D])
    with contextlib.ExitStack() as _es:
        w3 = _es.enter_context(nc.sbuf_tensor("qk_w", [128, 8, 3 * D], BF16))
        xnq = _es.enter_context(nc.sbuf_tensor("qk_x", [128, 2, 8, 512], BF16))
        qsb = _es.enter_context(nc.sbuf_tensor("qk_q", [128, 8, SEG], BF16))
        ksb = _es.enter_context(nc.sbuf_tensor("qk_k", [128, 8, SEG], BF16))
        vsb = _es.enter_context(nc.sbuf_tensor("qk_v", [128, 8, SEG], BF16))
        pf = _es.enter_context(nc.psum_tensor("qk_pf", [128, 2, 512], F32))
        s_lw = ctx.sem("qk_lw")
        with nc.Block() as blk:
            LW = s_lw.n
            @blk.gpsimd
            def _(gp_):
                for dc in range(8):
                    gp_.dma_start(w3[:, dc, :], wqkv.ap()[dc * 128:(dc + 1) * 128, :]).then_inc(s_lw.bump(16), 16)
                gp_.wait_ge(s_lw.h, LW + 128)
        for s in range(2):
            s_xf = ctx.sem("qk_xf"); s_fp = ctx.sem("qk_fp"); s_fe = ctx.sem("qk_fe")
            s_sp = ctx.sem("qk_sp")
            with nc.Block() as blk:
                XF, FP, FE, SP = s_xf.n, s_fp.n, s_fe.n, s_sp.n
                @blk.sync
                def _(sy):
                    for t5 in range(2):
                        sy.dma_start(
                            xnq[:, t5, :, :],
                            xn_fm.ap()[:, :, s * SEG + t5 * 512:s * SEG + (t5 + 1) * 512]
                            .rearrange("c p j -> p c j")).then_inc(s_xf.bump(16), 16)
                    sy.wait_ge(s_fe.h, FE + 48)
                    sy.dma_start(qf_d.ap()[s], qsb[:]).then_inc(s_sp.bump(16), 16)
                    sy.dma_start(cc1s.ap()[s, 0], ksb[:]).then_inc(s_sp.bump(16), 16)
                    sy.dma_start(cc1s.ap()[s, 1], vsb[:]).then_inc(s_sp.bump(16), 16)
                    sy.wait_ge(s_sp.h, SP + 48)
                @blk.tensor
                def _(t_):
                    i = 0
                    for part in range(2):
                        for t5 in range(2):
                            t_.wait_ge(s_xf.h, XF + 32)
                            for ec in range(8):
                                if i >= 2:
                                    t_.wait_ge(s_fe.h, FE + i - 1)
                                last = None
                                for dc in range(8):
                                    last = t_.matmul(pf[:, i % 2, :],
                                              w3[:, dc, part * D + ec * 128:part * D + (ec + 1) * 128],
                                              xnq[:, t5, dc, :], start=(dc == 0), stop=(dc == 7))
                                last.then_inc(s_fp.bump(), 1)
                                i += 1
                    t_.wait_ge(s_xf.h, XF + 32)
                    for tc in range(8):
                        for vh in range(2):
                            if i >= 2:
                                t_.wait_ge(s_fe.h, FE + i - 1)
                            last = None
                            for dc in range(8):
                                last = t_.matmul(pf[:, i % 2, :],
                                          xnq[:, tc // 4, dc, (tc % 4) * 128:(tc % 4 + 1) * 128],
                                          w3[:, dc, 2 * D + vh * 512:2 * D + (vh + 1) * 512],
                                          start=(dc == 0), stop=(dc == 7))
                            last.then_inc(s_fp.bump(), 1)
                            i += 1
                @blk.vector
                def _(v):
                    i = 0
                    for part in range(2):
                        dstb = (qsb, ksb)[part]
                        for t5 in range(2):
                            for ec in range(8):
                                v.wait_ge(s_fp.h, FP + i + 1)
                                v.tensor_copy(dstb[:, ec, t5 * 512:(t5 + 1) * 512],
                                              pf[:, i % 2, :]).then_inc(s_fe.bump(), 1)
                                i += 1
                    for tc in range(8):
                        for vh in range(2):
                            v.wait_ge(s_fp.h, FP + i + 1)
                            v.tensor_copy(vsb[:, tc, vh * 512:(vh + 1) * 512],
                                          pf[:, i % 2, :]).then_inc(s_fe.bump(), 1)
                            i += 1
        if SIM_SKIP_CC:
            ctx.sem("cc1").bump()
            return
        s_cc1 = ctx.sem("cc1")
        with nc.Block() as blk:
            @blk.gpsimd
            def _(gp_):
                gp_.collective_compute(
                    "AllGather", OP.bypass, replica_groups=[list(range(NCORE))],
                    ins=[ctx.dram["cc1src"].ap().opt()],
                    outs=[ctx.dram["cc1dst"].ap().opt()]).then_inc(s_cc1.bump(), 1)



def emit_attn(ctx):
    nc = ctx.nc
    g = ctx.glob
    qf_d = ctx.dtmp("qf", [2, 128, 8, SEG], BF16)
    cc1d = ctx.dtmp("cc1dst", [NCORE, 2, 2, 128, 8, SEG], BF16)
    g1_d = ctx.dtmp("g1", [128, 16, D], BF16)
    x1_d = ctx.dtmp("x1", [TOK, D], F32)
    out_d = ctx.dout("out", [TOK, D], F32)
    idb = g["idb"]
    with contextlib.ExitStack() as _es:
        qseg = _es.enter_context(nc.sbuf_tensor("at_q", [128, 8, SEG], BF16))
        kbuf = _es.enter_context(nc.sbuf_tensor("at_k", [128, 2, 8, 512], BF16))
        vbuf = _es.enter_context(nc.sbuf_tensor("at_v", [128, 2, 4, SEG], BF16))
        Oall = _es.enter_context(nc.sbuf_tensor("at_o", [128, 8, SEG], F32))
        Mall = _es.enter_context(nc.sbuf_tensor("at_m", [128, 8], F32))
        Dall = _es.enter_context(nc.sbuf_tensor("at_d", [128, 8], F32))
        mnew = _es.enter_context(nc.sbuf_tensor("at_mn", [128, 8], F32))
        nmn = _es.enter_context(nc.sbuf_tensor("at_nn", [128, 8], F32))
        resc = _es.enter_context(nc.sbuf_tensor("at_rc", [128, 8], F32))
        mxc = _es.enter_context(nc.sbuf_tensor("at_mx", [128, 2], F32))
        psc_ = _es.enter_context(nc.sbuf_tensor("at_ps", [128, 2], F32))
        P = _es.enter_context(nc.sbuf_tensor("at_P", [128, 2, 512], BF16))
        PT = _es.enter_context(nc.sbuf_tensor("at_PT", [128, 4, 128], BF16))
        osb = _es.enter_context(nc.sbuf_tensor("at_os", [128, 2, D], F32))
        x1t = _es.enter_context(nc.sbuf_tensor("at_x1", [128, 2, D], F32))
        g2t = _es.enter_context(nc.sbuf_tensor("at_g2", [128, 2, D], F32))
        scr = _es.enter_context(nc.sbuf_tensor("at_sc", [128, D], F32))
        psc = _es.enter_context(nc.psum_tensor("at_psc", [128, 3, 512], F32))
        ppt = _es.enter_context(nc.psum_tensor("at_ppt", [128, 8, 128], BF16))
        pav = _es.enter_context(nc.psum_tensor("at_pav", [128, 2, 512], F32))
        for s in range(2):
            s_kv = [ctx.sem("at_kvA"), ctx.sem("at_kvB")]
            s_q = ctx.sem("at_q"); s_sc = ctx.sem("at_sc")
            s_mx = ctx.sem("at_mx"); s_ax = ctx.sem("at_ax"); s_msk = ctx.sem("at_mk")
            s_pt = ctx.sem("at_pt"); s_pte = ctx.sem("at_pe"); s_av = ctx.sem("at_av")
            s_ou = ctx.sem("at_ou"); s_rs = ctx.sem("at_rs")
            s_xo = [ctx.sem("at_xoA"), ctx.sem("at_xoB")]
            s_x1 = [ctx.sem("at_x1A"), ctx.sem("at_x1B")]
            s_gt = [ctx.sem("at_gtA"), ctx.sem("at_gtB")]
            b = s
            N = 128
            with nc.Block() as blk:
                Q0, SC, MX, AX0, MSK = s_q.n, s_sc.n, s_mx.n, s_ax.n, s_msk.n
                PT0, PTE, AV, OU, RS = s_pt.n, s_pte.n, s_av.n, s_ou.n, s_rs.n
                KV = [x.n for x in s_kv]
                XO = [x.n for x in s_xo]
                X1 = [x.n for x in s_x1]
                GT = [x.n for x in s_gt]
                @blk.sync
                def _(sy):
                    if not SIM_SKIP_CC:
                        sy.wait_ge(ctx.sem("cc1").h, 1)
                    sy.dma_start(qseg[:], qf_d.ap()[s]).then_inc(s_q.bump(16), 16)
                    for t in range(16):
                        qq = 8 * b + t // 2
                        rk = min(qq, 15 - qq)
                        sp = 0 if rk == qq else 1
                        half = t % 2
                        if t >= 2:
                            sy.wait_ge(s_sc.h, SC + 8 * (t - 1))
                            sy.wait_ge(s_av.h, AV + 8 * (t - 1))
                        sy.dma_start(kbuf[:, t % 2, :, :],
                                     cc1d.ap()[rk, sp, 0, :, :, half * 512:(half + 1) * 512]
                                     ).then_inc(s_kv[t % 2].bump(16), 16)
                        sy.dma_start(vbuf[:, t % 2, :, :],
                                     cc1d.ap()[rk, sp, 1, :, 4 * half:4 * half + 4, :]
                                     ).then_inc(s_kv[t % 2].bump(16), 16)
                    for k in range(8):
                        if k >= 2:
                            sy.wait_ge(s_rs.h, RS + k - 1)
                        sy.dma_start(x1t[:, k % 2, :],
                                     x1_d.ap()[s * SEG + k * 128:s * SEG + (k + 1) * 128, :]
                                     ).then_inc(s_x1[k % 2].bump(16), 16)
                        sy.wait_ge(s_rs.h, RS + k + 1)
                        sy.dma_start(out_d.ap()[s * SEG + k * 128:s * SEG + (k + 1) * 128, :],
                                     osb[:, k % 2, :]).then_inc(s_xo[k % 2].bump(16), 16)
                    sy.wait_ge(s_xo[0].h, XO[0] + 64)
                    sy.wait_ge(s_xo[1].h, XO[1] + 64)
                @blk.gpsimd
                def _(gp_):
                    for k in range(8):
                        if k >= 2:
                            gp_.wait_ge(s_rs.h, RS + k - 1)
                        gp_.dma_start(g2t[:, k % 2, :], g1_d.ap()[:, s * 8 + k, :]).then_inc(s_gt[k % 2].bump(16), 16)
                @blk.tensor
                def _(t_):
                    def scores(i):
                        t, k = divmod(i, 8)
                        t_.wait_ge(s_kv[t % 2].h, KV[t % 2] + 32 * (t // 2 + 1))
                        if i == 0:
                            t_.wait_ge(s_q.h, Q0 + 16)
                        if i >= 3:
                            t_.wait_ge(s_ax.h, AX0 + i - 2)
                        last = None
                        for dc in range(8):
                            last = t_.matmul(psc[:, i % 3, :], qseg[:, dc, k * 128:(k + 1) * 128],
                                      kbuf[:, (i // 8) % 2, dc, :],
                                      start=(dc == 0), stop=(dc == 7))
                        last.then_inc(s_sc.bump(), 1)
                    scores(0)
                    scores(1)
                    for i in range(N):
                        if i + 2 < N:
                            scores(i + 2)
                        t_.wait_ge(s_msk.h, MSK + i + 1)
                        if i >= 1:
                            t_.wait_ge(s_pte.h, PTE + i)
                        last = None
                        for jb in range(4):
                            last = t_.transpose(ppt[:, jb, :], P[:, i % 2, jb * 128:(jb + 1) * 128], idb[:])
                        last.then_inc(s_pt.bump(), 1)
                        t_.wait_ge(s_pte.h, PTE + i + 1)
                        if i >= 1:
                            t_.wait_ge(s_ou.h, OU + i)
                        last = None
                        for ot in range(2):
                            for jb in range(4):
                                last = t_.matmul(pav[:, ot, :], PT[:, jb, :],
                                          vbuf[:, (i // 8) % 2, jb, ot * 512:(ot + 1) * 512],
                                          start=(jb == 0), stop=(jb == 3))
                        last.then_inc(s_av.bump(), 1)
                @blk.vector
                def _(v):
                    for k in range(8):
                        v.memset(Oall[:, k, :], 0.0)
                    v.memset(Mall[:], NEGBIG)
                    v.memset(Dall[:], 0.0)
                    for i in range(N):
                        t, k = divmod(i, 8)
                        v.wait_ge(s_sc.h, SC + i + 1)
                        v.reduce_max(mxc[:, i % 2:i % 2 + 1], psc[:, i % 3, :], axis=AX)
                        v.drain()
                        v.tensor_scalar(mxc[:, i % 2:i % 2 + 1], mxc[:, i % 2:i % 2 + 1],
                                        1.0 / 32.0, None, OP.mult)
                        v.drain()
                        v.tensor_tensor(mnew[:, k:k + 1], Mall[:, k:k + 1],
                                        mxc[:, i % 2:i % 2 + 1], OP.max)
                        v.drain()
                        v.tensor_scalar(nmn[:, k:k + 1], mnew[:, k:k + 1], -1.0,
                                        None, OP.mult).then_inc(s_mx.bump(), 1)
                        v.wait_ge(s_ax.h, AX0 + i + 1)
                        v.scalar_tensor_tensor(P[:, i % 2, :], g["iota_pj"][:],
                                               g["thresh"][:, s * 128 + k * 16 + t:s * 128 + k * 16 + t + 1],
                                               P[:, i % 2, :], OP.is_ge, OP.mult)
                        v.drain()
                        v.reduce_sum(psc_[:, i % 2:i % 2 + 1], P[:, i % 2, :], axis=AX)
                        v.drain()
                        v.scalar_tensor_tensor(Dall[:, k:k + 1], Dall[:, k:k + 1],
                                               resc[:, k:k + 1], psc_[:, i % 2:i % 2 + 1],
                                               OP.mult, OP.add).then_inc(s_msk.bump(), 1)
                        v.wait_ge(s_av.h, AV + i + 1)
                        for ot in range(2):
                            v.scalar_tensor_tensor(Oall[:, k, ot * 512:(ot + 1) * 512],
                                                   Oall[:, k, ot * 512:(ot + 1) * 512],
                                                   resc[:, k:k + 1], pav[:, ot, :],
                                                   OP.mult, OP.add)
                        v.drain()
                        v.tensor_copy(Mall[:, k:k + 1], mnew[:, k:k + 1]).then_inc(s_ou.bump(), 1)
                    for k in range(8):
                        v.wait_ge(s_gt[k % 2].h, GT[k % 2] + 16 * (k // 2 + 1))
                        v.wait_ge(s_x1[k % 2].h, X1[k % 2] + 16 * (k // 2 + 1))
                        if k >= 2:
                            v.wait_ge(s_xo[k % 2].h, XO[k % 2] + 16 * (k // 2))
                        v.reciprocal(mxc[:, 0:1], Dall[:, k:k + 1])
                        v.drain()
                        v.tensor_scalar(scr[:], Oall[:, k, :], mxc[:, 0:1], None, OP.mult)
                        v.drain()
                        v.tensor_tensor(scr[:], scr[:], g2t[:, k % 2, :], OP.mult)
                        v.drain()
                        v.tensor_tensor(osb[:, k % 2, :], scr[:], x1t[:, k % 2, :],
                                        OP.add).then_inc(s_rs.bump(), 1)
                @blk.scalar
                def _(sc):
                    for i in range(N):
                        t, k = divmod(i, 8)
                        sc.wait_ge(s_mx.h, MX + i + 1)
                        sc.activation(resc[:, k:k + 1], Mall[:, k:k + 1], AF.Exp,
                                      bias=nmn[:, k:k + 1])
                        if i >= 2:
                            sc.wait_ge(s_pt.h, PT0 + i - 1)
                        sc.activation(P[:, i % 2, :], psc[:, i % 3, :], AF.Exp, scale=1.0 / 32.0,
                                      bias=nmn[:, k:k + 1]).then_inc(s_ax.bump(), 1)
                        sc.wait_ge(s_pt.h, PT0 + i + 1)
                        if i >= 1:
                            sc.wait_ge(s_av.h, AV + i)
                        sc.copy(PT[:], ppt[:, 0:4, :]).then_inc(s_pte.bump(), 1)



_BUILD_CACHE = {}


def build_program(upto="attn", taps=()):
    key = (upto, tuple(sorted(taps)))
    if key in _BUILD_CACHE:
        return _BUILD_CACHE[key]
    nc = bass.Bass("TRN2", target_bir_lowering=False)
    es = contextlib.ExitStack()
    ctx = Ctx(nc, es, taps)
    with es:
        x_in = ctx.din("x", [TOK, D])
        emit_setup(ctx)
        phases = ["ln1", "mixa", "moe0", "mixb", "ln2", "qkv", "moe1", "attn"]
        stop = phases.index({"mixp": "mixa", "mixn": "mixa", "mixb1": "mixb", "mixb2": "mixb", "mixb3": "mixb", "mixb4": "mixb", "mixb5": "mixb", "mixb6": "mixb"}.get(upto, upto))
        xn_fm = ctx.dtmp("xn_fm", [8, 128, TOK], BF16)
        emit_ln(ctx, x_in, xn_fm)
        if stop >= 1:
            emit_mixer_a(ctx, parts={"mixp": "prod", "mixn": "nocc"}.get(upto, "all"))
        if stop >= 2:
            emit_moe(ctx, 0)
        if stop >= 3:
            emit_mixer_b(ctx, parts={"mixb1": "m1", "mixb2": "m2", "mixb3": "m3rg", "mixb4": "m3ch", "mixb5": "m3c1", "mixb6": "m3c2"}.get(upto, "all"))
        if stop >= 4:
            emit_ln(ctx, ctx.dtmp("x1", [TOK, D], F32), xn_fm)
        if stop >= 5:
            emit_qkv(ctx)
        if stop >= 6:
            emit_moe(ctx, 1)
        if stop >= 7:
            emit_attn(ctx)
        if 1 <= stop < 3 and upto not in ("mixp", "mixn") and not SIM_SKIP_CC:
            with nc.Block() as blk:
                @blk.gpsimd
                def _(gp_):
                    gp_.wait_ge(ctx.sem("cc0").h, 1)
        if 5 <= stop < 7 and not SIM_SKIP_CC:
            with nc.Block() as blk:
                @blk.gpsimd
                def _(gp_):
                    gp_.wait_ge(ctx.sem("cc1").h, 1)
    innames = [k for k, v in ctx.kinds.items() if v == "in"]
    _BUILD_CACHE[key] = (nc, innames)
    return nc, innames


def kernel(**inputs):
    nc, innames = build_program("attn")
    in_maps = prep_in_maps(inputs)
    in_maps = [{k: m[k] for k in innames} for m in in_maps]
    res = run_bass_kernel_spmd(nc, in_maps, core_ids=list(range(NCORE)))
    return scatter_out(res.results)


# version 03
# speedup vs baseline: 1.0187x; 1.0187x over previous
import contextlib
import numpy as np
import concourse.bass as bass
import concourse.mybir as mybir
from concourse.bass_utils import run_bass_kernel_spmd

F32 = mybir.dt.float32
BF16 = mybir.dt.bfloat16
AF = mybir.ActivationFunctionType
OP = mybir.AluOpType
AX = mybir.AxisListType.X

D = 1024
TOK = 2048
SEG = 1024
H, HD = 16, 64
CH = 32
NCHUNK = SEG // CH
NCORE = 8
NEGBIG = -1.0e30
SIM_SKIP_CC = False
DEBUG_NCH = 8


def _prep_consts():
    c = {}
    c["iden"] = np.eye(128, dtype=np.float32)
    m = np.triu(np.ones((32, 32), np.float32))
    c["mask64"] = np.ascontiguousarray(np.tile(np.tile(m, (1, 16)), (2, 1)))
    c["ivec"] = np.ascontiguousarray(
        np.tile(np.arange(1, CH + 1, dtype=np.float32)[None, :], (128, 1)))
    c["revseg"] = (SEG - 1.0 - np.arange(SEG, dtype=np.float32))[None, :]
    c["w31col2"] = (31.0 - (np.arange(64) % 32)).astype(np.float32)[None, :]
    c["hsel"] = np.ascontiguousarray(np.repeat(np.eye(16, dtype=np.float32), HD, axis=1))
    c["iota_pj"] = np.ascontiguousarray(
        np.arange(128, dtype=np.float32)[:, None] - np.arange(512, dtype=np.float32)[None, :])
    c["ones_1x"] = np.ones((1, 128), np.float32)
    return c


def _prep_core_consts(core):
    qA, qB = core, 15 - core
    segs = [(qA // 8, qA % 8), (qB // 8, qB % 8)]
    d = {}
    expo = np.full((2, 16), 1.0e6, np.float32)
    for s, (b, kpos) in enumerate(segs):
        for rank in range(8):
            for sp in range(2):
                q = rank if sp == 0 else 15 - rank
                bp, kp = q // 8, q % 8
                if bp == b and kp < kpos:
                    expo[s, 2 * rank + sp] = SEG * (kpos - 1 - kp)
    d["sin_nexpo"] = np.ascontiguousarray(np.tile(-expo.reshape(1, 32), (128, 1)))
    th = np.zeros((2, 8, 16), np.float32)
    for s, (b, kpos) in enumerate(segs):
        for k in range(8):
            g0 = kpos * SEG + k * 128
            for t in range(16):
                th[s, k, t] = t * 512 - g0
    d["attn_thresh"] = np.ascontiguousarray(np.tile(th.reshape(1, 256), (128, 1)))
    return d


def _prep_shared(inputs):
    g = lambda k: np.asarray(inputs[k], dtype=np.float32)
    s = {}
    s["wk"] = np.ascontiguousarray(g("L0_attn_Wk").T)
    s["wv"] = np.ascontiguousarray(g("L0_attn_Wv").T)
    s["wr"] = np.ascontiguousarray(g("L0_attn_Wr").T)
    s["wgm"] = np.ascontiguousarray(g("L0_attn_Wg").T)
    s["wo"] = np.ascontiguousarray(g("L0_attn_Wo").T)
    tm = g("L0_attn_time_mix").reshape(D)
    s["tm_col"] = np.ascontiguousarray(tm.reshape(8, 128).T)
    td = g("L0_attn_time_decay")
    s["td_col"] = np.ascontiguousarray(td.reshape(D).reshape(8, 128).T)
    s["td_row"] = np.ascontiguousarray(td.reshape(1, D))
    s["td_nat"] = np.ascontiguousarray(td)
    tf = g("L0_attn_time_first")
    s["tf_col"] = np.ascontiguousarray(tf.reshape(D).reshape(8, 128).T)
    s["m0_w1t"] = np.ascontiguousarray(g("L0_ffn_W1").transpose(0, 2, 1))
    s["m0_w2t"] = np.ascontiguousarray(g("L0_ffn_W2").transpose(0, 2, 1))
    s["m0_wg"] = np.ascontiguousarray(g("L0_ffn_Wg").T)
    s["m0_bg"] = np.ascontiguousarray(g("L0_ffn_bg").reshape(1, 8))
    s["m0_b1"] = np.ascontiguousarray(g("L0_ffn_b1").reshape(8, 16, 128).transpose(0, 2, 1))
    s["m0_b2"] = np.ascontiguousarray(g("L0_ffn_b2"))
    s["wqkv"] = np.ascontiguousarray(g("L1_attn_W_qkv"))
    s["rw"] = np.ascontiguousarray(g("L1_ffn_router_w"))
    s["m1_w1"] = np.ascontiguousarray(g("L1_ffn_w1"))
    s["m1_w2"] = np.ascontiguousarray(g("L1_ffn_w2"))
    s["m1_b1"] = np.ascontiguousarray(g("L1_ffn_b1").reshape(16, 16, 128).transpose(0, 2, 1))
    s["m1_b2"] = np.ascontiguousarray(g("L1_ffn_b2"))
    return s


def prep_in_maps(inputs):
    x = np.asarray(inputs["L0_attn_x"], dtype=np.float32)
    shared = _prep_shared(inputs)
    consts = _prep_consts()
    in_maps = []
    for c in range(NCORE):
        qA, qB = c, 15 - c
        shard = np.concatenate(
            [x[qA // 8, (qA % 8) * SEG:(qA % 8 + 1) * SEG],
             x[qB // 8, (qB % 8) * SEG:(qB % 8 + 1) * SEG]], axis=0)
        m = {"x": np.ascontiguousarray(shard)}
        m.update(shared)
        m.update(consts)
        m.update(_prep_core_consts(c))
        in_maps.append(m)
    return in_maps


def scatter_out(results):
    out = np.empty((2, 8192, D), dtype=np.float32)
    for c in range(NCORE):
        y = np.asarray(results[c]["out"])
        qA, qB = c, 15 - c
        out[qA // 8, (qA % 8) * SEG:(qA % 8 + 1) * SEG] = y[:SEG]
        out[qB // 8, (qB % 8) * SEG:(qB % 8 + 1) * SEG] = y[SEG:]
    return out



class SemCt:

    def __init__(self, es, nc, name):
        self.h = es.enter_context(nc.semaphore(name))
        self.n = 0

    def bump(self, k=1):
        self.n += k
        return self.h


class Ctx:
    def __init__(self, nc, es, taps=()):
        self.nc = nc
        self.es = es
        self.dram = {}
        self.kinds = {}
        self.taps = set(taps)
        self.uid = 0
        self.glob = {}
        self.sems = {}

    def name(self, stem):
        self.uid += 1
        return f"{stem}{self.uid}"

    def sem(self, key):
        if key not in self.sems:
            self.sems[key] = SemCt(self.es, self.nc, "s_" + key)
        return self.sems[key]

    def din(self, key, shape, dtype=F32):
        if key not in self.dram:
            self.dram[key] = self.nc.dram_tensor(key, shape, dtype, kind="ExternalInput")
            self.kinds[key] = "in"
        return self.dram[key]

    def dout(self, key, shape, dtype=F32):
        if key not in self.dram:
            self.dram[key] = self.nc.dram_tensor(key, shape, dtype, kind="ExternalOutput")
            self.kinds[key] = "out"
        return self.dram[key]

    def dtmp(self, key, shape, dtype):
        if key not in self.dram:
            if key in self.taps:
                self.dram[key] = self.nc.dram_tensor(key, shape, dtype, kind="ExternalOutput")
                self.kinds[key] = "out"
            else:
                self.dram[key] = self.nc.dram_tensor(key, shape, dtype)
                self.kinds[key] = "tmp"
        return self.dram[key]



def emit_setup(ctx):
    nc = ctx.nc
    g = ctx.glob
    mk = lambda n, sh, dt: ctx.es.enter_context(nc.sbuf_tensor(n, sh, dt))
    g["idb"] = mk("g_idb", [128, 128], BF16)
    g["td_col"] = mk("g_tdcol", [128, 8], F32)
    g["tm_col"] = mk("g_tmcol", [128, 8], F32)
    g["tf_col"] = mk("g_tfcol", [128, 8], F32)
    g["td_nat"] = mk("g_tdnat", [16, 64], F32)
    g["dhat"] = mk("g_dhat", [128, 8, CH], BF16)
    g["dneg"] = mk("g_dneg", [128, 8, CH], BF16)
    g["tfmap"] = mk("g_tfmap", [128, 8, CH], F32)
    g["dc32"] = mk("g_dc32", [128, D], F32)
    g["dbar32"] = mk("g_dbar32", [64, D], BF16)
    g["mask64"] = mk("g_mask", [64, 512], BF16)
    g["iota_pj"] = mk("g_iota", [128, 512], F32)
    g["thresh"] = mk("g_thr", [128, 256], F32)
    g["nexpo"] = mk("g_nexpo", [128, 32], F32)
    g["hsel"] = mk("g_hsel", [16, D], F32)
    g["ones1b"] = mk("g_ones1b", [1, 128], BF16)
    g["zrow"] = mk("g_zrow", [1, 512], BF16)

    with contextlib.ExitStack() as _es:
        idf = _es.enter_context(nc.sbuf_tensor("su_idf", [128, 128], F32))
        ivec = _es.enter_context(nc.sbuf_tensor("su_ivec", [128, CH], F32))
        tmp = _es.enter_context(nc.sbuf_tensor("su_t", [128, D], F32))
        mskf = _es.enter_context(nc.sbuf_tensor("su_m", [64, 512], F32))
        w31 = _es.enter_context(nc.sbuf_tensor("su_w31", [1, 64], F32))
        tdrow_l = _es.enter_context(nc.sbuf_tensor("su_tdrow", [1, D], F32))
        ones1 = _es.enter_context(nc.sbuf_tensor("su_ones", [1, 128], F32))
        ps = _es.enter_context(nc.psum_tensor("su_ps", [128, D], F32))
        pb = _es.enter_context(nc.psum_tensor("su_pb", [128, D], F32))
        s_ld = _es.enter_context(nc.semaphore("su_ld"))
        w31_host = ctx.din("w31col2", [1, 64])
        with nc.Block() as blk:
            @blk.sync
            def _(sy):
                n = [0]
                def ld(dst, key, shape):
                    sy.dma_start(dst, ctx.din(key, shape).ap()).then_inc(s_ld, 16)
                    n[0] += 16
                ld(idf[:], "iden", [128, 128])
                ld(g["td_col"][:], "td_col", [128, 8])
                ld(g["tm_col"][:], "tm_col", [128, 8])
                ld(g["tf_col"][:], "tf_col", [128, 8])
                ld(g["td_nat"][:], "td_nat", [16, 64])
                ld(tdrow_l[:], "td_row", [1, D])
                ld(g["iota_pj"][:], "iota_pj", [128, 512])
                ld(g["thresh"][:], "attn_thresh", [128, 256])
                ld(g["nexpo"][:], "sin_nexpo", [128, 32])
                ld(g["hsel"][:], "hsel", [16, D])
                ld(ivec[:], "ivec", [128, CH])
                ld(mskf[:], "mask64", [64, 512])
                ld(w31[:], "w31col2", [1, 64])
                ld(ones1[:], "ones_1x", [1, 128])
                sy.wait_ge(s_ld, n[0])
        with nc.Block() as blk:
            @blk.vector
            def _(v):
                v.memset(g["zrow"][:], 0.0)
                v.tensor_copy(g["idb"][:], idf[:])
                v.tensor_copy(g["ones1b"][:], ones1[:])
                v.tensor_copy(g["mask64"][:], mskf[:])
                for c in range(8):
                    v.tensor_scalar(tmp[:, c * CH:(c + 1) * CH], ivec[:],
                                    g["td_col"][:, c:c + 1], None, OP.mult)
                    v.tensor_scalar(g["tfmap"][:, c, :], ivec[:],
                                    0.0, g["tf_col"][:, c:c + 1], OP.mult, OP.add)
        with nc.Block() as blk:
            @blk.scalar
            def _(sc):
                for c in range(8):
                    sc.activation(g["dhat"][:, c, :], tmp[:, c * CH:(c + 1) * CH],
                                  AF.Exp, scale=-1.0)
                    sc.activation(g["dneg"][:, c, :], tmp[:, c * CH:(c + 1) * CH],
                                  AF.Exp, scale=1.0)
                sc.drain()
                sc.activation(tmp[0:16, 0:64], g["td_nat"][:], AF.Exp, scale=-32.0)
        with nc.Block() as blk:
            @blk.tensor
            def _(t):
                for hh in (0, 64):
                    for cc in (0, 512):
                        t.matmul(ps[hh:hh + 64, cc:cc + 512], tmp[0:16, 0:64],
                                 g["hsel"][:, cc:cc + 512], start=True, stop=True)
                for cc in (0, 512):
                    t.matmul(pb[0:64, cc:cc + 512], w31[:], tdrow_l[:, cc:cc + 512],
                             start=True, stop=True)
        with nc.Block() as blk:
            @blk.vector
            def _(v):
                v.tensor_copy(g["dc32"][:], ps[:])
            @blk.scalar
            def _(sc):
                sc.activation(g["dbar32"][:], pb[0:64, :], AF.Exp, scale=-1.0)



def emit_ln(ctx, src_hdl, dst_hdl):
    nc = ctx.nc
    idb = ctx.glob["idb"]
    with contextlib.ExitStack() as _es:
        xs = _es.enter_context(nc.sbuf_tensor(ctx.name("lx"), [128, 16, D], F32))
        sq = _es.enter_context(nc.sbuf_tensor(ctx.name("lq"), [128, D], F32))
        s1 = _es.enter_context(nc.sbuf_tensor(ctx.name("ls1"), [128, 16], F32))
        s2 = _es.enter_context(nc.sbuf_tensor(ctx.name("ls2"), [128, 16], F32))
        nmu = _es.enter_context(nc.sbuf_tensor(ctx.name("lmu"), [128, 16], F32))
        var = _es.enter_context(nc.sbuf_tensor(ctx.name("lva"), [128, 16], F32))
        lnv = _es.enter_context(nc.sbuf_tensor(ctx.name("llv"), [128, 16], F32))
        epsb = _es.enter_context(nc.sbuf_tensor(ctx.name("lep"), [128, 1], F32))
        rstd = _es.enter_context(nc.sbuf_tensor(ctx.name("lrs"), [128, 16], F32))
        xn = _es.enter_context(nc.sbuf_tensor(ctx.name("lxn"), [128, D], BF16))
        xnT = _es.enter_context(nc.sbuf_tensor(ctx.name("lxT"), [128, 2, 8, 128], BF16))
        psT = _es.enter_context(nc.psum_tensor(ctx.name("lp"), [128, 8, 128], BF16))
        s_ld = ctx.sem("ln_ld"); s_v = ctx.sem("ln_v")
        s_pe = ctx.sem("ln_pe"); s_ev = ctx.sem("ln_ev")
        s_o2 = [ctx.sem("ln_oA"), ctx.sem("ln_oB")]
        L, V, P, E = s_ld.n, s_v.n, s_pe.n, s_ev.n
        O2 = [x.n for x in s_o2]
        src = src_hdl.ap()
        dst = dst_hdl.ap()
        with nc.Block() as blk:
            @blk.sync
            def _(sy):
                for t in range(16):
                    sy.dma_start(xs[:, t, :], src[t * 128:(t + 1) * 128, :]).then_inc(s_ld.bump(16), 16)
            @blk.vector
            def _(v):
                v.memset(epsb[:], 1e-5)
                v.wait_ge(s_ld.h, L + 256)
                for t in range(16):
                    v.tensor_tensor(sq[:], xs[:, t, :], xs[:, t, :], OP.mult)
                    v.drain()
                    v.reduce_sum(s1[:, t:t + 1], xs[:, t, :], axis=AX)
                    v.reduce_sum(s2[:, t:t + 1], sq[:], axis=AX)
                    v.drain()
                v.tensor_scalar(nmu[:], s1[:], -1.0 / D, None, OP.mult)
                v.tensor_scalar(var[:], s2[:], 1.0 / D, None, OP.mult)
                v.drain()
                v.tensor_tensor(s2[:], nmu[:], nmu[:], OP.mult)
                v.drain()
                v.tensor_tensor(var[:], var[:], s2[:], OP.subtract).then_inc(s_v.bump(), 1)
            @blk.scalar
            def _(sc):
                sc.wait_ge(s_v.h, V + 1)
                sc.activation(lnv[:], var[:], AF.Ln, bias=epsb[:])
                sc.drain()
                sc.activation(rstd[:], lnv[:], AF.Exp, scale=-0.5).then_inc(s_v.bump(), 1)
        with nc.Block() as blk:
            @blk.vector
            def _(v):
                for t in range(16):
                    v.tensor_scalar(xn[:], xs[:, t, :], nmu[:, t:t + 1],
                                    rstd[:, t:t + 1], OP.add, OP.mult).then_inc(s_v.bump(), 1)
                    v.wait_ge(s_pe.h, P + 8 * (t + 1))
                    if t >= 2:
                        v.wait_ge(s_o2[t % 2].h, O2[t % 2] + 16 * (t // 2))
                    v.tensor_copy(xnT[:, t % 2, :, :], psT[:]).then_inc(s_ev.bump(), 1)
            @blk.tensor
            def _(t_):
                for t in range(16):
                    t_.wait_ge(s_v.h, V + 2 + t + 1)
                    if t > 0:
                        t_.wait_ge(s_ev.h, E + t)
                    for cc in range(8):
                        t_.transpose(psT[:, cc, :], xn[:, cc * 128:(cc + 1) * 128],
                                     idb[:]).then_inc(s_pe.bump(), 1)
            @blk.sync
            def _(sy):
                for t in range(16):
                    sy.wait_ge(s_ev.h, E + t + 1)
                    sy.dma_start(dst[:, :, t * 128:(t + 1) * 128].rearrange("c p j -> p c j"),
                                 xnT[:, t % 2, :, :]).then_inc(s_o2[t % 2].bump(16), 16)
                sy.wait_ge(s_o2[0].h, O2[0] + 128)
                sy.wait_ge(s_o2[1].h, O2[1] + 128)



def emit_mixer_a(ctx, parts="all"):
    nc = ctx.nc
    g = ctx.glob
    xn_fm = ctx.dtmp("xn_fm", [8, 128, TOK], BF16)
    ktm_d = ctx.dtmp("ktm", [2, 64, 16, D], BF16)
    vtm_d = ctx.dtmp("vtm", [2, 64, 16, D], BF16)
    cc0s = ctx.dtmp("cc0src", [128, D], F32)
    cc0d = ctx.dtmp("cc0dst", [NCORE * 128, D], F32)
    with contextlib.ExitStack() as _es:
        wsb = _es.enter_context(nc.sbuf_tensor("ma_w", [128, 8, 2048], BF16))
        ktm = _es.enter_context(nc.sbuf_tensor("ma_ktm", [64, 16, D], BF16))
        vtm = _es.enter_context(nc.sbuf_tensor("ma_vtm", [64, 16, D], BF16))
        xnt = _es.enter_context(nc.sbuf_tensor("ma_xn", [128, 2, 8, 128], BF16))
        dbu = _es.enter_context(nc.sbuf_tensor("ma_dbu", [64, D], BF16))
        tdrow_a = _es.enter_context(nc.sbuf_tensor("ma_tdrow", [1, D], F32))
        rev_a = _es.enter_context(nc.sbuf_tensor("ma_rev", [1, SEG], F32))
        kbu = _es.enter_context(nc.sbuf_tensor("ma_kbu", [64, D], BF16))
        usb = _es.enter_context(nc.sbuf_tensor("ma_u", [64, D], F32))
        pj0 = _es.enter_context(nc.psum_tensor("ma_p0", [128, D], F32))
        pj1 = _es.enter_context(nc.psum_tensor("ma_p1", [128, D], F32))
        pdb = _es.enter_context(nc.psum_tensor("ma_pd", [128, D], F32))
        pu = _es.enter_context(nc.psum_tensor("ma_pu", [128, D], F32))
        wk = ctx.din("wk", [D, D]); wv = ctx.din("wv", [D, D])
        s_lw = ctx.sem("ma_lw")
        with nc.Block() as blk:
            LW = s_lw.n
            @blk.gpsimd
            def _(gp):
                gp.dma_start(tdrow_a[:], ctx.din("td_row", [1, D]).ap()).then_inc(s_lw.bump(16), 16)
                gp.dma_start(rev_a[:], ctx.din("revseg", [1, SEG]).ap()).then_inc(s_lw.bump(16), 16)
                for dc in range(8):
                    gp.dma_start(wsb[:, dc, 0:D], wk.ap()[dc * 128:(dc + 1) * 128, :]).then_inc(s_lw.bump(16), 16)
                    gp.dma_start(wsb[:, dc, D:2 * D], wv.ap()[dc * 128:(dc + 1) * 128, :]).then_inc(s_lw.bump(16), 16)
            @blk.vector
            def _(v):
                v.wait_ge(s_lw.h, LW + 288)
                for dc in range(8):
                    v.tensor_scalar(wsb[:, dc, :], wsb[:, dc, :],
                                    g["tm_col"][:, dc:dc + 1], None, OP.mult)
        for s in range(2):
            s_lx = [ctx.sem("ma_lxA"), ctx.sem("ma_lxB")]
            s_pe = ctx.sem("ma_pe"); s_ev = ctx.sem("ma_ev")
            with nc.Block() as blk:
                LX = [x.n for x in s_lx]
                PE, EV = s_pe.n, s_ev.n
                @blk.sync
                def _(sy):
                    for tc in range(8):
                        if tc >= 2:
                            sy.wait_ge(s_pe.h, PE + 4 * (tc - 1))
                        sy.dma_start(
                            xnt[:, tc % 2, :, :],
                            xn_fm.ap()[:, :, s * SEG + tc * 128: s * SEG + (tc + 1) * 128]
                            .rearrange("c p j -> p c j")).then_inc(s_lx[tc % 2].bump(16), 16)
                @blk.tensor
                def _(t_):
                    i = 0
                    for tc in range(8):
                        t_.wait_ge(s_lx[tc % 2].h, LX[tc % 2] + 16 * (tc // 2 + 1))
                        for w in range(2):
                            for jg in range(2):
                                ps = (pj0, pj1)[i % 2]
                                if i >= 2:
                                    t_.wait_ge(s_ev.h, EV + i - 1)
                                last = None
                                for cc2 in range(2):
                                    for dc in range(8):
                                        last = t_.matmul(ps[0:64, cc2 * 512:(cc2 + 1) * 512],
                                                  xnt[:, tc % 2, dc, jg * 64:(jg + 1) * 64],
                                                  wsb[:, dc, w * D + cc2 * 512:w * D + (cc2 + 1) * 512],
                                                  start=(dc == 0), stop=(dc == 7))
                                last.then_inc(s_pe.bump(), 1)
                                i += 1
                @blk.vector
                def _(v):
                    i = 0
                    for tc in range(8):
                        for w in range(2):
                            dstb = (ktm, vtm)[w]
                            for jg in range(2):
                                ps = (pj0, pj1)[i % 2]
                                v.wait_ge(s_pe.h, PE + i + 1)
                                v.tensor_copy(dstb[:, 2 * tc + jg, :], ps[0:64, :]).then_inc(s_ev.bump(), 1)
                                i += 1
            if parts == "prod":
                continue
            s_db = ctx.sem("ma_db"); s_ex = ctx.sem("ma_ex"); s_kb = ctx.sem("ma_kb")
            s_up = ctx.sem("ma_up"); s_sp = ctx.sem("ma_sp")
            with nc.Block() as blk:
                DB, EX, KB, UP, SP = s_db.n, s_ex.n, s_kb.n, s_up.n, s_sp.n
                @blk.tensor
                def _(t_):
                    for gr in range(16):
                        if gr > 0:
                            t_.wait_ge(s_ex.h, EX + gr)
                        last = None
                        for cc2 in range(2):
                            last = t_.matmul(pdb[0:64, cc2 * 512:(cc2 + 1) * 512],
                                      rev_a[:, 64 * gr:64 * gr + 64],
                                      tdrow_a[:, cc2 * 512:(cc2 + 1) * 512],
                                      start=True, stop=True)
                        last.then_inc(s_db.bump(), 1)
                        t_.wait_ge(s_kb.h, KB + gr + 1)
                        last = None
                        for n2 in range(2):
                            n = 2 * gr + n2
                            base = 32 * n2
                            for h in range(H):
                                last = t_.matmul(pu[0:64, h * 64:(h + 1) * 64],
                                          kbu[base:base + 32, h * 64:(h + 1) * 64],
                                          vtm[base:base + 32, gr, h * 64:(h + 1) * 64],
                                          start=(n == 0 and h in (0, 8)),
                                          stop=(n == NCHUNK - 1 and h in (7, 15)))
                        last.then_inc(s_up.bump(), 1)
                @blk.scalar
                def _(sc):
                    for gr in range(16):
                        sc.wait_ge(s_db.h, DB + gr + 1)
                        if gr > 0:
                            sc.wait_ge(s_kb.h, KB + gr)
                        sc.activation(dbu[:], pdb[0:64, :], AF.Exp, scale=-1.0).then_inc(s_ex.bump(), 1)
                @blk.vector
                def _(v):
                    for gr in range(16):
                        v.wait_ge(s_ex.h, EX + gr + 1)
                        if gr > 0:
                            v.wait_ge(s_up.h, UP + gr)
                        v.tensor_tensor(kbu[:], ktm[:, gr, :], dbu[:], OP.mult).then_inc(s_kb.bump(), 1)
                    v.wait_ge(s_up.h, UP + 16)
                    v.tensor_copy(usb[:], pu[0:64, :]).then_inc(s_kb.bump(), 1)
                @blk.sync
                def _(sy):
                    sy.wait_ge(s_kb.h, KB + 17)
                    sy.dma_start(cc0s.ap()[s * 64:(s + 1) * 64, :], usb[:]).then_inc(s_sp.bump(16), 16)
                    if "cc0dbg" in ctx.taps:
                        dbg = ctx.dtmp("cc0dbg", [128, D], F32)
                        sy.dma_start(dbg.ap()[s * 64:(s + 1) * 64, :], usb[:]).then_inc(s_sp.bump(16), 16)
                    sy.dma_start(ktm_d.ap()[s], ktm[:]).then_inc(s_sp.bump(16), 16)
                    sy.dma_start(vtm_d.ap()[s], vtm[:]).then_inc(s_sp.bump(16), 16)
                    sy.wait_ge(s_sp.h, s_sp.n)
        if parts == "nocc" or SIM_SKIP_CC:
            ctx.sem("cc0").bump()
            return
        s_cc0 = ctx.sem("cc0")
        with nc.Block() as blk:
            @blk.gpsimd
            def _(gp):
                gp.collective_compute(
                    "AllGather", OP.bypass, replica_groups=[list(range(NCORE))],
                    ins=[ctx.dram["cc0src"].ap().opt()],
                    outs=[ctx.dram["cc0dst"].ap().opt()]).then_inc(s_cc0.bump(), 1)



def emit_moe(ctx, layer):
    nc = ctx.nc
    g = ctx.glob
    E = 8 if layer == 0 else 16
    act = AF.Relu if layer == 0 else AF.Gelu
    xn_fm = ctx.dtmp("xn_fm", [8, 128, TOK], BF16)
    g_d = ctx.dtmp(f"g{layer}", [128, 16, D], BF16)
    if layer == 0:
        w1 = ctx.din("m0_w1t", [E, D, 2048]); w2 = ctx.din("m0_w2t", [E, 2048, D])
        b1 = ctx.din("m0_b1", [E, 128, 16]); b2 = ctx.din("m0_b2", [E, D])
        wg_h = ctx.din("m0_wg", [D, E])
    else:
        w1 = ctx.din("m1_w1", [E, D, 2048]); w2 = ctx.din("m1_w2", [E, 2048, D])
        b1 = ctx.din("m1_b1", [E, 128, 16]); b2 = ctx.din("m1_b2", [E, D])
        wg_h = ctx.din("rw", [D, E])
    with contextlib.ExitStack() as _es:
        comb = _es.enter_context(nc.sbuf_tensor(ctx.name("mcb"), [128, 16, E], F32))
        lgall = _es.enter_context(nc.sbuf_tensor(ctx.name("mlg"), [128, 16, E], F32))
        m1n = _es.enter_context(nc.sbuf_tensor(ctx.name("mm1"), [128, 16], F32))
        m2a = _es.enter_context(nc.sbuf_tensor(ctx.name("mm2"), [128, 16], F32))
        em2 = _es.enter_context(nc.sbuf_tensor(ctx.name("mem"), [128, 16], F32))
        aux = _es.enter_context(nc.sbuf_tensor(ctx.name("mau"), [128, 16], F32))
        bgb = _es.enter_context(nc.sbuf_tensor(ctx.name("mgb"), [128, E], F32))
        xng = _es.enter_context(nc.sbuf_tensor(ctx.name("mxg"), [128, 2, 8, 128], BF16))
        bgsb = _es.enter_context(nc.sbuf_tensor(ctx.name("mbg16"), [1, 8], BF16))
        wgsb = _es.enter_context(nc.sbuf_tensor(ctx.name("mwg"), [128, 8, E], BF16))
        gacc = _es.enter_context(nc.sbuf_tensor(ctx.name("mga"), [128, 16, D], F32))
        pga = _es.enter_context(nc.psum_tensor(ctx.name("mpga"), [128, 512], F32))
        pgb = _es.enter_context(nc.psum_tensor(ctx.name("mpgb"), [128, 512], F32))
        s_l = ctx.sem("mo_l"); s_gp = ctx.sem("mo_gp"); s_gv = ctx.sem("mo_gv")
        s_gx = [ctx.sem("mo_gxA"), ctx.sem("mo_gxB")]
        with nc.Block() as blk:
            L0_, GP, GV = s_l.n, s_gp.n, s_gv.n
            GX = [x.n for x in s_gx]
            @blk.gpsimd
            def _(gp_):
                for dc in range(8):
                    gp_.dma_start(wgsb[:, dc, :], wg_h.ap()[dc * 128:(dc + 1) * 128, :]).then_inc(s_l.bump(16), 16)
                if layer == 0:
                    gp_.dma_start(bgsb[:], ctx.din("m0_bg", [1, 8]).ap()).then_inc(s_l.bump(16), 16)
            @blk.sync
            def _(sy):
                for tc in range(16):
                    if tc >= 2:
                        sy.wait_ge(s_gp.h, GP + tc - 1)
                    sy.dma_start(
                        xng[:, tc % 2, :, :],
                        xn_fm.ap()[:, :, tc * 128:(tc + 1) * 128].rearrange("c p j -> p c j")
                    ).then_inc(s_gx[tc % 2].bump(16), 16)
            @blk.tensor
            def _(t_):
                t_.wait_ge(s_l.h, L0_ + 16 * (9 if layer == 0 else 8))
                for tc in range(16):
                    t_.wait_ge(s_gx[tc % 2].h, GX[tc % 2] + 16 * (tc // 2 + 1))
                    if tc >= 2:
                        t_.wait_ge(s_gv.h, GV + tc - 1)
                    pgx = (pga, pgb)[tc % 2]
                    last = None
                    for dc in range(8):
                        last = t_.matmul(pgx[:, 0:E], xng[:, tc % 2, dc, :], wgsb[:, dc, :],
                                  start=(dc == 0), stop=(dc == 7))
                    last.then_inc(s_gp.bump(), 1)
            @blk.vector
            def _(v):
                for tc in range(16):
                    v.wait_ge(s_gp.h, GP + tc + 1)
                    v.tensor_copy(lgall[:, tc, :], (pga, pgb)[tc % 2][:, 0:E]).then_inc(s_gv.bump(), 1)
                for t in range(16):
                    v.memset(gacc[:, t, :], 0.0)
        if layer == 0:
            with nc.Block() as blk:
                @blk.tensor
                def _(t_):
                    t_.matmul(pga[:, 0:E], g["ones1b"][:], bgsb[:], start=True, stop=True)
            with nc.Block() as blk:
                @blk.vector
                def _(v):
                    v.tensor_copy(bgb[:], pga[:, 0:E])
                    v.drain()
                    for tc in range(16):
                        v.tensor_tensor(lgall[:, tc, :], lgall[:, tc, :], bgb[:], OP.add)
        with nc.Block() as blk:
            @blk.vector
            def _(v):
                for tc in range(16):
                    v.reduce_max(m1n[:, tc:tc + 1], lgall[:, tc, :], axis=AX, negate=True)
                v.drain()
                if layer == 0:
                    for tc in range(16):
                        v.tensor_scalar(aux[:, tc:tc + 1], m1n[:, tc:tc + 1], -1.0, None, OP.mult)
                    v.drain()
                    for tc in range(16):
                        v.tensor_scalar(comb[:, tc, :], lgall[:, tc, :], aux[:, tc:tc + 1], None, OP.is_ge)
                        v.drain()
                        v.scalar_tensor_tensor(comb[:, tc, :], comb[:, tc, :], NEGBIG,
                                               lgall[:, tc, :], OP.mult, OP.add)
                        v.drain()
                        v.reduce_max(m2a[:, tc:tc + 1], comb[:, tc, :], axis=AX)
                        v.drain()
                        v.tensor_scalar(comb[:, tc, :], lgall[:, tc, :], m2a[:, tc:tc + 1],
                                        None, OP.is_ge)
                else:
                    for tc in range(16):
                        v.tensor_scalar(aux[:, tc:tc + 1], m1n[:, tc:tc + 1], -1.0, None, OP.mult)
                        v.drain()
                        v.tensor_scalar(comb[:, tc, :], lgall[:, tc, :], aux[:, tc:tc + 1],
                                        None, OP.is_ge)
        with nc.Block() as blk:
            @blk.scalar
            def _(sc):
                if layer == 0:
                    for tc in range(16):
                        sc.activation(lgall[:, tc, :], lgall[:, tc, :], AF.Exp,
                                      bias=m1n[:, tc:tc + 1])
                        sc.activation(em2[:, tc:tc + 1], m2a[:, tc:tc + 1], AF.Exp,
                                      bias=m1n[:, tc:tc + 1])
                else:
                    for tc in range(16):
                        sc.activation(em2[:, tc:tc + 1], m1n[:, tc:tc + 1], AF.Exp)
        with nc.Block() as blk:
            @blk.vector
            def _(v):
                v.tensor_scalar(aux[:], em2[:], 1.0, None, OP.add)
                v.drain()
                v.reciprocal(em2[:], aux[:])
                v.drain()
                if layer == 0:
                    for tc in range(16):
                        v.tensor_tensor(comb[:, tc, :], comb[:, tc, :], lgall[:, tc, :], OP.mult)
                        v.drain()
                        v.tensor_scalar(comb[:, tc, :], comb[:, tc, :], em2[:, tc:tc + 1],
                                        None, OP.mult)
                        v.drain()
                else:
                    for tc in range(16):
                        v.tensor_scalar(comb[:, tc, :], comb[:, tc, :], em2[:, tc:tc + 1],
                                        None, OP.mult)
        with contextlib.ExitStack() as _es:
            w1sb = _es.enter_context(nc.sbuf_tensor(ctx.name("mw1"), [128, 8, 2048], BF16))
            w2sb = _es.enter_context(nc.sbuf_tensor(ctx.name("mw2"), [128, 16, D], BF16))
            hsb = _es.enter_context(nc.sbuf_tensor(ctx.name("mh"), [128, 16, 512], BF16))
            xnt = _es.enter_context(nc.sbuf_tensor(ctx.name("mx5"), [128, 2, 8, 512], BF16))
            b1sb = _es.enter_context(nc.sbuf_tensor(ctx.name("mb1"), [128, 2, 16], F32))
            b2sb = _es.enter_context(nc.sbuf_tensor(ctx.name("mb2"), [1, 2, D], BF16))
            ph = _es.enter_context(nc.psum_tensor(ctx.name("mph"), [128, 2, 512], F32))
            po = _es.enter_context(nc.psum_tensor(ctx.name("mpo"), [128, 2, 512], F32))
            s_w1 = ctx.sem("mo_w1"); s_w2 = ctx.sem("mo_w2")
            s_b1s = ctx.sem("mo_b1"); s_b2s = ctx.sem("mo_b2")
            s_xn = [ctx.sem("mo_xnA"), ctx.sem("mo_xnB")]
            s_hp = ctx.sem("mo_hp"); s_ha = ctx.sem("mo_ha")
            s_ht = ctx.sem("mo_ht"); s_op = ctx.sem("mo_op"); s_mac = ctx.sem("mo_mac")
            s_ot = ctx.sem("mo_ot")
            with nc.Block() as blk:
                W1, W2, HP, HA, OPM, MAC = (
                    s_w1.n, s_w2.n, s_hp.n, s_ha.n, s_op.n, s_mac.n)
                B1, B2 = s_b1s.n, s_b2s.n
                XN = [x.n for x in s_xn]
                @blk.gpsimd
                def _(gp_):
                    for e in range(E):
                        if e >= 1:
                            gp_.wait_ge(s_hp.h, HP + 64 * e)
                        for dc in range(8):
                            gp_.dma_start(w1sb[:, dc, :], w1.ap()[e, dc * 128:(dc + 1) * 128, :]
                                          ).then_inc(s_w1.bump(16), 16)
                        if e >= 2:
                            gp_.wait_ge(s_op.h, OPM + 32 * (e - 1))
                        gp_.dma_start(b1sb[:, e % 2, :], b1.ap()[e]).then_inc(s_b1s.bump(16), 16)
                        gp_.dma_start(b2sb[0:1, e % 2, :], b2.ap()[e:e + 1, :]).then_inc(s_b2s.bump(16), 16)
                        if e >= 1:
                            gp_.wait_ge(s_op.h, OPM + 32 * e)
                        for fc in range(16):
                            gp_.dma_start(w2sb[:, fc, :], w2.ap()[e, fc * 128:(fc + 1) * 128, :]
                                          ).then_inc(s_w2.bump(16), 16)
                @blk.sync
                def _(sy):
                    i = 0
                    for e in range(E):
                        for hf in range(2):
                            for t5 in range(2):
                                if i >= 2:
                                    sy.wait_ge(s_hp.h, HP + 16 * (i - 1))
                                sy.dma_start(
                                    xnt[:, i % 2, :, :],
                                    xn_fm.ap()[:, :, hf * 1024 + t5 * 512:hf * 1024 + (t5 + 1) * 512]
                                    .rearrange("c p j -> p c j")).then_inc(s_xn[i % 2].bump(16), 16)
                                i += 1
                @blk.tensor
                def _(t_):
                    i = 0
                    for e in range(E):
                        for hf in range(2):
                            for t5 in range(2):
                                t_.wait_ge(s_w1.h, W1 + 128 * (e + 1))
                                t_.wait_ge(s_xn[i % 2].h, XN[i % 2] + 16 * (i // 2 + 1))
                                for fc in range(16):
                                    gi = (i * 16 + fc)
                                    if gi >= 2:
                                        t_.wait_ge(s_ha.h, HA + gi - 1)
                                    last = None
                                    for dc in range(8):
                                        last = t_.matmul(ph[:, gi % 2, :], w1sb[:, dc, fc * 128:(fc + 1) * 128],
                                                  xnt[:, i % 2, dc, :], start=(dc == 0), stop=(dc == 7))
                                    last.then_inc(s_hp.bump(), 1)
                                t_.wait_ge(s_w2.h, W2 + 256 * (e + 1))
                                t_.wait_ge(s_b2s.h, B2 + 16 * (e + 1))
                                t_.wait_ge(s_ha.h, HA + 16 * (i + 1))
                                for tloc in range(4):
                                    for ot in range(2):
                                        oi = i * 8 + tloc * 2 + ot
                                        if oi >= 2:
                                            t_.wait_ge(s_mac.h, MAC + oi - 1)
                                        for fc in range(16):
                                            t_.matmul(po[:, oi % 2, :], hsb[:, fc, tloc * 128:(tloc + 1) * 128],
                                                      w2sb[:, fc, ot * 512:(ot + 1) * 512],
                                                      start=(fc == 0), stop=False)
                                        t_.matmul(po[:, oi % 2, :], g["ones1b"][:],
                                                  b2sb[0:1, e % 2, ot * 512:(ot + 1) * 512],
                                                  start=False, stop=True).then_inc(s_op.bump(), 1)
                                i += 1
                @blk.scalar
                def _(sc):
                    i = 0
                    for e in range(E):
                        for hf in range(2):
                            for t5 in range(2):
                                for fc in range(16):
                                    gi = i * 16 + fc
                                    sc.wait_ge(s_hp.h, HP + gi + 1)
                                    if hf == 0 and t5 == 0 and fc == 0:
                                        sc.wait_ge(s_b1s.h, B1 + 16 * (e + 1))
                                    if i >= 1 and fc == 0:
                                        sc.wait_ge(s_op.h, OPM + 8 * i)
                                    sc.activation(hsb[:, fc, :],
                                                  ph[:, gi % 2, :], act,
                                                  bias=b1sb[:, e % 2, fc:fc + 1]).then_inc(s_ha.bump(), 1)
                                i += 1
                @blk.vector
                def _(v):
                    i = 0
                    for e in range(E):
                        for hf in range(2):
                            for t5 in range(2):
                                for tloc in range(4):
                                    for ot in range(2):
                                        oi = i * 8 + tloc * 2 + ot
                                        tg = hf * 8 + t5 * 4 + tloc
                                        v.wait_ge(s_op.h, OPM + oi + 1)
                                        v.scalar_tensor_tensor(
                                            gacc[:, tg, ot * 512:(ot + 1) * 512],
                                            po[:, oi % 2, :], comb[:, tg, e:e + 1],
                                            gacc[:, tg, ot * 512:(ot + 1) * 512],
                                            OP.mult, OP.add).then_inc(s_mac.bump(), 1)
                                i += 1
            with (nc.sbuf_tensor(ctx.name("mgs"), [128, 2, D], BF16) as gsb,):
                s_sg = ctx.sem("mo_sg")
                s_go = [ctx.sem("mo_goA"), ctx.sem("mo_goB")]
                with nc.Block() as blk:
                    SG = s_sg.n
                    GO = [x.n for x in s_go]
                    @blk.scalar
                    def _(sc):
                        for t in range(16):
                            if t >= 2:
                                sc.wait_ge(s_go[t % 2].h, GO[t % 2] + 16 * (t // 2))
                            sc.activation(gsb[:, t % 2, :], gacc[:, t, :], AF.Silu).then_inc(s_sg.bump(), 1)
                    @blk.sync
                    def _(sy):
                        for t in range(16):
                            sy.wait_ge(s_sg.h, SG + t + 1)
                            sy.dma_start(g_d.ap()[:, t, :], gsb[:, t % 2, :]).then_inc(s_go[t % 2].bump(16), 16)
                        sy.wait_ge(s_go[0].h, GO[0] + 128)
                        sy.wait_ge(s_go[1].h, GO[1] + 128)


def emit_mixer_b(ctx, parts="all"):
    nc = ctx.nc
    g = ctx.glob
    x_in = ctx.din("x", [TOK, D])
    xn_fm = ctx.dtmp("xn_fm", [8, 128, TOK], BF16)
    ktm_d = ctx.dtmp("ktm", [2, 64, 16, D], BF16)
    vtm_d = ctx.dtmp("vtm", [2, 64, 16, D], BF16)
    cc0d = ctx.dtmp("cc0dst", [NCORE * 128, D], F32)
    g0_d = ctx.dtmp("g0", [128, 16, D], BF16)
    x1_d = ctx.dtmp("x1", [TOK, D], F32)
    wk = ctx.din("wk", [D, D]); wv = ctx.din("wv", [D, D])
    wr = ctx.din("wr", [D, D]); wgm = ctx.din("wgm", [D, D])
    wo = ctx.din("wo", [D, D])
    with contextlib.ExitStack() as mbes:
        ec_ = mbes.enter_context
        ktmq = ec_(nc.sbuf_tensor("mb_ktq", [64, 2, 4, D], BF16))
        vtmq = ec_(nc.sbuf_tensor("mb_vtq", [64, 2, 4, D], BF16))
        kfm = ec_(nc.sbuf_tensor("mb_kfm", [128, 8, SEG], BF16))
        vfm = ec_(nc.sbuf_tensor("mb_vfm", [128, 8, SEG], BF16))
        wbuf = ec_(nc.sbuf_tensor("mb_w", [128, 8, 2048], BF16))
        wosb = ec_(nc.sbuf_tensor("mb_wo", [128, 8, D], BF16))
        rq = ec_(nc.sbuf_tensor("mb_rq", [128, 8, 256], BF16))
        gq = ec_(nc.sbuf_tensor("mb_gq", [128, 8, 256], BF16))
        xnq = ec_(nc.sbuf_tensor("mb_xq", [128, 2, 8, 256], BF16))
        khb = ec_(nc.sbuf_tensor("mb_kh", [128, 2, 8, CH], BF16))
        ktb = ec_(nc.sbuf_tensor("mb_kt", [128, 2, 8, CH], BF16))
        kb = ec_(nc.sbuf_tensor("mb_kb", [64, D], BF16))
        am = ec_(nc.sbuf_tensor("mb_am", [64, 512], BF16))
        S = ec_(nc.sbuf_tensor("mb_S", [128, D], F32))
        S2 = ec_(nc.sbuf_tensor("mb_S2", [128, D], BF16))
        St = ec_(nc.sbuf_tensor("mb_St", [128, D], F32))
        tw = ec_(nc.sbuf_tensor("mb_tw", [128, 8, CH], F32))
        ws = ec_(nc.sbuf_tensor("mb_ws", [128, 8, CH], F32))
        rp = ec_(nc.sbuf_tensor("mb_rp", [128, 8, 256], BF16))
        usl = ec_(nc.sbuf_tensor("mb_us", [128, D], F32))
        xres = ec_(nc.sbuf_tensor("mb_xr", [128, 2, D], F32))
        gres = ec_(nc.sbuf_tensor("mb_gr", [128, 2, D], BF16))
        x1sb = ec_(nc.sbuf_tensor("mb_x1", [128, 2, D], F32))
        cfn = ec_(nc.sbuf_tensor("mb_cn", [16, 2, 64], F32))
        for s in range(2):
            s_lw = ctx.sem("mb_lw")
            with nc.Block() as blk:
                LW = s_lw.n
                @blk.gpsimd
                def _(gp_):
                    for dc in range(8):
                        gp_.dma_start(wbuf[:, dc, 0:D], wk.ap()[dc * 128:(dc + 1) * 128, :]).then_inc(s_lw.bump(16), 16)
                        gp_.dma_start(wbuf[:, dc, D:2 * D], wv.ap()[dc * 128:(dc + 1) * 128, :]).then_inc(s_lw.bump(16), 16)
                        gp_.dma_start(wosb[:, dc, :], wo.ap()[dc * 128:(dc + 1) * 128, :]).then_inc(s_lw.bump(16), 16)
                @blk.vector
                def _(v):
                    v.wait_ge(s_lw.h, LW + 16 * 24)
                    for dc in range(8):
                        v.tensor_scalar(wbuf[:, dc, :], wbuf[:, dc, :],
                                        g["tm_col"][:, dc:dc + 1], None, OP.mult)
            s_xf = [ctx.sem("mb_xfA"), ctx.sem("mb_xfB")]
            s_fp = ctx.sem("mb_fp"); s_fe = ctx.sem("mb_fe")
            with contextlib.ExitStack() as pes1:
                pfa = pes1.enter_context(nc.psum_tensor(ctx.name("mbpfa"), [128, 512], F32))
                pfb = pes1.enter_context(nc.psum_tensor(ctx.name("mbpfb"), [128, 512], F32))
                with nc.Block() as blk:
                    FP, FE = s_fp.n, s_fe.n
                    XF = [x.n for x in s_xf]
                    @blk.sync
                    def _(sy):
                        for qq in range(4):
                            if qq >= 2:
                                sy.wait_ge(s_fp.h, FP + 16 * (qq - 1))
                            sy.dma_start(
                                xnq[:, qq % 2, :, :],
                                xn_fm.ap()[:, :, s * SEG + qq * 256:s * SEG + (qq + 1) * 256]
                                .rearrange("c p j -> p c j")).then_inc(s_xf[qq % 2].bump(16), 16)
                    @blk.tensor
                    def _(t_):
                        i = 0
                        for qq in range(4):
                            t_.wait_ge(s_xf[qq % 2].h, XF[qq % 2] + 16 * (qq // 2 + 1))
                            for w in range(2):
                                for ec in range(8):
                                    if i >= 2:
                                        t_.wait_ge(s_fe.h, FE + i - 1)
                                    last = None
                                    for dc in range(8):
                                        last = t_.matmul((pfa, pfb)[i % 2][:, 0:256],
                                                  wbuf[:, dc, w * D + ec * 128:w * D + (ec + 1) * 128],
                                                  xnq[:, qq % 2, dc, :],
                                                  start=(dc == 0), stop=(dc == 7))
                                    last.then_inc(s_fp.bump(), 1)
                                    i += 1
                    @blk.vector
                    def _(v):
                        i = 0
                        for qq in range(4):
                            for w in range(2):
                                dstb = (kfm, vfm)[w]
                                for ec in range(8):
                                    v.wait_ge(s_fp.h, FP + i + 1)
                                    v.tensor_copy(dstb[:, ec, qq * 256:(qq + 1) * 256],
                                                  (pfa, pfb)[i % 2][:, 0:256]).then_inc(s_fe.bump(), 1)
                                    i += 1
            if parts == "m1":
                continue
            s_cs = [ctx.sem("mb_csA"), ctx.sem("mb_csB")]
            s_cp = ctx.sem("mb_cp"); s_cv = ctx.sem("mb_cv")
            s_ca = ctx.sem("mb_ca")
            with contextlib.ExitStack() as pes2:
                pco = pes2.enter_context(nc.psum_tensor(ctx.name("mbpc"), [128, D], F32))
                with nc.Block() as blk:
                    CP, CV, CA = s_cp.n, s_cv.n, s_ca.n
                    CS = [x.n for x in s_cs]
                    s_cc0 = ctx.sem("cc0")
                    @blk.vector
                    def _(v):
                        v.memset(S[:], 0.0)
                        for sl in range(16):
                            v.wait_ge(s_cs[0].h, CS[0] + 32 * (sl + 1))
                            v.wait_ge(s_cp.h, CP + sl + 1)
                            v.tensor_tensor(St[:], pco[:], usl[:], OP.mult)
                            v.drain()
                            v.tensor_tensor(S[:], S[:], St[:], OP.add).then_inc(s_cv.bump(), 1)
                            v.drain()
                        v.tensor_copy(S2[:], S[:])
                    @blk.sync
                    def _(sy):
                        if not SIM_SKIP_CC:
                            sy.wait_ge(s_cc0.h, 1)
                        for sl in range(16):
                            if sl >= 1:
                                sy.wait_ge(s_cv.h, CV + sl)
                            sy.dma_start(usl[0:64, :],
                                         cc0d.ap()[sl * 64:(sl + 1) * 64, :]).then_inc(s_cs[0].bump(16), 16)
                            sy.dma_start(usl[64:128, :],
                                         cc0d.ap()[sl * 64:(sl + 1) * 64, :]).then_inc(s_cs[0].bump(16), 16)
                    @blk.scalar
                    def _(sc):
                        for sl in range(16):
                            if sl >= 1:
                                sc.wait_ge(s_cp.h, CP + sl)
                            sc.activation(cfn[:, sl % 2, :], g["td_nat"][:], AF.Exp,
                                          scale=g["nexpo"][0:16, s * 16 + sl:s * 16 + sl + 1]
                                          ).then_inc(s_ca.bump(), 1)
                    @blk.tensor
                    def _(t_):
                        for sl in range(16):
                            t_.wait_ge(s_ca.h, CA + sl + 1)
                            if sl >= 1:
                                t_.wait_ge(s_cv.h, CV + sl)
                            last = None
                            for hh in (0, 64):
                                for cc in (0, 512):
                                    last = t_.matmul(pco[hh:hh + 64, cc:cc + 512],
                                              cfn[:, sl % 2, :], g["hsel"][:, cc:cc + 512],
                                              start=True, stop=True)
                            last.then_inc(s_cp.bump(), 1)
            if parts == "m2":
                continue
            m3rg = parts == "m3rg"
            m3ch = parts in ("m3rg", "m3ch", "m3c1", "m3c2")
            c_wkv = parts not in ("m3c1",)
            c_u = parts not in ("m3c1", "m3c2")
            s_fold = ctx.sem("mb_fo")
            s_xq2 = [ctx.sem("mb_x2A"), ctx.sem("mb_x2B")]
            s_kv2 = [ctx.sem("mb_kvA"), ctx.sem("mb_kvB")]
            s_rgp = ctx.sem("mb_rg")
            s_rge = ctx.sem("mb_re"); s_ge = ctx.sem("mb_gse")
            s_pA = ctx.sem("mb_pA"); s_pw = ctx.sem("mb_pw"); s_pU = ctx.sem("mb_pU")
            s_wo2 = ctx.sem("mb_wo"); s_dkh = ctx.sem("mb_d1"); s_dam = ctx.sem("mb_d2")
            s_drg = ctx.sem("mb_d3"); s_dkb = ctx.sem("mb_d4"); s_dsu = ctx.sem("mb_d5")
            s_rs = ctx.sem("mb_rs")
            s_xt = [ctx.sem("mb_xtA"), ctx.sem("mb_xtB")]
            s_gt = [ctx.sem("mb_gtA"), ctx.sem("mb_gtB")]
            s_xo = [ctx.sem("mb_xoA"), ctx.sem("mb_xoB")]
            s_lw2 = ctx.sem("mb_l2")
            with contextlib.ExitStack() as pes3:
                pA = pes3.enter_context(nc.psum_tensor(ctx.name("mbpa"), [128, 512], F32))
                pw = pes3.enter_context(nc.psum_tensor(ctx.name("mbpw"), [128, 16, CH], F32))
                pU = pes3.enter_context(nc.psum_tensor(ctx.name("mbpu"), [128, D], F32))
                pRGa = pes3.enter_context(nc.psum_tensor(ctx.name("mbra"), [128, 512], F32))
                pRGb = pes3.enter_context(nc.psum_tensor(ctx.name("mbrb"), [128, 512], F32))
                ps_s = pes3.enter_context(nc.psum_tensor(ctx.name("mbps"), [128, 2, 512], F32))
                with nc.Block() as blk:
                    FO, RGP, RGE, GE = s_fold.n, s_rgp.n, s_rge.n, s_ge.n
                    PA, PW, PU, WO2 = s_pA.n, s_pw.n, s_pU.n, s_wo2.n
                    KH, AM, RG3, KB4, SU = s_dkh.n, s_dam.n, s_drg.n, s_dkb.n, s_dsu.n
                    RS, LW2 = s_rs.n, s_lw2.n
                    XQ = [x.n for x in s_xq2]
                    KV2 = [x.n for x in s_kv2]
                    XT = [x.n for x in s_xt]
                    GT = [x.n for x in s_gt]
                    XO = [x.n for x in s_xo]
                    @blk.gpsimd
                    def _(gp_):
                        for dc in range(8):
                            gp_.dma_start(wbuf[:, dc, 0:D], wr.ap()[dc * 128:(dc + 1) * 128, :]).then_inc(s_lw2.bump(16), 16)
                            gp_.dma_start(wbuf[:, dc, D:2 * D], wgm.ap()[dc * 128:(dc + 1) * 128, :]).then_inc(s_lw2.bump(16), 16)
                        for tc in range(8):
                            if m3ch:
                                continue
                            if tc >= 2:
                                gp_.wait_ge(s_rs.h, RS + tc - 1)
                            gp_.dma_start(gres[:, tc % 2, :],
                                          g0_d.ap()[:, s * 8 + tc, :]).then_inc(s_gt[tc % 2].bump(16), 16)
                    @blk.sync
                    def _(sy):
                        for q in range(4):
                            if q >= 2:
                                sy.wait_ge(s_rgp.h, RGP + 16 * (q - 1))
                                if not m3rg:
                                    sy.wait_ge(s_pU.h, PU + DEBUG_NCH * (q - 1))
                            sy.dma_start(
                                xnq[:, q % 2, :, :],
                                xn_fm.ap()[:, :, s * SEG + q * 256:s * SEG + (q + 1) * 256]
                                .rearrange("c p j -> p c j")).then_inc(s_xq2[q % 2].bump(16), 16)
                            if not m3rg:
                                sy.dma_start(ktmq[:, q % 2, :, :],
                                             ktm_d.ap()[s, :, 4 * q:4 * q + 4, :]).then_inc(s_kv2[q % 2].bump(16), 16)
                                sy.dma_start(vtmq[:, q % 2, :, :],
                                             vtm_d.ap()[s, :, 4 * q:4 * q + 4, :]).then_inc(s_kv2[q % 2].bump(16), 16)
                            if m3ch:
                                continue
                            for tc in (2 * q, 2 * q + 1):
                                if tc >= 2:
                                    sy.wait_ge(s_rs.h, RS + tc - 1)
                                sy.dma_start(xres[:, tc % 2, :],
                                             x_in.ap()[s * SEG + tc * 128:s * SEG + (tc + 1) * 128, :]
                                             ).then_inc(s_xt[tc % 2].bump(16), 16)
                                if tc >= 2:
                                    to = tc - 2
                                    sy.wait_ge(s_rs.h, RS + to + 1)
                                    sy.dma_start(x1_d.ap()[s * SEG + to * 128:s * SEG + (to + 1) * 128, :],
                                                 x1sb[:, to % 2, :]).then_inc(s_xo[to % 2].bump(16), 16)
                        if not m3ch:
                            for tc in (6, 7):
                                sy.wait_ge(s_rs.h, RS + tc + 1)
                                sy.dma_start(x1_d.ap()[s * SEG + tc * 128:s * SEG + (tc + 1) * 128, :],
                                             x1sb[:, tc % 2, :]).then_inc(s_xo[tc % 2].bump(16), 16)
                            sy.wait_ge(s_xo[0].h, XO[0] + 64)
                            sy.wait_ge(s_xo[1].h, XO[1] + 64)
                    @blk.vector
                    def _(v):
                        v.wait_ge(s_lw2.h, LW2 + 256)
                        last = None
                        for dc in range(8):
                            last = v.tensor_scalar(wbuf[:, dc, :], wbuf[:, dc, :],
                                            g["tm_col"][:, dc:dc + 1], None, OP.mult)
                        last.then_inc(s_fold.bump(), 1)
                        for q in range(4):
                            for ec in range(8):
                                v.wait_ge(s_rgp.h, RGP + 16 * q + ec + 1)
                                v.tensor_copy(rq[:, ec, :], (pRGa, pRGb)[ec % 2][:, 0:256]).then_inc(s_rge.bump(), 1)
                            if m3rg:
                                continue
                            v.wait_ge(s_kv2[q % 2].h, KV2[q % 2] + 32 * (q // 2 + 1))
                            for m in range(8 * q, 8 * q + DEBUG_NCH):
                                sl = slice(m * CH, (m + 1) * CH)
                                par = m % 2
                                if m >= 2 and c_wkv:
                                    v.wait_ge(s_pw.h, PW + m - 1)
                                v.tensor_tensor(khb[:, par, :, :], kfm[:, :, sl], g["dhat"][:], OP.mult)
                                v.tensor_tensor(ktb[:, par, :, :], kfm[:, :, sl], g["dneg"][:], OP.mult).then_inc(s_dkh.bump(), 1)
                                v.wait_ge(s_pA.h, PA + m + 1)
                                v.tensor_tensor(am[32 * par:32 * par + 32, :],
                                                pA[32 * par:32 * par + 32, :],
                                                g["mask64"][32 * par:32 * par + 32, :],
                                                OP.mult).then_inc(s_dam.bump(), 1)
                                if not c_wkv:
                                    v.nop().then_inc(s_drg.bump(), 1)
                                    v.nop().then_inc(s_dkb.bump(), 1)
                                    v.nop().then_inc(s_dsu.bump(), 1)
                                    continue
                                v.tensor_tensor(tw[:], kfm[:, :, sl], vfm[:, :, sl], OP.mult)
                                v.drain()
                                v.tensor_tensor(tw[:], tw[:], g["tfmap"][:], OP.mult)
                                v.wait_ge(s_pw.h, PW + m + 1)
                                v.wait_ge(s_ge.h, GE + 8 * q + 8)
                                v.drain()
                                v.tensor_tensor(ws[:], pw[:, 0:8, :], tw[:], OP.add)
                                v.drain()
                                v.tensor_tensor(ws[:], ws[:], rq[:, :, (m % 8) * CH:(m % 8 + 1) * CH], OP.mult)
                                v.drain()
                                v.tensor_tensor(rp[:, :, (m % 8) * CH:(m % 8 + 1) * CH],
                                                ws[:], gq[:, :, (m % 8) * CH:(m % 8 + 1) * CH],
                                                OP.mult).then_inc(s_drg.bump(), 1)
                                if not c_u:
                                    v.nop().then_inc(s_dkb.bump(), 1)
                                    v.nop().then_inc(s_dsu.bump(), 1)
                                    continue
                                if m >= 2:
                                    v.wait_ge(s_pU.h, PU + m - 1)
                                v.tensor_tensor(kb[32 * par:32 * par + 32, :],
                                                ktmq[32 * par:32 * par + 32, q % 2, m // 2 - 4 * q, :],
                                                g["dbar32"][32 * par:32 * par + 32, :],
                                                OP.mult).then_inc(s_dkb.bump(), 1)
                                v.wait_ge(s_pU.h, PU + m + 1)
                                v.tensor_tensor(St[:], S[:], g["dc32"][:], OP.mult)
                                v.drain()
                                v.tensor_tensor(S2[:], St[:], pU[:], OP.add).then_inc(s_dsu.bump(), 1)
                                v.drain()
                                v.tensor_tensor(S[:], St[:], pU[:], OP.add)
                                v.drain()
                            for hh in range(2):
                                if m3ch:
                                    continue
                                tc = 2 * q + hh
                                v.wait_ge(s_wo2.h, WO2 + tc + 1)
                                v.wait_ge(s_xt[tc % 2].h, XT[tc % 2] + 16 * (tc // 2 + 1))
                                v.wait_ge(s_gt[tc % 2].h, GT[tc % 2] + 16 * (tc // 2 + 1))
                                if tc >= 2:
                                    v.wait_ge(s_xo[tc % 2].h, XO[tc % 2] + 16 * (tc // 2))
                                last = None
                                for ot in range(2):
                                    v.tensor_tensor(St[:, 0:512], ps_s[:, ot, :],
                                                    gres[:, tc % 2, ot * 512:(ot + 1) * 512], OP.mult)
                                    v.drain()
                                    last = v.tensor_tensor(x1sb[:, tc % 2, ot * 512:(ot + 1) * 512],
                                                    St[:, 0:512], xres[:, tc % 2, ot * 512:(ot + 1) * 512],
                                                    OP.add)
                                    v.drain()
                                last.then_inc(s_rs.bump(), 1)
                    @blk.scalar
                    def _(sc):
                        for q in range(4):
                            for ec in range(8):
                                sc.wait_ge(s_rgp.h, RGP + 16 * q + 8 + ec + 1)
                                sc.activation(gq[:, ec, :], (pRGa, pRGb)[ec % 2][:, 0:256],
                                              AF.Silu).then_inc(s_ge.bump(), 1)
                    @blk.tensor
                    def _(t_):
                        prevb = [None]
                        def bmm(base, *a, **kw):
                            if prevb[0] is not None and prevb[0] != base:
                                t_.drain()
                            prevb[0] = base
                            return t_.matmul(*a, **kw)
                        for q in range(4):
                            t_.wait_ge(s_fold.h, FO + 1)
                            t_.wait_ge(s_xq2[q % 2].h, XQ[q % 2] + 16 * (q // 2 + 1))
                            if q >= 1 and not m3ch:
                                t_.wait_ge(s_drg.h, RG3 + DEBUG_NCH * q)
                            for w in range(2):
                                for ec in range(8):
                                    gi = 16 * q + w * 8 + ec
                                    if gi >= 2:
                                        ndone = gi - 1
                                        nfullq, rem = divmod(ndone, 16)
                                        rdone = nfullq * 8 + min(rem, 8)
                                        gdone = nfullq * 8 + max(rem - 8, 0)
                                        if rdone > 0:
                                            t_.wait_ge(s_rge.h, RGE + rdone)
                                        if gdone > 0:
                                            t_.wait_ge(s_ge.h, GE + gdone)
                                    prg = (pRGa, pRGb)[ec % 2]
                                    last = None
                                    for dc in range(8):
                                        last = bmm(0, prg[:, 0:256], wbuf[:, dc, w * D + ec * 128:w * D + (ec + 1) * 128],
                                                  xnq[:, q % 2, dc, :],
                                                  start=(dc == 0), stop=(dc == 7))
                                    last.then_inc(s_rgp.bump(), 1)
                            if m3rg:
                                continue
                            t_.wait_ge(s_kv2[q % 2].h, KV2[q % 2] + 32 * (q // 2 + 1))
                            for m in range(8 * q, 8 * q + DEBUG_NCH):
                                par = m % 2
                                t_.wait_ge(s_dkh.h, KH + m + 1)
                                t_.wait_ge(s_dam.h, AM + m)
                                first = True
                                last = None
                                for h in [hh for hh in range(H) if hh % 2 == 0] + [hh for hh in range(H) if hh % 2 == 1]:
                                    hb = 64 * (h % 2)
                                    last = bmm(hb, pA[32 * par:32 * par + 32, h * 32:(h + 1) * 32],
                                              ktb[hb:hb + 64, par, h // 2, :],
                                              khb[hb:hb + 64, par, h // 2, :],
                                              start=first, stop=(h == H - 1))
                                    first = False
                                last.then_inc(s_pA.bump(), 1)
                                if not c_wkv:
                                    t_.nop().then_inc(s_pw.bump(), 1)
                                    t_.nop().then_inc(s_pU.bump(), 1)
                                    continue
                                t_.wait_ge(s_dam.h, AM + m + 1)
                                t_.wait_ge(s_dsu.h, SU + m)
                                t_.wait_ge(s_drg.h, RG3 + m)
                                pwflat = bass.AP(pw, 0, [[512, 128], [1, 512]])
                                bmm("z", pwflat, g["ones1b"][:], g["zrow"][:],
                                    start=True, stop=False)
                                for h in range(H):
                                    bmm(32 * par, pw[64 * (h % 2):64 * (h % 2) + 64, h // 2, :],
                                        vtmq[32 * par:32 * par + 32, q % 2, m // 2 - 4 * q, h * 64:(h + 1) * 64],
                                        am[32 * par:32 * par + 32, h * 32:(h + 1) * 32],
                                        start=False, stop=False)
                                for h in [hh for hh in range(H) if hh % 2 == 0] + [hh for hh in range(H) if hh % 2 == 1]:
                                    hb = 64 * (h % 2)
                                    bmm(hb, pw[hb:hb + 64, h // 2, :],
                                        S2[hb:hb + 64, h * 64:(h + 1) * 64],
                                        khb[hb:hb + 64, par, h // 2, :],
                                        start=False, stop=False)
                                bmm("z", pwflat, g["ones1b"][:], g["zrow"][:],
                                    start=False, stop=True).then_inc(s_pw.bump(), 1)
                                if not c_u:
                                    t_.nop().then_inc(s_pU.bump(), 1)
                                    if m % 4 == 3 and not m3ch:
                                        pass
                                    continue
                                t_.wait_ge(s_dkb.h, KB4 + m + 1)
                                t_.wait_ge(s_dsu.h, SU + m)
                                for cc in (0, 512):
                                    bmm("z", pU[:, cc:cc + 512], g["ones1b"][:], g["zrow"][:],
                                        start=True, stop=False)
                                for half in range(2):
                                    for h in range(H):
                                        bmm(32 * par, pU[64 * half:64 * half + 64, h * 64:(h + 1) * 64],
                                            kb[32 * par:32 * par + 32, h * 64:(h + 1) * 64],
                                            vtmq[32 * par:32 * par + 32, q % 2, m // 2 - 4 * q, h * 64:(h + 1) * 64],
                                            start=False, stop=False)
                                last = None
                                for cc in (0, 512):
                                    last = bmm("z", pU[:, cc:cc + 512], g["ones1b"][:], g["zrow"][:],
                                               start=False, stop=True)
                                last.then_inc(s_pU.bump(), 1)
                                if m % 4 == 3 and not m3ch:
                                    tc = m // 4
                                    t_.wait_ge(s_drg.h, RG3 + m + 1)
                                    t_.wait_ge(s_rs.h, RS + tc)
                                    last = None
                                    for ot in range(2):
                                        for dc in range(8):
                                            last = bmm(0, ps_s[:, ot, :],
                                                      rp[:, dc, (tc % 2) * 128:(tc % 2) * 128 + 128],
                                                      wosb[:, dc, ot * 512:(ot + 1) * 512],
                                                      start=(dc == 0), stop=(dc == 7))
                                    last.then_inc(s_wo2.bump(), 1)


def emit_qkv(ctx):
    nc = ctx.nc
    xn_fm = ctx.dtmp("xn_fm", [8, 128, TOK], BF16)
    qf_d = ctx.dtmp("qf", [2, 128, 8, SEG], BF16)
    cc1s = ctx.dtmp("cc1src", [2, 2, 128, 8, SEG], BF16)
    cc1d = ctx.dtmp("cc1dst", [NCORE, 2, 2, 128, 8, SEG], BF16)
    wqkv = ctx.din("wqkv", [D, 3 * D])
    with contextlib.ExitStack() as _es:
        w3 = _es.enter_context(nc.sbuf_tensor("qk_w", [128, 8, 3 * D], BF16))
        xnq = _es.enter_context(nc.sbuf_tensor("qk_x", [128, 2, 8, 512], BF16))
        qsb = _es.enter_context(nc.sbuf_tensor("qk_q", [128, 8, SEG], BF16))
        ksb = _es.enter_context(nc.sbuf_tensor("qk_k", [128, 8, SEG], BF16))
        vsb = _es.enter_context(nc.sbuf_tensor("qk_v", [128, 8, SEG], BF16))
        pf = _es.enter_context(nc.psum_tensor("qk_pf", [128, 2, 512], F32))
        s_lw = ctx.sem("qk_lw")
        with nc.Block() as blk:
            LW = s_lw.n
            @blk.gpsimd
            def _(gp_):
                for dc in range(8):
                    gp_.dma_start(w3[:, dc, :], wqkv.ap()[dc * 128:(dc + 1) * 128, :]).then_inc(s_lw.bump(16), 16)
                gp_.wait_ge(s_lw.h, LW + 128)
        for s in range(2):
            s_xf = ctx.sem("qk_xf"); s_fp = ctx.sem("qk_fp"); s_fe = ctx.sem("qk_fe")
            s_sp = ctx.sem("qk_sp")
            with nc.Block() as blk:
                XF, FP, FE, SP = s_xf.n, s_fp.n, s_fe.n, s_sp.n
                @blk.sync
                def _(sy):
                    for t5 in range(2):
                        sy.dma_start(
                            xnq[:, t5, :, :],
                            xn_fm.ap()[:, :, s * SEG + t5 * 512:s * SEG + (t5 + 1) * 512]
                            .rearrange("c p j -> p c j")).then_inc(s_xf.bump(16), 16)
                    sy.wait_ge(s_fe.h, FE + 48)
                    sy.dma_start(qf_d.ap()[s], qsb[:]).then_inc(s_sp.bump(16), 16)
                    sy.dma_start(cc1s.ap()[s, 0], ksb[:]).then_inc(s_sp.bump(16), 16)
                    sy.dma_start(cc1s.ap()[s, 1], vsb[:]).then_inc(s_sp.bump(16), 16)
                    sy.wait_ge(s_sp.h, SP + 48)
                @blk.tensor
                def _(t_):
                    i = 0
                    for part in range(2):
                        for t5 in range(2):
                            t_.wait_ge(s_xf.h, XF + 32)
                            for ec in range(8):
                                if i >= 2:
                                    t_.wait_ge(s_fe.h, FE + i - 1)
                                last = None
                                for dc in range(8):
                                    last = t_.matmul(pf[:, i % 2, :],
                                              w3[:, dc, part * D + ec * 128:part * D + (ec + 1) * 128],
                                              xnq[:, t5, dc, :], start=(dc == 0), stop=(dc == 7))
                                last.then_inc(s_fp.bump(), 1)
                                i += 1
                    t_.wait_ge(s_xf.h, XF + 32)
                    for tc in range(8):
                        for vh in range(2):
                            if i >= 2:
                                t_.wait_ge(s_fe.h, FE + i - 1)
                            last = None
                            for dc in range(8):
                                last = t_.matmul(pf[:, i % 2, :],
                                          xnq[:, tc // 4, dc, (tc % 4) * 128:(tc % 4 + 1) * 128],
                                          w3[:, dc, 2 * D + vh * 512:2 * D + (vh + 1) * 512],
                                          start=(dc == 0), stop=(dc == 7))
                            last.then_inc(s_fp.bump(), 1)
                            i += 1
                @blk.vector
                def _(v):
                    i = 0
                    for part in range(2):
                        dstb = (qsb, ksb)[part]
                        for t5 in range(2):
                            for ec in range(8):
                                v.wait_ge(s_fp.h, FP + i + 1)
                                v.tensor_copy(dstb[:, ec, t5 * 512:(t5 + 1) * 512],
                                              pf[:, i % 2, :]).then_inc(s_fe.bump(), 1)
                                i += 1
                    for tc in range(8):
                        for vh in range(2):
                            v.wait_ge(s_fp.h, FP + i + 1)
                            v.tensor_copy(vsb[:, tc, vh * 512:(vh + 1) * 512],
                                          pf[:, i % 2, :]).then_inc(s_fe.bump(), 1)
                            i += 1
        if SIM_SKIP_CC:
            ctx.sem("cc1").bump()
            return
        s_cc1 = ctx.sem("cc1")
        with nc.Block() as blk:
            @blk.gpsimd
            def _(gp_):
                gp_.collective_compute(
                    "AllGather", OP.bypass, replica_groups=[list(range(NCORE))],
                    ins=[ctx.dram["cc1src"].ap().opt()],
                    outs=[ctx.dram["cc1dst"].ap().opt()]).then_inc(s_cc1.bump(), 1)



def emit_attn(ctx):
    nc = ctx.nc
    g = ctx.glob
    qf_d = ctx.dtmp("qf", [2, 128, 8, SEG], BF16)
    cc1d = ctx.dtmp("cc1dst", [NCORE, 2, 2, 128, 8, SEG], BF16)
    g1_d = ctx.dtmp("g1", [128, 16, D], BF16)
    x1_d = ctx.dtmp("x1", [TOK, D], F32)
    out_d = ctx.dout("out", [TOK, D], F32)
    idb = g["idb"]
    with contextlib.ExitStack() as _es:
        qseg = _es.enter_context(nc.sbuf_tensor("at_q", [128, 8, SEG], BF16))
        kbuf = _es.enter_context(nc.sbuf_tensor("at_k", [128, 2, 8, 512], BF16))
        vbuf = _es.enter_context(nc.sbuf_tensor("at_v", [128, 2, 4, SEG], BF16))
        Oall = _es.enter_context(nc.sbuf_tensor("at_o", [128, 8, SEG], F32))
        Mall = _es.enter_context(nc.sbuf_tensor("at_m", [128, 8], F32))
        Dall = _es.enter_context(nc.sbuf_tensor("at_d", [128, 8], F32))
        mnew = _es.enter_context(nc.sbuf_tensor("at_mn", [128, 8], F32))
        nmn = _es.enter_context(nc.sbuf_tensor("at_nn", [128, 8], F32))
        resc = _es.enter_context(nc.sbuf_tensor("at_rc", [128, 8], F32))
        mxc = _es.enter_context(nc.sbuf_tensor("at_mx", [128, 2], F32))
        psc_ = _es.enter_context(nc.sbuf_tensor("at_ps", [128, 2], F32))
        P = _es.enter_context(nc.sbuf_tensor("at_P", [128, 2, 512], BF16))
        PT = _es.enter_context(nc.sbuf_tensor("at_PT", [128, 4, 128], BF16))
        osb = _es.enter_context(nc.sbuf_tensor("at_os", [128, 2, D], F32))
        x1t = _es.enter_context(nc.sbuf_tensor("at_x1", [128, 2, D], F32))
        g2t = _es.enter_context(nc.sbuf_tensor("at_g2", [128, 2, D], F32))
        scr = _es.enter_context(nc.sbuf_tensor("at_sc", [128, D], F32))
        psc = _es.enter_context(nc.psum_tensor("at_psc", [128, 3, 512], F32))
        ppt = _es.enter_context(nc.psum_tensor("at_ppt", [128, 8, 128], BF16))
        pav = _es.enter_context(nc.psum_tensor("at_pav", [128, 2, 512], F32))
        for s in range(2):
            s_kv = [ctx.sem("at_kvA"), ctx.sem("at_kvB")]
            s_q = ctx.sem("at_q"); s_sc = ctx.sem("at_sc")
            s_mx = ctx.sem("at_mx"); s_ax = ctx.sem("at_ax"); s_msk = ctx.sem("at_mk")
            s_pt = ctx.sem("at_pt"); s_pte = ctx.sem("at_pe"); s_av = ctx.sem("at_av")
            s_ou = ctx.sem("at_ou"); s_rs = ctx.sem("at_rs")
            s_xo = [ctx.sem("at_xoA"), ctx.sem("at_xoB")]
            s_x1 = [ctx.sem("at_x1A"), ctx.sem("at_x1B")]
            s_gt = [ctx.sem("at_gtA"), ctx.sem("at_gtB")]
            b = s
            N = 128
            with nc.Block() as blk:
                Q0, SC, MX, AX0, MSK = s_q.n, s_sc.n, s_mx.n, s_ax.n, s_msk.n
                PT0, PTE, AV, OU, RS = s_pt.n, s_pte.n, s_av.n, s_ou.n, s_rs.n
                KV = [x.n for x in s_kv]
                XO = [x.n for x in s_xo]
                X1 = [x.n for x in s_x1]
                GT = [x.n for x in s_gt]
                @blk.sync
                def _(sy):
                    if not SIM_SKIP_CC:
                        sy.wait_ge(ctx.sem("cc1").h, 1)
                    sy.dma_start(qseg[:], qf_d.ap()[s]).then_inc(s_q.bump(16), 16)
                    for t in range(16):
                        qq = 8 * b + t // 2
                        rk = min(qq, 15 - qq)
                        sp = 0 if rk == qq else 1
                        half = t % 2
                        if t >= 2:
                            sy.wait_ge(s_sc.h, SC + 8 * (t - 1))
                            sy.wait_ge(s_av.h, AV + 8 * (t - 1))
                        sy.dma_start(kbuf[:, t % 2, :, :],
                                     cc1d.ap()[rk, sp, 0, :, :, half * 512:(half + 1) * 512]
                                     ).then_inc(s_kv[t % 2].bump(16), 16)
                        sy.dma_start(vbuf[:, t % 2, :, :],
                                     cc1d.ap()[rk, sp, 1, :, 4 * half:4 * half + 4, :]
                                     ).then_inc(s_kv[t % 2].bump(16), 16)
                    for k in range(8):
                        if k >= 2:
                            sy.wait_ge(s_rs.h, RS + k - 1)
                        sy.dma_start(x1t[:, k % 2, :],
                                     x1_d.ap()[s * SEG + k * 128:s * SEG + (k + 1) * 128, :]
                                     ).then_inc(s_x1[k % 2].bump(16), 16)
                        sy.wait_ge(s_rs.h, RS + k + 1)
                        sy.dma_start(out_d.ap()[s * SEG + k * 128:s * SEG + (k + 1) * 128, :],
                                     osb[:, k % 2, :]).then_inc(s_xo[k % 2].bump(16), 16)
                    sy.wait_ge(s_xo[0].h, XO[0] + 64)
                    sy.wait_ge(s_xo[1].h, XO[1] + 64)
                @blk.gpsimd
                def _(gp_):
                    for k in range(8):
                        if k >= 2:
                            gp_.wait_ge(s_rs.h, RS + k - 1)
                        gp_.dma_start(g2t[:, k % 2, :], g1_d.ap()[:, s * 8 + k, :]).then_inc(s_gt[k % 2].bump(16), 16)
                @blk.tensor
                def _(t_):
                    def scores(i):
                        t, k = divmod(i, 8)
                        t_.wait_ge(s_kv[t % 2].h, KV[t % 2] + 32 * (t // 2 + 1))
                        if i == 0:
                            t_.wait_ge(s_q.h, Q0 + 16)
                        if i >= 3:
                            t_.wait_ge(s_ax.h, AX0 + i - 2)
                        last = None
                        for dc in range(8):
                            last = t_.matmul(psc[:, i % 3, :], qseg[:, dc, k * 128:(k + 1) * 128],
                                      kbuf[:, (i // 8) % 2, dc, :],
                                      start=(dc == 0), stop=(dc == 7))
                        last.then_inc(s_sc.bump(), 1)
                    scores(0)
                    scores(1)
                    for i in range(N):
                        if i + 2 < N:
                            scores(i + 2)
                        t_.wait_ge(s_msk.h, MSK + i + 1)
                        if i >= 1:
                            t_.wait_ge(s_pte.h, PTE + i)
                        last = None
                        for jb in range(4):
                            last = t_.transpose(ppt[:, jb, :], P[:, i % 2, jb * 128:(jb + 1) * 128], idb[:])
                        last.then_inc(s_pt.bump(), 1)
                        t_.wait_ge(s_pte.h, PTE + i + 1)
                        if i >= 1:
                            t_.wait_ge(s_ou.h, OU + i)
                        last = None
                        for ot in range(2):
                            for jb in range(4):
                                last = t_.matmul(pav[:, ot, :], PT[:, jb, :],
                                          vbuf[:, (i // 8) % 2, jb, ot * 512:(ot + 1) * 512],
                                          start=(jb == 0), stop=(jb == 3))
                        last.then_inc(s_av.bump(), 1)
                @blk.vector
                def _(v):
                    for k in range(8):
                        v.memset(Oall[:, k, :], 0.0)
                    v.memset(Mall[:], NEGBIG)
                    v.memset(Dall[:], 0.0)
                    def mxblock(i):
                        k = i % 8
                        v.wait_ge(s_sc.h, SC + i + 1)
                        if i >= 8:
                            v.wait_ge(s_ax.h, AX0 + i - 7)
                        v.reduce_max(mxc[:, i % 2:i % 2 + 1], psc[:, i % 3, :], axis=AX)
                        v.drain()
                        v.tensor_scalar(mxc[:, i % 2:i % 2 + 1], mxc[:, i % 2:i % 2 + 1],
                                        1.0 / 32.0, None, OP.mult)
                        v.drain()
                        v.tensor_tensor(mnew[:, k:k + 1], Mall[:, k:k + 1],
                                        mxc[:, i % 2:i % 2 + 1], OP.max)
                        v.drain()
                        v.tensor_scalar(nmn[:, k:k + 1], mnew[:, k:k + 1], -1.0,
                                        None, OP.mult).then_inc(s_mx.bump(), 1)
                    mxblock(0)
                    for i in range(N):
                        t, k = divmod(i, 8)
                        if i + 1 < N:
                            mxblock(i + 1)
                        v.wait_ge(s_ax.h, AX0 + i + 1)
                        v.scalar_tensor_tensor(P[:, i % 2, :], g["iota_pj"][:],
                                               g["thresh"][:, s * 128 + k * 16 + t:s * 128 + k * 16 + t + 1],
                                               P[:, i % 2, :], OP.is_ge, OP.mult)
                        v.drain()
                        v.reduce_sum(psc_[:, i % 2:i % 2 + 1], P[:, i % 2, :], axis=AX)
                        v.drain()
                        v.scalar_tensor_tensor(Dall[:, k:k + 1], Dall[:, k:k + 1],
                                               resc[:, k:k + 1], psc_[:, i % 2:i % 2 + 1],
                                               OP.mult, OP.add).then_inc(s_msk.bump(), 1)
                        v.wait_ge(s_av.h, AV + i + 1)
                        for ot in range(2):
                            v.scalar_tensor_tensor(Oall[:, k, ot * 512:(ot + 1) * 512],
                                                   Oall[:, k, ot * 512:(ot + 1) * 512],
                                                   resc[:, k:k + 1], pav[:, ot, :],
                                                   OP.mult, OP.add)
                        v.drain()
                        v.tensor_copy(Mall[:, k:k + 1], mnew[:, k:k + 1]).then_inc(s_ou.bump(), 1)
                    for k in range(8):
                        v.wait_ge(s_gt[k % 2].h, GT[k % 2] + 16 * (k // 2 + 1))
                        v.wait_ge(s_x1[k % 2].h, X1[k % 2] + 16 * (k // 2 + 1))
                        if k >= 2:
                            v.wait_ge(s_xo[k % 2].h, XO[k % 2] + 16 * (k // 2))
                        v.reciprocal(mxc[:, 0:1], Dall[:, k:k + 1])
                        v.drain()
                        v.tensor_scalar(scr[:], Oall[:, k, :], mxc[:, 0:1], None, OP.mult)
                        v.drain()
                        v.tensor_tensor(scr[:], scr[:], g2t[:, k % 2, :], OP.mult)
                        v.drain()
                        v.tensor_tensor(osb[:, k % 2, :], scr[:], x1t[:, k % 2, :],
                                        OP.add).then_inc(s_rs.bump(), 1)
                @blk.scalar
                def _(sc):
                    for i in range(N):
                        t, k = divmod(i, 8)
                        sc.wait_ge(s_mx.h, MX + i + 1)
                        sc.activation(resc[:, k:k + 1], Mall[:, k:k + 1], AF.Exp,
                                      bias=nmn[:, k:k + 1])
                        if i >= 2:
                            sc.wait_ge(s_pt.h, PT0 + i - 1)
                        sc.activation(P[:, i % 2, :], psc[:, i % 3, :], AF.Exp, scale=1.0 / 32.0,
                                      bias=nmn[:, k:k + 1]).then_inc(s_ax.bump(), 1)
                        if i >= 1:
                            sc.wait_ge(s_pt.h, PT0 + i)
                            if i >= 2:
                                sc.wait_ge(s_av.h, AV + i - 1)
                            sc.copy(PT[:], ppt[:, 0:4, :]).then_inc(s_pte.bump(), 1)
                    sc.wait_ge(s_pt.h, PT0 + N)
                    sc.wait_ge(s_av.h, AV + N - 1)
                    sc.copy(PT[:], ppt[:, 0:4, :]).then_inc(s_pte.bump(), 1)



_BUILD_CACHE = {}


def build_program(upto="attn", taps=()):
    key = (upto, tuple(sorted(taps)))
    if key in _BUILD_CACHE:
        return _BUILD_CACHE[key]
    nc = bass.Bass("TRN2", target_bir_lowering=False)
    es = contextlib.ExitStack()
    ctx = Ctx(nc, es, taps)
    with es:
        x_in = ctx.din("x", [TOK, D])
        emit_setup(ctx)
        phases = ["ln1", "mixa", "moe0", "mixb", "ln2", "qkv", "moe1", "attn"]
        stop = phases.index({"mixp": "mixa", "mixn": "mixa", "mixb1": "mixb", "mixb2": "mixb", "mixb3": "mixb", "mixb4": "mixb", "mixb5": "mixb", "mixb6": "mixb"}.get(upto, upto))
        xn_fm = ctx.dtmp("xn_fm", [8, 128, TOK], BF16)
        emit_ln(ctx, x_in, xn_fm)
        if stop >= 1:
            emit_mixer_a(ctx, parts={"mixp": "prod", "mixn": "nocc"}.get(upto, "all"))
        if stop >= 2:
            emit_moe(ctx, 0)
        if stop >= 3:
            emit_mixer_b(ctx, parts={"mixb1": "m1", "mixb2": "m2", "mixb3": "m3rg", "mixb4": "m3ch", "mixb5": "m3c1", "mixb6": "m3c2"}.get(upto, "all"))
        if stop >= 4:
            emit_ln(ctx, ctx.dtmp("x1", [TOK, D], F32), xn_fm)
        if stop >= 5:
            emit_qkv(ctx)
        if stop >= 6:
            emit_moe(ctx, 1)
        if stop >= 7:
            emit_attn(ctx)
        if 1 <= stop < 3 and upto not in ("mixp", "mixn") and not SIM_SKIP_CC:
            with nc.Block() as blk:
                @blk.gpsimd
                def _(gp_):
                    gp_.wait_ge(ctx.sem("cc0").h, 1)
        if 5 <= stop < 7 and not SIM_SKIP_CC:
            with nc.Block() as blk:
                @blk.gpsimd
                def _(gp_):
                    gp_.wait_ge(ctx.sem("cc1").h, 1)
    innames = [k for k, v in ctx.kinds.items() if v == "in"]
    _BUILD_CACHE[key] = (nc, innames)
    return nc, innames


def kernel(**inputs):
    nc, innames = build_program("attn")
    in_maps = prep_in_maps(inputs)
    in_maps = [{k: m[k] for k in innames} for m in in_maps]
    res = run_bass_kernel_spmd(nc, in_maps, core_ids=list(range(NCORE)))
    return scatter_out(res.results)


# version 04
# speedup vs baseline: 1.0744x; 1.0547x over previous
import contextlib
import numpy as np
import concourse.bass as bass
import concourse.mybir as mybir
from concourse.bass_utils import run_bass_kernel_spmd

F32 = mybir.dt.float32
BF16 = mybir.dt.bfloat16
AF = mybir.ActivationFunctionType
OP = mybir.AluOpType
AX = mybir.AxisListType.X

D = 1024
TOK = 2048
SEG = 1024
H, HD = 16, 64
CH = 32
NCHUNK = SEG // CH
NCORE = 8
NEGBIG = -1.0e30
SIM_SKIP_CC = False
DEBUG_NCH = 8


def _prep_consts():
    c = {}
    c["iden"] = np.eye(128, dtype=np.float32)
    m = np.triu(np.ones((32, 32), np.float32))
    c["mask64"] = np.ascontiguousarray(np.tile(np.tile(m, (1, 16)), (2, 1)))
    c["ivec"] = np.ascontiguousarray(
        np.tile(np.arange(1, CH + 1, dtype=np.float32)[None, :], (128, 1)))
    c["revseg"] = (SEG - 1.0 - np.arange(SEG, dtype=np.float32))[None, :]
    c["w31col2"] = (31.0 - (np.arange(64) % 32)).astype(np.float32)[None, :]
    c["hsel"] = np.ascontiguousarray(np.repeat(np.eye(16, dtype=np.float32), HD, axis=1))
    c["iota_pj"] = np.ascontiguousarray(
        np.arange(128, dtype=np.float32)[:, None] - np.arange(1024, dtype=np.float32)[None, :])
    c["ones_1x"] = np.ones((1, 128), np.float32)
    return c


def _prep_core_consts(core):
    qA, qB = core, 15 - core
    segs = [(qA // 8, qA % 8), (qB // 8, qB % 8)]
    d = {}
    expo = np.full((2, 16), 1.0e6, np.float32)
    for s, (b, kpos) in enumerate(segs):
        for rank in range(8):
            for sp in range(2):
                q = rank if sp == 0 else 15 - rank
                bp, kp = q // 8, q % 8
                if bp == b and kp < kpos:
                    expo[s, 2 * rank + sp] = SEG * (kpos - 1 - kp)
    d["sin_nexpo"] = np.ascontiguousarray(np.tile(-expo.reshape(1, 32), (128, 1)))
    th = np.zeros((2, 8, 8), np.float32)
    for s, (b, kpos) in enumerate(segs):
        for k in range(8):
            g0 = kpos * SEG + k * 128
            for t in range(8):
                th[s, k, t] = t * 1024 - g0
    d["attn_thresh"] = np.ascontiguousarray(np.tile(th.reshape(1, 128), (128, 1)))
    return d


def _prep_shared(inputs):
    g = lambda k: np.asarray(inputs[k], dtype=np.float32)
    s = {}
    s["wk"] = np.ascontiguousarray(g("L0_attn_Wk").T)
    s["wv"] = np.ascontiguousarray(g("L0_attn_Wv").T)
    s["wr"] = np.ascontiguousarray(g("L0_attn_Wr").T)
    s["wgm"] = np.ascontiguousarray(g("L0_attn_Wg").T)
    s["wo"] = np.ascontiguousarray(g("L0_attn_Wo").T)
    tm = g("L0_attn_time_mix").reshape(D)
    s["tm_col"] = np.ascontiguousarray(tm.reshape(8, 128).T)
    td = g("L0_attn_time_decay")
    s["td_col"] = np.ascontiguousarray(td.reshape(D).reshape(8, 128).T)
    s["td_row"] = np.ascontiguousarray(td.reshape(1, D))
    s["td_nat"] = np.ascontiguousarray(td)
    tf = g("L0_attn_time_first")
    s["tf_col"] = np.ascontiguousarray(tf.reshape(D).reshape(8, 128).T)
    s["m0_w1t"] = np.ascontiguousarray(g("L0_ffn_W1").transpose(0, 2, 1))
    s["m0_w2t"] = np.ascontiguousarray(g("L0_ffn_W2").transpose(0, 2, 1))
    s["m0_wg"] = np.ascontiguousarray(g("L0_ffn_Wg").T)
    s["m0_bg"] = np.ascontiguousarray(g("L0_ffn_bg").reshape(1, 8))
    s["m0_b1"] = np.ascontiguousarray(g("L0_ffn_b1").reshape(8, 16, 128).transpose(0, 2, 1))
    s["m0_b2"] = np.ascontiguousarray(g("L0_ffn_b2"))
    s["wqkv"] = np.ascontiguousarray(g("L1_attn_W_qkv"))
    s["rw"] = np.ascontiguousarray(g("L1_ffn_router_w"))
    s["m1_w1"] = np.ascontiguousarray(g("L1_ffn_w1"))
    s["m1_w2"] = np.ascontiguousarray(g("L1_ffn_w2"))
    s["m1_b1"] = np.ascontiguousarray(g("L1_ffn_b1").reshape(16, 16, 128).transpose(0, 2, 1))
    s["m1_b2"] = np.ascontiguousarray(g("L1_ffn_b2"))
    return s


def prep_in_maps(inputs):
    x = np.asarray(inputs["L0_attn_x"], dtype=np.float32)
    shared = _prep_shared(inputs)
    consts = _prep_consts()
    in_maps = []
    for c in range(NCORE):
        qA, qB = c, 15 - c
        shard = np.concatenate(
            [x[qA // 8, (qA % 8) * SEG:(qA % 8 + 1) * SEG],
             x[qB // 8, (qB % 8) * SEG:(qB % 8 + 1) * SEG]], axis=0)
        m = {"x": np.ascontiguousarray(shard)}
        m.update(shared)
        m.update(consts)
        m.update(_prep_core_consts(c))
        in_maps.append(m)
    return in_maps


def scatter_out(results):
    out = np.empty((2, 8192, D), dtype=np.float32)
    for c in range(NCORE):
        y = np.asarray(results[c]["out"])
        qA, qB = c, 15 - c
        out[qA // 8, (qA % 8) * SEG:(qA % 8 + 1) * SEG] = y[:SEG]
        out[qB // 8, (qB % 8) * SEG:(qB % 8 + 1) * SEG] = y[SEG:]
    return out



class SemCt:

    def __init__(self, es, nc, name):
        self.h = es.enter_context(nc.semaphore(name))
        self.n = 0

    def bump(self, k=1):
        self.n += k
        return self.h


class Ctx:
    def __init__(self, nc, es, taps=()):
        self.nc = nc
        self.es = es
        self.dram = {}
        self.kinds = {}
        self.taps = set(taps)
        self.uid = 0
        self.glob = {}
        self.sems = {}

    def name(self, stem):
        self.uid += 1
        return f"{stem}{self.uid}"

    def sem(self, key):
        if key not in self.sems:
            self.sems[key] = SemCt(self.es, self.nc, "s_" + key)
        return self.sems[key]

    def din(self, key, shape, dtype=F32):
        if key not in self.dram:
            self.dram[key] = self.nc.dram_tensor(key, shape, dtype, kind="ExternalInput")
            self.kinds[key] = "in"
        return self.dram[key]

    def dout(self, key, shape, dtype=F32):
        if key not in self.dram:
            self.dram[key] = self.nc.dram_tensor(key, shape, dtype, kind="ExternalOutput")
            self.kinds[key] = "out"
        return self.dram[key]

    def dtmp(self, key, shape, dtype):
        if key not in self.dram:
            if key in self.taps:
                self.dram[key] = self.nc.dram_tensor(key, shape, dtype, kind="ExternalOutput")
                self.kinds[key] = "out"
            else:
                self.dram[key] = self.nc.dram_tensor(key, shape, dtype)
                self.kinds[key] = "tmp"
        return self.dram[key]



def emit_setup(ctx):
    nc = ctx.nc
    g = ctx.glob
    mk = lambda n, sh, dt: ctx.es.enter_context(nc.sbuf_tensor(n, sh, dt))
    g["idb"] = mk("g_idb", [128, 128], BF16)
    g["td_col"] = mk("g_tdcol", [128, 8], F32)
    g["tm_col"] = mk("g_tmcol", [128, 8], F32)
    g["tf_col"] = mk("g_tfcol", [128, 8], F32)
    g["td_nat"] = mk("g_tdnat", [16, 64], F32)
    g["dhat"] = mk("g_dhat", [128, 8, CH], BF16)
    g["dneg"] = mk("g_dneg", [128, 8, CH], BF16)
    g["tfmap"] = mk("g_tfmap", [128, 8, CH], F32)
    g["dc32"] = mk("g_dc32", [128, D], F32)
    g["dbar32"] = mk("g_dbar32", [64, D], BF16)
    g["mask64"] = mk("g_mask", [64, 512], BF16)
    g["iota_pj"] = mk("g_iota", [128, 1024], F32)
    g["thresh"] = mk("g_thr", [128, 128], F32)
    g["nexpo"] = mk("g_nexpo", [128, 32], F32)
    g["hsel"] = mk("g_hsel", [16, D], F32)
    g["ones1b"] = mk("g_ones1b", [1, 128], BF16)
    g["zrow"] = mk("g_zrow", [1, 512], BF16)

    with contextlib.ExitStack() as _es:
        idf = _es.enter_context(nc.sbuf_tensor("su_idf", [128, 128], F32))
        ivec = _es.enter_context(nc.sbuf_tensor("su_ivec", [128, CH], F32))
        tmp = _es.enter_context(nc.sbuf_tensor("su_t", [128, D], F32))
        mskf = _es.enter_context(nc.sbuf_tensor("su_m", [64, 512], F32))
        w31 = _es.enter_context(nc.sbuf_tensor("su_w31", [1, 64], F32))
        tdrow_l = _es.enter_context(nc.sbuf_tensor("su_tdrow", [1, D], F32))
        ones1 = _es.enter_context(nc.sbuf_tensor("su_ones", [1, 128], F32))
        ps = _es.enter_context(nc.psum_tensor("su_ps", [128, D], F32))
        pb = _es.enter_context(nc.psum_tensor("su_pb", [128, D], F32))
        s_ld = _es.enter_context(nc.semaphore("su_ld"))
        w31_host = ctx.din("w31col2", [1, 64])
        with nc.Block() as blk:
            @blk.sync
            def _(sy):
                n = [0]
                def ld(dst, key, shape):
                    sy.dma_start(dst, ctx.din(key, shape).ap()).then_inc(s_ld, 16)
                    n[0] += 16
                ld(idf[:], "iden", [128, 128])
                ld(g["td_col"][:], "td_col", [128, 8])
                ld(g["tm_col"][:], "tm_col", [128, 8])
                ld(g["tf_col"][:], "tf_col", [128, 8])
                ld(g["td_nat"][:], "td_nat", [16, 64])
                ld(tdrow_l[:], "td_row", [1, D])
                ld(g["iota_pj"][:], "iota_pj", [128, 1024])
                ld(g["thresh"][:], "attn_thresh", [128, 128])
                ld(g["nexpo"][:], "sin_nexpo", [128, 32])
                ld(g["hsel"][:], "hsel", [16, D])
                ld(ivec[:], "ivec", [128, CH])
                ld(mskf[:], "mask64", [64, 512])
                ld(w31[:], "w31col2", [1, 64])
                ld(ones1[:], "ones_1x", [1, 128])
                sy.wait_ge(s_ld, n[0])
        with nc.Block() as blk:
            @blk.vector
            def _(v):
                v.memset(g["zrow"][:], 0.0)
                v.tensor_copy(g["idb"][:], idf[:])
                v.tensor_copy(g["ones1b"][:], ones1[:])
                v.tensor_copy(g["mask64"][:], mskf[:])
                for c in range(8):
                    v.tensor_scalar(tmp[:, c * CH:(c + 1) * CH], ivec[:],
                                    g["td_col"][:, c:c + 1], None, OP.mult)
                    v.tensor_scalar(g["tfmap"][:, c, :], ivec[:],
                                    0.0, g["tf_col"][:, c:c + 1], OP.mult, OP.add)
        with nc.Block() as blk:
            @blk.scalar
            def _(sc):
                for c in range(8):
                    sc.activation(g["dhat"][:, c, :], tmp[:, c * CH:(c + 1) * CH],
                                  AF.Exp, scale=-1.0)
                    sc.activation(g["dneg"][:, c, :], tmp[:, c * CH:(c + 1) * CH],
                                  AF.Exp, scale=1.0)
                sc.drain()
                sc.activation(tmp[0:16, 0:64], g["td_nat"][:], AF.Exp, scale=-32.0)
        with nc.Block() as blk:
            @blk.tensor
            def _(t):
                for hh in (0, 64):
                    for cc in (0, 512):
                        t.matmul(ps[hh:hh + 64, cc:cc + 512], tmp[0:16, 0:64],
                                 g["hsel"][:, cc:cc + 512], start=True, stop=True)
                for cc in (0, 512):
                    t.matmul(pb[0:64, cc:cc + 512], w31[:], tdrow_l[:, cc:cc + 512],
                             start=True, stop=True)
        with nc.Block() as blk:
            @blk.vector
            def _(v):
                v.tensor_copy(g["dc32"][:], ps[:])
            @blk.scalar
            def _(sc):
                sc.activation(g["dbar32"][:], pb[0:64, :], AF.Exp, scale=-1.0)



def emit_ln(ctx, src_hdl, dst_hdl):
    nc = ctx.nc
    idb = ctx.glob["idb"]
    with contextlib.ExitStack() as _es:
        xs = _es.enter_context(nc.sbuf_tensor(ctx.name("lx"), [128, 16, D], F32))
        sq = _es.enter_context(nc.sbuf_tensor(ctx.name("lq"), [128, D], F32))
        s1 = _es.enter_context(nc.sbuf_tensor(ctx.name("ls1"), [128, 16], F32))
        s2 = _es.enter_context(nc.sbuf_tensor(ctx.name("ls2"), [128, 16], F32))
        nmu = _es.enter_context(nc.sbuf_tensor(ctx.name("lmu"), [128, 16], F32))
        var = _es.enter_context(nc.sbuf_tensor(ctx.name("lva"), [128, 16], F32))
        lnv = _es.enter_context(nc.sbuf_tensor(ctx.name("llv"), [128, 16], F32))
        epsb = _es.enter_context(nc.sbuf_tensor(ctx.name("lep"), [128, 1], F32))
        rstd = _es.enter_context(nc.sbuf_tensor(ctx.name("lrs"), [128, 16], F32))
        xn = _es.enter_context(nc.sbuf_tensor(ctx.name("lxn"), [128, D], BF16))
        xnT = _es.enter_context(nc.sbuf_tensor(ctx.name("lxT"), [128, 2, 8, 128], BF16))
        psT = _es.enter_context(nc.psum_tensor(ctx.name("lp"), [128, 8, 128], BF16))
        s_ld = ctx.sem("ln_ld"); s_v = ctx.sem("ln_v")
        s_pe = ctx.sem("ln_pe"); s_ev = ctx.sem("ln_ev")
        s_o2 = [ctx.sem("ln_oA"), ctx.sem("ln_oB")]
        L, V, P, E = s_ld.n, s_v.n, s_pe.n, s_ev.n
        O2 = [x.n for x in s_o2]
        src = src_hdl.ap()
        dst = dst_hdl.ap()
        with nc.Block() as blk:
            @blk.sync
            def _(sy):
                for t in range(16):
                    sy.dma_start(xs[:, t, :], src[t * 128:(t + 1) * 128, :]).then_inc(s_ld.bump(16), 16)
            @blk.vector
            def _(v):
                v.memset(epsb[:], 1e-5)
                v.wait_ge(s_ld.h, L + 256)
                for t in range(16):
                    v.tensor_tensor(sq[:], xs[:, t, :], xs[:, t, :], OP.mult)
                    v.drain()
                    v.reduce_sum(s1[:, t:t + 1], xs[:, t, :], axis=AX)
                    v.reduce_sum(s2[:, t:t + 1], sq[:], axis=AX)
                    v.drain()
                v.tensor_scalar(nmu[:], s1[:], -1.0 / D, None, OP.mult)
                v.tensor_scalar(var[:], s2[:], 1.0 / D, None, OP.mult)
                v.drain()
                v.tensor_tensor(s2[:], nmu[:], nmu[:], OP.mult)
                v.drain()
                v.tensor_tensor(var[:], var[:], s2[:], OP.subtract).then_inc(s_v.bump(), 1)
            @blk.scalar
            def _(sc):
                sc.wait_ge(s_v.h, V + 1)
                sc.activation(lnv[:], var[:], AF.Ln, bias=epsb[:])
                sc.drain()
                sc.activation(rstd[:], lnv[:], AF.Exp, scale=-0.5).then_inc(s_v.bump(), 1)
        with nc.Block() as blk:
            @blk.vector
            def _(v):
                for t in range(16):
                    v.tensor_scalar(xn[:], xs[:, t, :], nmu[:, t:t + 1],
                                    rstd[:, t:t + 1], OP.add, OP.mult).then_inc(s_v.bump(), 1)
                    v.wait_ge(s_pe.h, P + 8 * (t + 1))
                    if t >= 2:
                        v.wait_ge(s_o2[t % 2].h, O2[t % 2] + 16 * (t // 2))
                    v.tensor_copy(xnT[:, t % 2, :, :], psT[:]).then_inc(s_ev.bump(), 1)
            @blk.tensor
            def _(t_):
                for t in range(16):
                    t_.wait_ge(s_v.h, V + 2 + t + 1)
                    if t > 0:
                        t_.wait_ge(s_ev.h, E + t)
                    for cc in range(8):
                        t_.transpose(psT[:, cc, :], xn[:, cc * 128:(cc + 1) * 128],
                                     idb[:]).then_inc(s_pe.bump(), 1)
            @blk.sync
            def _(sy):
                for t in range(16):
                    sy.wait_ge(s_ev.h, E + t + 1)
                    sy.dma_start(dst[:, :, t * 128:(t + 1) * 128].rearrange("c p j -> p c j"),
                                 xnT[:, t % 2, :, :]).then_inc(s_o2[t % 2].bump(16), 16)
                sy.wait_ge(s_o2[0].h, O2[0] + 128)
                sy.wait_ge(s_o2[1].h, O2[1] + 128)



def emit_mixer_a(ctx, parts="all"):
    nc = ctx.nc
    g = ctx.glob
    xn_fm = ctx.dtmp("xn_fm", [8, 128, TOK], BF16)
    ktm_d = ctx.dtmp("ktm", [2, 64, 16, D], BF16)
    vtm_d = ctx.dtmp("vtm", [2, 64, 16, D], BF16)
    cc0s = ctx.dtmp("cc0src", [128, D], F32)
    cc0d = ctx.dtmp("cc0dst", [NCORE * 128, D], F32)
    with contextlib.ExitStack() as _es:
        wsb = _es.enter_context(nc.sbuf_tensor("ma_w", [128, 8, 2048], BF16))
        ktm = _es.enter_context(nc.sbuf_tensor("ma_ktm", [64, 16, D], BF16))
        vtm = _es.enter_context(nc.sbuf_tensor("ma_vtm", [64, 16, D], BF16))
        xnt = _es.enter_context(nc.sbuf_tensor("ma_xn", [128, 2, 8, 128], BF16))
        dbu = _es.enter_context(nc.sbuf_tensor("ma_dbu", [64, D], BF16))
        tdrow_a = _es.enter_context(nc.sbuf_tensor("ma_tdrow", [1, D], F32))
        rev_a = _es.enter_context(nc.sbuf_tensor("ma_rev", [1, SEG], F32))
        kbu = _es.enter_context(nc.sbuf_tensor("ma_kbu", [64, D], BF16))
        usb = _es.enter_context(nc.sbuf_tensor("ma_u", [64, D], F32))
        pj0 = _es.enter_context(nc.psum_tensor("ma_p0", [128, D], F32))
        pj1 = _es.enter_context(nc.psum_tensor("ma_p1", [128, D], F32))
        pdb = _es.enter_context(nc.psum_tensor("ma_pd", [128, D], F32))
        pu = _es.enter_context(nc.psum_tensor("ma_pu", [128, D], F32))
        wk = ctx.din("wk", [D, D]); wv = ctx.din("wv", [D, D])
        s_lw = ctx.sem("ma_lw")
        with nc.Block() as blk:
            LW = s_lw.n
            @blk.gpsimd
            def _(gp):
                gp.dma_start(tdrow_a[:], ctx.din("td_row", [1, D]).ap()).then_inc(s_lw.bump(16), 16)
                gp.dma_start(rev_a[:], ctx.din("revseg", [1, SEG]).ap()).then_inc(s_lw.bump(16), 16)
                for dc in range(8):
                    gp.dma_start(wsb[:, dc, 0:D], wk.ap()[dc * 128:(dc + 1) * 128, :]).then_inc(s_lw.bump(16), 16)
                    gp.dma_start(wsb[:, dc, D:2 * D], wv.ap()[dc * 128:(dc + 1) * 128, :]).then_inc(s_lw.bump(16), 16)
            @blk.vector
            def _(v):
                v.wait_ge(s_lw.h, LW + 288)
                for dc in range(8):
                    v.tensor_scalar(wsb[:, dc, :], wsb[:, dc, :],
                                    g["tm_col"][:, dc:dc + 1], None, OP.mult)
        for s in range(2):
            s_lx = [ctx.sem("ma_lxA"), ctx.sem("ma_lxB")]
            s_pe = ctx.sem("ma_pe"); s_ev = ctx.sem("ma_ev")
            with nc.Block() as blk:
                LX = [x.n for x in s_lx]
                PE, EV = s_pe.n, s_ev.n
                @blk.sync
                def _(sy):
                    for tc in range(8):
                        if tc >= 2:
                            sy.wait_ge(s_pe.h, PE + 4 * (tc - 1))
                        sy.dma_start(
                            xnt[:, tc % 2, :, :],
                            xn_fm.ap()[:, :, s * SEG + tc * 128: s * SEG + (tc + 1) * 128]
                            .rearrange("c p j -> p c j")).then_inc(s_lx[tc % 2].bump(16), 16)
                @blk.tensor
                def _(t_):
                    i = 0
                    for tc in range(8):
                        t_.wait_ge(s_lx[tc % 2].h, LX[tc % 2] + 16 * (tc // 2 + 1))
                        for w in range(2):
                            for jg in range(2):
                                ps = (pj0, pj1)[i % 2]
                                if i >= 2:
                                    t_.wait_ge(s_ev.h, EV + i - 1)
                                last = None
                                for cc2 in range(2):
                                    for dc in range(8):
                                        last = t_.matmul(ps[0:64, cc2 * 512:(cc2 + 1) * 512],
                                                  xnt[:, tc % 2, dc, jg * 64:(jg + 1) * 64],
                                                  wsb[:, dc, w * D + cc2 * 512:w * D + (cc2 + 1) * 512],
                                                  start=(dc == 0), stop=(dc == 7))
                                last.then_inc(s_pe.bump(), 1)
                                i += 1
                @blk.vector
                def _(v):
                    i = 0
                    for tc in range(8):
                        for w in range(2):
                            dstb = (ktm, vtm)[w]
                            for jg in range(2):
                                ps = (pj0, pj1)[i % 2]
                                v.wait_ge(s_pe.h, PE + i + 1)
                                v.tensor_copy(dstb[:, 2 * tc + jg, :], ps[0:64, :]).then_inc(s_ev.bump(), 1)
                                i += 1
            if parts == "prod":
                continue
            s_db = ctx.sem("ma_db"); s_ex = ctx.sem("ma_ex"); s_kb = ctx.sem("ma_kb")
            s_up = ctx.sem("ma_up"); s_sp = ctx.sem("ma_sp")
            with nc.Block() as blk:
                DB, EX, KB, UP, SP = s_db.n, s_ex.n, s_kb.n, s_up.n, s_sp.n
                @blk.tensor
                def _(t_):
                    for gr in range(16):
                        if gr > 0:
                            t_.wait_ge(s_ex.h, EX + gr)
                        last = None
                        for cc2 in range(2):
                            last = t_.matmul(pdb[0:64, cc2 * 512:(cc2 + 1) * 512],
                                      rev_a[:, 64 * gr:64 * gr + 64],
                                      tdrow_a[:, cc2 * 512:(cc2 + 1) * 512],
                                      start=True, stop=True)
                        last.then_inc(s_db.bump(), 1)
                        t_.wait_ge(s_kb.h, KB + gr + 1)
                        last = None
                        for n2 in range(2):
                            n = 2 * gr + n2
                            base = 32 * n2
                            for h in range(H):
                                last = t_.matmul(pu[0:64, h * 64:(h + 1) * 64],
                                          kbu[base:base + 32, h * 64:(h + 1) * 64],
                                          vtm[base:base + 32, gr, h * 64:(h + 1) * 64],
                                          start=(n == 0 and h in (0, 8)),
                                          stop=(n == NCHUNK - 1 and h in (7, 15)))
                        last.then_inc(s_up.bump(), 1)
                @blk.scalar
                def _(sc):
                    for gr in range(16):
                        sc.wait_ge(s_db.h, DB + gr + 1)
                        if gr > 0:
                            sc.wait_ge(s_kb.h, KB + gr)
                        sc.activation(dbu[:], pdb[0:64, :], AF.Exp, scale=-1.0).then_inc(s_ex.bump(), 1)
                @blk.vector
                def _(v):
                    for gr in range(16):
                        v.wait_ge(s_ex.h, EX + gr + 1)
                        if gr > 0:
                            v.wait_ge(s_up.h, UP + gr)
                        v.tensor_tensor(kbu[:], ktm[:, gr, :], dbu[:], OP.mult).then_inc(s_kb.bump(), 1)
                    v.wait_ge(s_up.h, UP + 16)
                    v.tensor_copy(usb[:], pu[0:64, :]).then_inc(s_kb.bump(), 1)
                @blk.sync
                def _(sy):
                    sy.wait_ge(s_kb.h, KB + 17)
                    sy.dma_start(cc0s.ap()[s * 64:(s + 1) * 64, :], usb[:]).then_inc(s_sp.bump(16), 16)
                    if "cc0dbg" in ctx.taps:
                        dbg = ctx.dtmp("cc0dbg", [128, D], F32)
                        sy.dma_start(dbg.ap()[s * 64:(s + 1) * 64, :], usb[:]).then_inc(s_sp.bump(16), 16)
                    sy.dma_start(ktm_d.ap()[s], ktm[:]).then_inc(s_sp.bump(16), 16)
                    sy.dma_start(vtm_d.ap()[s], vtm[:]).then_inc(s_sp.bump(16), 16)
                    sy.wait_ge(s_sp.h, s_sp.n)
        if parts == "nocc" or SIM_SKIP_CC:
            ctx.sem("cc0").bump()
            return
        s_cc0 = ctx.sem("cc0")
        with nc.Block() as blk:
            @blk.gpsimd
            def _(gp):
                gp.collective_compute(
                    "AllGather", OP.bypass, replica_groups=[list(range(NCORE))],
                    ins=[ctx.dram["cc0src"].ap().opt()],
                    outs=[ctx.dram["cc0dst"].ap().opt()]).then_inc(s_cc0.bump(), 1)



def emit_moe(ctx, layer):
    nc = ctx.nc
    g = ctx.glob
    E = 8 if layer == 0 else 16
    act = AF.Relu if layer == 0 else AF.Gelu
    xn_fm = ctx.dtmp("xn_fm", [8, 128, TOK], BF16)
    g_d = ctx.dtmp(f"g{layer}", [128, 16, D], BF16)
    if layer == 0:
        w1 = ctx.din("m0_w1t", [E, D, 2048]); w2 = ctx.din("m0_w2t", [E, 2048, D])
        b1 = ctx.din("m0_b1", [E, 128, 16]); b2 = ctx.din("m0_b2", [E, D])
        wg_h = ctx.din("m0_wg", [D, E])
    else:
        w1 = ctx.din("m1_w1", [E, D, 2048]); w2 = ctx.din("m1_w2", [E, 2048, D])
        b1 = ctx.din("m1_b1", [E, 128, 16]); b2 = ctx.din("m1_b2", [E, D])
        wg_h = ctx.din("rw", [D, E])
    with contextlib.ExitStack() as _es:
        comb = _es.enter_context(nc.sbuf_tensor(ctx.name("mcb"), [128, 16, E], F32))
        lgall = _es.enter_context(nc.sbuf_tensor(ctx.name("mlg"), [128, 16, E], F32))
        m1n = _es.enter_context(nc.sbuf_tensor(ctx.name("mm1"), [128, 16], F32))
        m2a = _es.enter_context(nc.sbuf_tensor(ctx.name("mm2"), [128, 16], F32))
        em2 = _es.enter_context(nc.sbuf_tensor(ctx.name("mem"), [128, 16], F32))
        aux = _es.enter_context(nc.sbuf_tensor(ctx.name("mau"), [128, 16], F32))
        bgb = _es.enter_context(nc.sbuf_tensor(ctx.name("mgb"), [128, E], F32))
        xng = _es.enter_context(nc.sbuf_tensor(ctx.name("mxg"), [128, 2, 8, 128], BF16))
        bgsb = _es.enter_context(nc.sbuf_tensor(ctx.name("mbg16"), [1, 8], BF16))
        wgsb = _es.enter_context(nc.sbuf_tensor(ctx.name("mwg"), [128, 8, E], BF16))
        gacc = _es.enter_context(nc.sbuf_tensor(ctx.name("mga"), [128, 16, D], F32))
        pga = _es.enter_context(nc.psum_tensor(ctx.name("mpga"), [128, 512], F32))
        pgb = _es.enter_context(nc.psum_tensor(ctx.name("mpgb"), [128, 512], F32))
        s_l = ctx.sem("mo_l"); s_gp = ctx.sem("mo_gp"); s_gv = ctx.sem("mo_gv")
        s_gx = [ctx.sem("mo_gxA"), ctx.sem("mo_gxB")]
        with nc.Block() as blk:
            L0_, GP, GV = s_l.n, s_gp.n, s_gv.n
            GX = [x.n for x in s_gx]
            @blk.gpsimd
            def _(gp_):
                for dc in range(8):
                    gp_.dma_start(wgsb[:, dc, :], wg_h.ap()[dc * 128:(dc + 1) * 128, :]).then_inc(s_l.bump(16), 16)
                if layer == 0:
                    gp_.dma_start(bgsb[:], ctx.din("m0_bg", [1, 8]).ap()).then_inc(s_l.bump(16), 16)
            @blk.sync
            def _(sy):
                for tc in range(16):
                    if tc >= 2:
                        sy.wait_ge(s_gp.h, GP + tc - 1)
                    sy.dma_start(
                        xng[:, tc % 2, :, :],
                        xn_fm.ap()[:, :, tc * 128:(tc + 1) * 128].rearrange("c p j -> p c j")
                    ).then_inc(s_gx[tc % 2].bump(16), 16)
            @blk.tensor
            def _(t_):
                t_.wait_ge(s_l.h, L0_ + 16 * (9 if layer == 0 else 8))
                for tc in range(16):
                    t_.wait_ge(s_gx[tc % 2].h, GX[tc % 2] + 16 * (tc // 2 + 1))
                    if tc >= 2:
                        t_.wait_ge(s_gv.h, GV + tc - 1)
                    pgx = (pga, pgb)[tc % 2]
                    last = None
                    for dc in range(8):
                        last = t_.matmul(pgx[:, 0:E], xng[:, tc % 2, dc, :], wgsb[:, dc, :],
                                  start=(dc == 0), stop=(dc == 7))
                    last.then_inc(s_gp.bump(), 1)
            @blk.vector
            def _(v):
                for tc in range(16):
                    v.wait_ge(s_gp.h, GP + tc + 1)
                    v.tensor_copy(lgall[:, tc, :], (pga, pgb)[tc % 2][:, 0:E]).then_inc(s_gv.bump(), 1)
                for t in range(16):
                    v.memset(gacc[:, t, :], 0.0)
        if layer == 0:
            with nc.Block() as blk:
                @blk.tensor
                def _(t_):
                    t_.matmul(pga[:, 0:E], g["ones1b"][:], bgsb[:], start=True, stop=True)
            with nc.Block() as blk:
                @blk.vector
                def _(v):
                    v.tensor_copy(bgb[:], pga[:, 0:E])
                    v.drain()
                    for tc in range(16):
                        v.tensor_tensor(lgall[:, tc, :], lgall[:, tc, :], bgb[:], OP.add)
        with nc.Block() as blk:
            @blk.vector
            def _(v):
                for tc in range(16):
                    v.reduce_max(m1n[:, tc:tc + 1], lgall[:, tc, :], axis=AX, negate=True)
                v.drain()
                if layer == 0:
                    for tc in range(16):
                        v.tensor_scalar(aux[:, tc:tc + 1], m1n[:, tc:tc + 1], -1.0, None, OP.mult)
                    v.drain()
                    for tc in range(16):
                        v.tensor_scalar(comb[:, tc, :], lgall[:, tc, :], aux[:, tc:tc + 1], None, OP.is_ge)
                        v.drain()
                        v.scalar_tensor_tensor(comb[:, tc, :], comb[:, tc, :], NEGBIG,
                                               lgall[:, tc, :], OP.mult, OP.add)
                        v.drain()
                        v.reduce_max(m2a[:, tc:tc + 1], comb[:, tc, :], axis=AX)
                        v.drain()
                        v.tensor_scalar(comb[:, tc, :], lgall[:, tc, :], m2a[:, tc:tc + 1],
                                        None, OP.is_ge)
                else:
                    for tc in range(16):
                        v.tensor_scalar(aux[:, tc:tc + 1], m1n[:, tc:tc + 1], -1.0, None, OP.mult)
                        v.drain()
                        v.tensor_scalar(comb[:, tc, :], lgall[:, tc, :], aux[:, tc:tc + 1],
                                        None, OP.is_ge)
        with nc.Block() as blk:
            @blk.scalar
            def _(sc):
                if layer == 0:
                    for tc in range(16):
                        sc.activation(lgall[:, tc, :], lgall[:, tc, :], AF.Exp,
                                      bias=m1n[:, tc:tc + 1])
                        sc.activation(em2[:, tc:tc + 1], m2a[:, tc:tc + 1], AF.Exp,
                                      bias=m1n[:, tc:tc + 1])
                else:
                    for tc in range(16):
                        sc.activation(em2[:, tc:tc + 1], m1n[:, tc:tc + 1], AF.Exp)
        with nc.Block() as blk:
            @blk.vector
            def _(v):
                v.tensor_scalar(aux[:], em2[:], 1.0, None, OP.add)
                v.drain()
                v.reciprocal(em2[:], aux[:])
                v.drain()
                if layer == 0:
                    for tc in range(16):
                        v.tensor_tensor(comb[:, tc, :], comb[:, tc, :], lgall[:, tc, :], OP.mult)
                        v.drain()
                        v.tensor_scalar(comb[:, tc, :], comb[:, tc, :], em2[:, tc:tc + 1],
                                        None, OP.mult)
                        v.drain()
                else:
                    for tc in range(16):
                        v.tensor_scalar(comb[:, tc, :], comb[:, tc, :], em2[:, tc:tc + 1],
                                        None, OP.mult)
        with contextlib.ExitStack() as _es:
            w1sb = _es.enter_context(nc.sbuf_tensor(ctx.name("mw1"), [128, 8, 2048], BF16))
            w2sb = _es.enter_context(nc.sbuf_tensor(ctx.name("mw2"), [128, 16, D], BF16))
            hsb = _es.enter_context(nc.sbuf_tensor(ctx.name("mh"), [128, 16, 512], BF16))
            xnt = _es.enter_context(nc.sbuf_tensor(ctx.name("mx5"), [128, 2, 8, 512], BF16))
            b1sb = _es.enter_context(nc.sbuf_tensor(ctx.name("mb1"), [128, 2, 16], F32))
            b2sb = _es.enter_context(nc.sbuf_tensor(ctx.name("mb2"), [1, 2, D], BF16))
            ph = _es.enter_context(nc.psum_tensor(ctx.name("mph"), [128, 2, 512], F32))
            po = _es.enter_context(nc.psum_tensor(ctx.name("mpo"), [128, 2, 512], F32))
            s_w1 = ctx.sem("mo_w1"); s_w2 = ctx.sem("mo_w2")
            s_b1s = ctx.sem("mo_b1"); s_b2s = ctx.sem("mo_b2")
            s_xn = [ctx.sem("mo_xnA"), ctx.sem("mo_xnB")]
            s_hp = ctx.sem("mo_hp"); s_ha = ctx.sem("mo_ha")
            s_ht = ctx.sem("mo_ht"); s_op = ctx.sem("mo_op"); s_mac = ctx.sem("mo_mac")
            s_ot = ctx.sem("mo_ot")
            with nc.Block() as blk:
                W1, W2, HP, HA, OPM, MAC = (
                    s_w1.n, s_w2.n, s_hp.n, s_ha.n, s_op.n, s_mac.n)
                B1, B2 = s_b1s.n, s_b2s.n
                XN = [x.n for x in s_xn]
                @blk.gpsimd
                def _(gp_):
                    for e in range(E):
                        if e >= 1:
                            gp_.wait_ge(s_hp.h, HP + 64 * e)
                        for dc in range(8):
                            gp_.dma_start(w1sb[:, dc, :], w1.ap()[e, dc * 128:(dc + 1) * 128, :]
                                          ).then_inc(s_w1.bump(16), 16)
                        if e >= 2:
                            gp_.wait_ge(s_op.h, OPM + 32 * (e - 1))
                        gp_.dma_start(b1sb[:, e % 2, :], b1.ap()[e]).then_inc(s_b1s.bump(16), 16)
                        gp_.dma_start(b2sb[0:1, e % 2, :], b2.ap()[e:e + 1, :]).then_inc(s_b2s.bump(16), 16)
                        if e >= 1:
                            gp_.wait_ge(s_op.h, OPM + 32 * e)
                        for fc in range(16):
                            gp_.dma_start(w2sb[:, fc, :], w2.ap()[e, fc * 128:(fc + 1) * 128, :]
                                          ).then_inc(s_w2.bump(16), 16)
                @blk.sync
                def _(sy):
                    i = 0
                    for e in range(E):
                        for hf in range(2):
                            for t5 in range(2):
                                if i >= 2:
                                    sy.wait_ge(s_hp.h, HP + 16 * (i - 1))
                                sy.dma_start(
                                    xnt[:, i % 2, :, :],
                                    xn_fm.ap()[:, :, hf * 1024 + t5 * 512:hf * 1024 + (t5 + 1) * 512]
                                    .rearrange("c p j -> p c j")).then_inc(s_xn[i % 2].bump(16), 16)
                                i += 1
                @blk.tensor
                def _(t_):
                    i = 0
                    for e in range(E):
                        for hf in range(2):
                            for t5 in range(2):
                                t_.wait_ge(s_w1.h, W1 + 128 * (e + 1))
                                t_.wait_ge(s_xn[i % 2].h, XN[i % 2] + 16 * (i // 2 + 1))
                                for fc in range(16):
                                    gi = (i * 16 + fc)
                                    if gi >= 2:
                                        t_.wait_ge(s_ha.h, HA + gi - 1)
                                    last = None
                                    for dc in range(8):
                                        last = t_.matmul(ph[:, gi % 2, :], w1sb[:, dc, fc * 128:(fc + 1) * 128],
                                                  xnt[:, i % 2, dc, :], start=(dc == 0), stop=(dc == 7))
                                    last.then_inc(s_hp.bump(), 1)
                                t_.wait_ge(s_w2.h, W2 + 256 * (e + 1))
                                t_.wait_ge(s_b2s.h, B2 + 16 * (e + 1))
                                t_.wait_ge(s_ha.h, HA + 16 * (i + 1))
                                for tloc in range(4):
                                    for ot in range(2):
                                        oi = i * 8 + tloc * 2 + ot
                                        if oi >= 2:
                                            t_.wait_ge(s_mac.h, MAC + oi - 1)
                                        for fc in range(16):
                                            t_.matmul(po[:, oi % 2, :], hsb[:, fc, tloc * 128:(tloc + 1) * 128],
                                                      w2sb[:, fc, ot * 512:(ot + 1) * 512],
                                                      start=(fc == 0), stop=False)
                                        t_.matmul(po[:, oi % 2, :], g["ones1b"][:],
                                                  b2sb[0:1, e % 2, ot * 512:(ot + 1) * 512],
                                                  start=False, stop=True).then_inc(s_op.bump(), 1)
                                i += 1
                @blk.scalar
                def _(sc):
                    i = 0
                    for e in range(E):
                        for hf in range(2):
                            for t5 in range(2):
                                for fc in range(16):
                                    gi = i * 16 + fc
                                    sc.wait_ge(s_hp.h, HP + gi + 1)
                                    if hf == 0 and t5 == 0 and fc == 0:
                                        sc.wait_ge(s_b1s.h, B1 + 16 * (e + 1))
                                    if i >= 1 and fc == 0:
                                        sc.wait_ge(s_op.h, OPM + 8 * i)
                                    sc.activation(hsb[:, fc, :],
                                                  ph[:, gi % 2, :], act,
                                                  bias=b1sb[:, e % 2, fc:fc + 1]).then_inc(s_ha.bump(), 1)
                                i += 1
                @blk.vector
                def _(v):
                    i = 0
                    for e in range(E):
                        for hf in range(2):
                            for t5 in range(2):
                                for tloc in range(4):
                                    for ot in range(2):
                                        oi = i * 8 + tloc * 2 + ot
                                        tg = hf * 8 + t5 * 4 + tloc
                                        v.wait_ge(s_op.h, OPM + oi + 1)
                                        v.scalar_tensor_tensor(
                                            gacc[:, tg, ot * 512:(ot + 1) * 512],
                                            po[:, oi % 2, :], comb[:, tg, e:e + 1],
                                            gacc[:, tg, ot * 512:(ot + 1) * 512],
                                            OP.mult, OP.add).then_inc(s_mac.bump(), 1)
                                i += 1
            with (nc.sbuf_tensor(ctx.name("mgs"), [128, 2, D], BF16) as gsb,):
                s_sg = ctx.sem("mo_sg")
                s_go = [ctx.sem("mo_goA"), ctx.sem("mo_goB")]
                with nc.Block() as blk:
                    SG = s_sg.n
                    GO = [x.n for x in s_go]
                    @blk.scalar
                    def _(sc):
                        for t in range(16):
                            if t >= 2:
                                sc.wait_ge(s_go[t % 2].h, GO[t % 2] + 16 * (t // 2))
                            sc.activation(gsb[:, t % 2, :], gacc[:, t, :], AF.Silu).then_inc(s_sg.bump(), 1)
                    @blk.sync
                    def _(sy):
                        for t in range(16):
                            sy.wait_ge(s_sg.h, SG + t + 1)
                            sy.dma_start(g_d.ap()[:, t, :], gsb[:, t % 2, :]).then_inc(s_go[t % 2].bump(16), 16)
                        sy.wait_ge(s_go[0].h, GO[0] + 128)
                        sy.wait_ge(s_go[1].h, GO[1] + 128)


def emit_mixer_b(ctx, parts="all"):
    nc = ctx.nc
    g = ctx.glob
    x_in = ctx.din("x", [TOK, D])
    xn_fm = ctx.dtmp("xn_fm", [8, 128, TOK], BF16)
    ktm_d = ctx.dtmp("ktm", [2, 64, 16, D], BF16)
    vtm_d = ctx.dtmp("vtm", [2, 64, 16, D], BF16)
    cc0d = ctx.dtmp("cc0dst", [NCORE * 128, D], F32)
    g0_d = ctx.dtmp("g0", [128, 16, D], BF16)
    x1_d = ctx.dtmp("x1", [TOK, D], F32)
    wk = ctx.din("wk", [D, D]); wv = ctx.din("wv", [D, D])
    wr = ctx.din("wr", [D, D]); wgm = ctx.din("wgm", [D, D])
    wo = ctx.din("wo", [D, D])
    with contextlib.ExitStack() as mbes:
        ec_ = mbes.enter_context
        ktmq = ec_(nc.sbuf_tensor("mb_ktq", [64, 2, 4, D], BF16))
        vtmq = ec_(nc.sbuf_tensor("mb_vtq", [64, 2, 4, D], BF16))
        kfm = ec_(nc.sbuf_tensor("mb_kfm", [128, 8, SEG], BF16))
        vfm = ec_(nc.sbuf_tensor("mb_vfm", [128, 8, SEG], BF16))
        wbuf = ec_(nc.sbuf_tensor("mb_w", [128, 8, 2048], BF16))
        wosb = ec_(nc.sbuf_tensor("mb_wo", [128, 8, D], BF16))
        rq = ec_(nc.sbuf_tensor("mb_rq", [128, 8, 256], BF16))
        gq = ec_(nc.sbuf_tensor("mb_gq", [128, 8, 256], BF16))
        xnq = ec_(nc.sbuf_tensor("mb_xq", [128, 2, 8, 256], BF16))
        khb = ec_(nc.sbuf_tensor("mb_kh", [128, 2, 8, CH], BF16))
        ktb = ec_(nc.sbuf_tensor("mb_kt", [128, 2, 8, CH], BF16))
        kb = ec_(nc.sbuf_tensor("mb_kb", [64, D], BF16))
        am = ec_(nc.sbuf_tensor("mb_am", [64, 512], BF16))
        S = ec_(nc.sbuf_tensor("mb_S", [128, D], F32))
        S2 = ec_(nc.sbuf_tensor("mb_S2", [128, D], BF16))
        St = ec_(nc.sbuf_tensor("mb_St", [128, D], F32))
        tw = ec_(nc.sbuf_tensor("mb_tw", [128, 8, CH], F32))
        ws = ec_(nc.sbuf_tensor("mb_ws", [128, 8, CH], F32))
        rp = ec_(nc.sbuf_tensor("mb_rp", [128, 8, 256], BF16))
        usl = ec_(nc.sbuf_tensor("mb_us", [128, D], F32))
        xres = ec_(nc.sbuf_tensor("mb_xr", [128, 2, D], F32))
        gres = ec_(nc.sbuf_tensor("mb_gr", [128, 2, D], BF16))
        x1sb = ec_(nc.sbuf_tensor("mb_x1", [128, 2, D], F32))
        cfn = ec_(nc.sbuf_tensor("mb_cn", [16, 2, 64], F32))
        for s in range(2):
            s_lw = ctx.sem("mb_lw")
            with nc.Block() as blk:
                LW = s_lw.n
                @blk.gpsimd
                def _(gp_):
                    for dc in range(8):
                        gp_.dma_start(wbuf[:, dc, 0:D], wk.ap()[dc * 128:(dc + 1) * 128, :]).then_inc(s_lw.bump(16), 16)
                        gp_.dma_start(wbuf[:, dc, D:2 * D], wv.ap()[dc * 128:(dc + 1) * 128, :]).then_inc(s_lw.bump(16), 16)
                        gp_.dma_start(wosb[:, dc, :], wo.ap()[dc * 128:(dc + 1) * 128, :]).then_inc(s_lw.bump(16), 16)
                @blk.vector
                def _(v):
                    v.wait_ge(s_lw.h, LW + 16 * 24)
                    for dc in range(8):
                        v.tensor_scalar(wbuf[:, dc, :], wbuf[:, dc, :],
                                        g["tm_col"][:, dc:dc + 1], None, OP.mult)
            s_xf = [ctx.sem("mb_xfA"), ctx.sem("mb_xfB")]
            s_fp = ctx.sem("mb_fp"); s_fe = ctx.sem("mb_fe")
            with contextlib.ExitStack() as pes1:
                pfa = pes1.enter_context(nc.psum_tensor(ctx.name("mbpfa"), [128, 512], F32))
                pfb = pes1.enter_context(nc.psum_tensor(ctx.name("mbpfb"), [128, 512], F32))
                with nc.Block() as blk:
                    FP, FE = s_fp.n, s_fe.n
                    XF = [x.n for x in s_xf]
                    @blk.sync
                    def _(sy):
                        for qq in range(4):
                            if qq >= 2:
                                sy.wait_ge(s_fp.h, FP + 16 * (qq - 1))
                            sy.dma_start(
                                xnq[:, qq % 2, :, :],
                                xn_fm.ap()[:, :, s * SEG + qq * 256:s * SEG + (qq + 1) * 256]
                                .rearrange("c p j -> p c j")).then_inc(s_xf[qq % 2].bump(16), 16)
                    @blk.tensor
                    def _(t_):
                        i = 0
                        for qq in range(4):
                            t_.wait_ge(s_xf[qq % 2].h, XF[qq % 2] + 16 * (qq // 2 + 1))
                            for w in range(2):
                                for ec in range(8):
                                    if i >= 2:
                                        t_.wait_ge(s_fe.h, FE + i - 1)
                                    last = None
                                    for dc in range(8):
                                        last = t_.matmul((pfa, pfb)[i % 2][:, 0:256],
                                                  wbuf[:, dc, w * D + ec * 128:w * D + (ec + 1) * 128],
                                                  xnq[:, qq % 2, dc, :],
                                                  start=(dc == 0), stop=(dc == 7))
                                    last.then_inc(s_fp.bump(), 1)
                                    i += 1
                    @blk.vector
                    def _(v):
                        i = 0
                        for qq in range(4):
                            for w in range(2):
                                dstb = (kfm, vfm)[w]
                                for ec in range(8):
                                    v.wait_ge(s_fp.h, FP + i + 1)
                                    v.tensor_copy(dstb[:, ec, qq * 256:(qq + 1) * 256],
                                                  (pfa, pfb)[i % 2][:, 0:256]).then_inc(s_fe.bump(), 1)
                                    i += 1
            if parts == "m1":
                continue
            s_cs = [ctx.sem("mb_csA"), ctx.sem("mb_csB")]
            s_cp = ctx.sem("mb_cp"); s_cv = ctx.sem("mb_cv")
            s_ca = ctx.sem("mb_ca")
            with contextlib.ExitStack() as pes2:
                pco = pes2.enter_context(nc.psum_tensor(ctx.name("mbpc"), [128, D], F32))
                with nc.Block() as blk:
                    CP, CV, CA = s_cp.n, s_cv.n, s_ca.n
                    CS = [x.n for x in s_cs]
                    s_cc0 = ctx.sem("cc0")
                    @blk.vector
                    def _(v):
                        v.memset(S[:], 0.0)
                        for sl in range(16):
                            v.wait_ge(s_cs[0].h, CS[0] + 32 * (sl + 1))
                            v.wait_ge(s_cp.h, CP + sl + 1)
                            v.tensor_tensor(St[:], pco[:], usl[:], OP.mult)
                            v.drain()
                            v.tensor_tensor(S[:], S[:], St[:], OP.add).then_inc(s_cv.bump(), 1)
                            v.drain()
                        v.tensor_copy(S2[:], S[:])
                    @blk.sync
                    def _(sy):
                        if not SIM_SKIP_CC:
                            sy.wait_ge(s_cc0.h, 1)
                        for sl in range(16):
                            if sl >= 1:
                                sy.wait_ge(s_cv.h, CV + sl)
                            sy.dma_start(usl[0:64, :],
                                         cc0d.ap()[sl * 64:(sl + 1) * 64, :]).then_inc(s_cs[0].bump(16), 16)
                            sy.dma_start(usl[64:128, :],
                                         cc0d.ap()[sl * 64:(sl + 1) * 64, :]).then_inc(s_cs[0].bump(16), 16)
                    @blk.scalar
                    def _(sc):
                        for sl in range(16):
                            if sl >= 1:
                                sc.wait_ge(s_cp.h, CP + sl)
                            sc.activation(cfn[:, sl % 2, :], g["td_nat"][:], AF.Exp,
                                          scale=g["nexpo"][0:16, s * 16 + sl:s * 16 + sl + 1]
                                          ).then_inc(s_ca.bump(), 1)
                    @blk.tensor
                    def _(t_):
                        for sl in range(16):
                            t_.wait_ge(s_ca.h, CA + sl + 1)
                            if sl >= 1:
                                t_.wait_ge(s_cv.h, CV + sl)
                            last = None
                            for hh in (0, 64):
                                for cc in (0, 512):
                                    last = t_.matmul(pco[hh:hh + 64, cc:cc + 512],
                                              cfn[:, sl % 2, :], g["hsel"][:, cc:cc + 512],
                                              start=True, stop=True)
                            last.then_inc(s_cp.bump(), 1)
            if parts == "m2":
                continue
            m3rg = parts == "m3rg"
            m3ch = parts in ("m3rg", "m3ch", "m3c1", "m3c2")
            c_wkv = parts not in ("m3c1",)
            c_u = parts not in ("m3c1", "m3c2")
            s_fold = ctx.sem("mb_fo")
            s_xq2 = [ctx.sem("mb_x2A"), ctx.sem("mb_x2B")]
            s_kv2 = [ctx.sem("mb_kvA"), ctx.sem("mb_kvB")]
            s_rgp = ctx.sem("mb_rg")
            s_rge = ctx.sem("mb_re"); s_ge = ctx.sem("mb_gse")
            s_pA = ctx.sem("mb_pA"); s_pw = ctx.sem("mb_pw"); s_pU = ctx.sem("mb_pU")
            s_wo2 = ctx.sem("mb_wo"); s_dkh = ctx.sem("mb_d1"); s_dam = ctx.sem("mb_d2")
            s_drg = ctx.sem("mb_d3"); s_dkb = ctx.sem("mb_d4"); s_dsu = ctx.sem("mb_d5")
            s_rs = ctx.sem("mb_rs")
            s_xt = [ctx.sem("mb_xtA"), ctx.sem("mb_xtB")]
            s_gt = [ctx.sem("mb_gtA"), ctx.sem("mb_gtB")]
            s_xo = [ctx.sem("mb_xoA"), ctx.sem("mb_xoB")]
            s_lw2 = ctx.sem("mb_l2")
            with contextlib.ExitStack() as pes3:
                pA = pes3.enter_context(nc.psum_tensor(ctx.name("mbpa"), [128, 512], F32))
                pw = pes3.enter_context(nc.psum_tensor(ctx.name("mbpw"), [128, 16, CH], F32))
                pU = pes3.enter_context(nc.psum_tensor(ctx.name("mbpu"), [128, D], F32))
                pRGa = pes3.enter_context(nc.psum_tensor(ctx.name("mbra"), [128, 512], F32))
                pRGb = pes3.enter_context(nc.psum_tensor(ctx.name("mbrb"), [128, 512], F32))
                ps_s = pes3.enter_context(nc.psum_tensor(ctx.name("mbps"), [128, 2, 512], F32))
                with nc.Block() as blk:
                    FO, RGP, RGE, GE = s_fold.n, s_rgp.n, s_rge.n, s_ge.n
                    PA, PW, PU, WO2 = s_pA.n, s_pw.n, s_pU.n, s_wo2.n
                    KH, AM, RG3, KB4, SU = s_dkh.n, s_dam.n, s_drg.n, s_dkb.n, s_dsu.n
                    RS, LW2 = s_rs.n, s_lw2.n
                    XQ = [x.n for x in s_xq2]
                    KV2 = [x.n for x in s_kv2]
                    XT = [x.n for x in s_xt]
                    GT = [x.n for x in s_gt]
                    XO = [x.n for x in s_xo]
                    @blk.gpsimd
                    def _(gp_):
                        for dc in range(8):
                            gp_.dma_start(wbuf[:, dc, 0:D], wr.ap()[dc * 128:(dc + 1) * 128, :]).then_inc(s_lw2.bump(16), 16)
                            gp_.dma_start(wbuf[:, dc, D:2 * D], wgm.ap()[dc * 128:(dc + 1) * 128, :]).then_inc(s_lw2.bump(16), 16)
                        for tc in range(8):
                            if m3ch:
                                continue
                            if tc >= 2:
                                gp_.wait_ge(s_rs.h, RS + tc - 1)
                            gp_.dma_start(gres[:, tc % 2, :],
                                          g0_d.ap()[:, s * 8 + tc, :]).then_inc(s_gt[tc % 2].bump(16), 16)
                    @blk.sync
                    def _(sy):
                        for q in range(4):
                            if q >= 2:
                                sy.wait_ge(s_rgp.h, RGP + 16 * (q - 1))
                                if not m3rg:
                                    sy.wait_ge(s_pU.h, PU + DEBUG_NCH * (q - 1))
                            sy.dma_start(
                                xnq[:, q % 2, :, :],
                                xn_fm.ap()[:, :, s * SEG + q * 256:s * SEG + (q + 1) * 256]
                                .rearrange("c p j -> p c j")).then_inc(s_xq2[q % 2].bump(16), 16)
                            if not m3rg:
                                sy.dma_start(ktmq[:, q % 2, :, :],
                                             ktm_d.ap()[s, :, 4 * q:4 * q + 4, :]).then_inc(s_kv2[q % 2].bump(16), 16)
                                sy.dma_start(vtmq[:, q % 2, :, :],
                                             vtm_d.ap()[s, :, 4 * q:4 * q + 4, :]).then_inc(s_kv2[q % 2].bump(16), 16)
                            if m3ch:
                                continue
                            for tc in (2 * q, 2 * q + 1):
                                if tc >= 2:
                                    sy.wait_ge(s_rs.h, RS + tc - 1)
                                sy.dma_start(xres[:, tc % 2, :],
                                             x_in.ap()[s * SEG + tc * 128:s * SEG + (tc + 1) * 128, :]
                                             ).then_inc(s_xt[tc % 2].bump(16), 16)
                                if tc >= 2:
                                    to = tc - 2
                                    sy.wait_ge(s_rs.h, RS + to + 1)
                                    sy.dma_start(x1_d.ap()[s * SEG + to * 128:s * SEG + (to + 1) * 128, :],
                                                 x1sb[:, to % 2, :]).then_inc(s_xo[to % 2].bump(16), 16)
                        if not m3ch:
                            for tc in (6, 7):
                                sy.wait_ge(s_rs.h, RS + tc + 1)
                                sy.dma_start(x1_d.ap()[s * SEG + tc * 128:s * SEG + (tc + 1) * 128, :],
                                             x1sb[:, tc % 2, :]).then_inc(s_xo[tc % 2].bump(16), 16)
                            sy.wait_ge(s_xo[0].h, XO[0] + 64)
                            sy.wait_ge(s_xo[1].h, XO[1] + 64)
                    @blk.vector
                    def _(v):
                        v.wait_ge(s_lw2.h, LW2 + 256)
                        last = None
                        for dc in range(8):
                            last = v.tensor_scalar(wbuf[:, dc, :], wbuf[:, dc, :],
                                            g["tm_col"][:, dc:dc + 1], None, OP.mult)
                        last.then_inc(s_fold.bump(), 1)
                        for q in range(4):
                            for ec in range(8):
                                v.wait_ge(s_rgp.h, RGP + 16 * q + ec + 1)
                                v.tensor_copy(rq[:, ec, :], (pRGa, pRGb)[ec % 2][:, 0:256]).then_inc(s_rge.bump(), 1)
                            if m3rg:
                                continue
                            v.wait_ge(s_kv2[q % 2].h, KV2[q % 2] + 32 * (q // 2 + 1))
                            for m in range(8 * q, 8 * q + DEBUG_NCH):
                                sl = slice(m * CH, (m + 1) * CH)
                                par = m % 2
                                if m >= 2 and c_wkv:
                                    v.wait_ge(s_pw.h, PW + m - 1)
                                v.tensor_tensor(khb[:, par, :, :], kfm[:, :, sl], g["dhat"][:], OP.mult)
                                v.tensor_tensor(ktb[:, par, :, :], kfm[:, :, sl], g["dneg"][:], OP.mult).then_inc(s_dkh.bump(), 1)
                                v.wait_ge(s_pA.h, PA + m + 1)
                                v.tensor_tensor(am[32 * par:32 * par + 32, :],
                                                pA[32 * par:32 * par + 32, :],
                                                g["mask64"][32 * par:32 * par + 32, :],
                                                OP.mult).then_inc(s_dam.bump(), 1)
                                if not c_wkv:
                                    v.nop().then_inc(s_drg.bump(), 1)
                                    v.nop().then_inc(s_dkb.bump(), 1)
                                    v.nop().then_inc(s_dsu.bump(), 1)
                                    continue
                                v.tensor_tensor(tw[:], kfm[:, :, sl], vfm[:, :, sl], OP.mult)
                                v.drain()
                                v.tensor_tensor(tw[:], tw[:], g["tfmap"][:], OP.mult)
                                v.wait_ge(s_pw.h, PW + m + 1)
                                v.wait_ge(s_ge.h, GE + 8 * q + 8)
                                v.drain()
                                v.tensor_tensor(ws[:], pw[:, 0:8, :], tw[:], OP.add)
                                v.drain()
                                v.tensor_tensor(ws[:], ws[:], rq[:, :, (m % 8) * CH:(m % 8 + 1) * CH], OP.mult)
                                v.drain()
                                v.tensor_tensor(rp[:, :, (m % 8) * CH:(m % 8 + 1) * CH],
                                                ws[:], gq[:, :, (m % 8) * CH:(m % 8 + 1) * CH],
                                                OP.mult).then_inc(s_drg.bump(), 1)
                                if not c_u:
                                    v.nop().then_inc(s_dkb.bump(), 1)
                                    v.nop().then_inc(s_dsu.bump(), 1)
                                    continue
                                if m >= 2:
                                    v.wait_ge(s_pU.h, PU + m - 1)
                                v.tensor_tensor(kb[32 * par:32 * par + 32, :],
                                                ktmq[32 * par:32 * par + 32, q % 2, m // 2 - 4 * q, :],
                                                g["dbar32"][32 * par:32 * par + 32, :],
                                                OP.mult).then_inc(s_dkb.bump(), 1)
                                v.wait_ge(s_pU.h, PU + m + 1)
                                v.tensor_tensor(St[:], S[:], g["dc32"][:], OP.mult)
                                v.drain()
                                v.tensor_tensor(S2[:], St[:], pU[:], OP.add).then_inc(s_dsu.bump(), 1)
                                v.drain()
                                v.tensor_tensor(S[:], St[:], pU[:], OP.add)
                                v.drain()
                            for hh in range(2):
                                if m3ch:
                                    continue
                                tc = 2 * q + hh
                                v.wait_ge(s_wo2.h, WO2 + tc + 1)
                                v.wait_ge(s_xt[tc % 2].h, XT[tc % 2] + 16 * (tc // 2 + 1))
                                v.wait_ge(s_gt[tc % 2].h, GT[tc % 2] + 16 * (tc // 2 + 1))
                                if tc >= 2:
                                    v.wait_ge(s_xo[tc % 2].h, XO[tc % 2] + 16 * (tc // 2))
                                last = None
                                for ot in range(2):
                                    v.tensor_tensor(St[:, 0:512], ps_s[:, ot, :],
                                                    gres[:, tc % 2, ot * 512:(ot + 1) * 512], OP.mult)
                                    v.drain()
                                    last = v.tensor_tensor(x1sb[:, tc % 2, ot * 512:(ot + 1) * 512],
                                                    St[:, 0:512], xres[:, tc % 2, ot * 512:(ot + 1) * 512],
                                                    OP.add)
                                    v.drain()
                                last.then_inc(s_rs.bump(), 1)
                    @blk.scalar
                    def _(sc):
                        for q in range(4):
                            for ec in range(8):
                                sc.wait_ge(s_rgp.h, RGP + 16 * q + 8 + ec + 1)
                                sc.activation(gq[:, ec, :], (pRGa, pRGb)[ec % 2][:, 0:256],
                                              AF.Silu).then_inc(s_ge.bump(), 1)
                    @blk.tensor
                    def _(t_):
                        prevb = [None]
                        def bmm(base, *a, **kw):
                            if prevb[0] is not None and prevb[0] != base:
                                t_.drain()
                            prevb[0] = base
                            return t_.matmul(*a, **kw)
                        for q in range(4):
                            t_.wait_ge(s_fold.h, FO + 1)
                            t_.wait_ge(s_xq2[q % 2].h, XQ[q % 2] + 16 * (q // 2 + 1))
                            if q >= 1 and not m3ch:
                                t_.wait_ge(s_drg.h, RG3 + DEBUG_NCH * q)
                            for w in range(2):
                                for ec in range(8):
                                    gi = 16 * q + w * 8 + ec
                                    if gi >= 2:
                                        ndone = gi - 1
                                        nfullq, rem = divmod(ndone, 16)
                                        rdone = nfullq * 8 + min(rem, 8)
                                        gdone = nfullq * 8 + max(rem - 8, 0)
                                        if rdone > 0:
                                            t_.wait_ge(s_rge.h, RGE + rdone)
                                        if gdone > 0:
                                            t_.wait_ge(s_ge.h, GE + gdone)
                                    prg = (pRGa, pRGb)[ec % 2]
                                    last = None
                                    for dc in range(8):
                                        last = bmm(0, prg[:, 0:256], wbuf[:, dc, w * D + ec * 128:w * D + (ec + 1) * 128],
                                                  xnq[:, q % 2, dc, :],
                                                  start=(dc == 0), stop=(dc == 7))
                                    last.then_inc(s_rgp.bump(), 1)
                            if m3rg:
                                continue
                            t_.wait_ge(s_kv2[q % 2].h, KV2[q % 2] + 32 * (q // 2 + 1))
                            for m in range(8 * q, 8 * q + DEBUG_NCH):
                                par = m % 2
                                t_.wait_ge(s_dkh.h, KH + m + 1)
                                t_.wait_ge(s_dam.h, AM + m)
                                first = True
                                last = None
                                for h in [hh for hh in range(H) if hh % 2 == 0] + [hh for hh in range(H) if hh % 2 == 1]:
                                    hb = 64 * (h % 2)
                                    last = bmm(hb, pA[32 * par:32 * par + 32, h * 32:(h + 1) * 32],
                                              ktb[hb:hb + 64, par, h // 2, :],
                                              khb[hb:hb + 64, par, h // 2, :],
                                              start=first, stop=(h == H - 1))
                                    first = False
                                last.then_inc(s_pA.bump(), 1)
                                if not c_wkv:
                                    t_.nop().then_inc(s_pw.bump(), 1)
                                    t_.nop().then_inc(s_pU.bump(), 1)
                                    continue
                                t_.wait_ge(s_dam.h, AM + m + 1)
                                t_.wait_ge(s_dsu.h, SU + m)
                                t_.wait_ge(s_drg.h, RG3 + m)
                                pwflat = bass.AP(pw, 0, [[512, 128], [1, 512]])
                                bmm("z", pwflat, g["ones1b"][:], g["zrow"][:],
                                    start=True, stop=False)
                                for h in range(H):
                                    bmm(32 * par, pw[64 * (h % 2):64 * (h % 2) + 64, h // 2, :],
                                        vtmq[32 * par:32 * par + 32, q % 2, m // 2 - 4 * q, h * 64:(h + 1) * 64],
                                        am[32 * par:32 * par + 32, h * 32:(h + 1) * 32],
                                        start=False, stop=False)
                                for h in [hh for hh in range(H) if hh % 2 == 0] + [hh for hh in range(H) if hh % 2 == 1]:
                                    hb = 64 * (h % 2)
                                    bmm(hb, pw[hb:hb + 64, h // 2, :],
                                        S2[hb:hb + 64, h * 64:(h + 1) * 64],
                                        khb[hb:hb + 64, par, h // 2, :],
                                        start=False, stop=False)
                                bmm("z", pwflat, g["ones1b"][:], g["zrow"][:],
                                    start=False, stop=True).then_inc(s_pw.bump(), 1)
                                if not c_u:
                                    t_.nop().then_inc(s_pU.bump(), 1)
                                    if m % 4 == 3 and not m3ch:
                                        pass
                                    continue
                                t_.wait_ge(s_dkb.h, KB4 + m + 1)
                                t_.wait_ge(s_dsu.h, SU + m)
                                for cc in (0, 512):
                                    bmm("z", pU[:, cc:cc + 512], g["ones1b"][:], g["zrow"][:],
                                        start=True, stop=False)
                                for half in range(2):
                                    for h in range(H):
                                        bmm(32 * par, pU[64 * half:64 * half + 64, h * 64:(h + 1) * 64],
                                            kb[32 * par:32 * par + 32, h * 64:(h + 1) * 64],
                                            vtmq[32 * par:32 * par + 32, q % 2, m // 2 - 4 * q, h * 64:(h + 1) * 64],
                                            start=False, stop=False)
                                last = None
                                for cc in (0, 512):
                                    last = bmm("z", pU[:, cc:cc + 512], g["ones1b"][:], g["zrow"][:],
                                               start=False, stop=True)
                                last.then_inc(s_pU.bump(), 1)
                                if m % 4 == 3 and not m3ch:
                                    tc = m // 4
                                    t_.wait_ge(s_drg.h, RG3 + m + 1)
                                    t_.wait_ge(s_rs.h, RS + tc)
                                    last = None
                                    for ot in range(2):
                                        for dc in range(8):
                                            last = bmm(0, ps_s[:, ot, :],
                                                      rp[:, dc, (tc % 2) * 128:(tc % 2) * 128 + 128],
                                                      wosb[:, dc, ot * 512:(ot + 1) * 512],
                                                      start=(dc == 0), stop=(dc == 7))
                                    last.then_inc(s_wo2.bump(), 1)


def emit_qkv(ctx):
    nc = ctx.nc
    xn_fm = ctx.dtmp("xn_fm", [8, 128, TOK], BF16)
    qf_d = ctx.dtmp("qf", [2, 128, 8, SEG], BF16)
    cc1s = ctx.dtmp("cc1src", [2, 2, 128, 8, SEG], BF16)
    cc1d = ctx.dtmp("cc1dst", [NCORE, 2, 2, 128, 8, SEG], BF16)
    wqkv = ctx.din("wqkv", [D, 3 * D])
    with contextlib.ExitStack() as _es:
        w3 = _es.enter_context(nc.sbuf_tensor("qk_w", [128, 8, 3 * D], BF16))
        xnq = _es.enter_context(nc.sbuf_tensor("qk_x", [128, 2, 8, 512], BF16))
        qsb = _es.enter_context(nc.sbuf_tensor("qk_q", [128, 8, SEG], BF16))
        ksb = _es.enter_context(nc.sbuf_tensor("qk_k", [128, 8, SEG], BF16))
        vsb = _es.enter_context(nc.sbuf_tensor("qk_v", [128, 8, SEG], BF16))
        pf = _es.enter_context(nc.psum_tensor("qk_pf", [128, 2, 512], F32))
        s_lw = ctx.sem("qk_lw")
        with nc.Block() as blk:
            LW = s_lw.n
            @blk.gpsimd
            def _(gp_):
                for dc in range(8):
                    gp_.dma_start(w3[:, dc, :], wqkv.ap()[dc * 128:(dc + 1) * 128, :]).then_inc(s_lw.bump(16), 16)
                gp_.wait_ge(s_lw.h, LW + 128)
        for s in range(2):
            s_xf = ctx.sem("qk_xf"); s_fp = ctx.sem("qk_fp"); s_fe = ctx.sem("qk_fe")
            s_sp = ctx.sem("qk_sp")
            with nc.Block() as blk:
                XF, FP, FE, SP = s_xf.n, s_fp.n, s_fe.n, s_sp.n
                @blk.sync
                def _(sy):
                    for t5 in range(2):
                        sy.dma_start(
                            xnq[:, t5, :, :],
                            xn_fm.ap()[:, :, s * SEG + t5 * 512:s * SEG + (t5 + 1) * 512]
                            .rearrange("c p j -> p c j")).then_inc(s_xf.bump(16), 16)
                    sy.wait_ge(s_fe.h, FE + 48)
                    sy.dma_start(qf_d.ap()[s], qsb[:]).then_inc(s_sp.bump(16), 16)
                    sy.dma_start(cc1s.ap()[s, 0], ksb[:]).then_inc(s_sp.bump(16), 16)
                    sy.dma_start(cc1s.ap()[s, 1], vsb[:]).then_inc(s_sp.bump(16), 16)
                    sy.wait_ge(s_sp.h, SP + 48)
                @blk.tensor
                def _(t_):
                    i = 0
                    for part in range(2):
                        for t5 in range(2):
                            t_.wait_ge(s_xf.h, XF + 32)
                            for ec in range(8):
                                if i >= 2:
                                    t_.wait_ge(s_fe.h, FE + i - 1)
                                last = None
                                for dc in range(8):
                                    last = t_.matmul(pf[:, i % 2, :],
                                              w3[:, dc, part * D + ec * 128:part * D + (ec + 1) * 128],
                                              xnq[:, t5, dc, :], start=(dc == 0), stop=(dc == 7))
                                last.then_inc(s_fp.bump(), 1)
                                i += 1
                    t_.wait_ge(s_xf.h, XF + 32)
                    for tc in range(8):
                        for vh in range(2):
                            if i >= 2:
                                t_.wait_ge(s_fe.h, FE + i - 1)
                            last = None
                            for dc in range(8):
                                last = t_.matmul(pf[:, i % 2, :],
                                          xnq[:, tc // 4, dc, (tc % 4) * 128:(tc % 4 + 1) * 128],
                                          w3[:, dc, 2 * D + vh * 512:2 * D + (vh + 1) * 512],
                                          start=(dc == 0), stop=(dc == 7))
                            last.then_inc(s_fp.bump(), 1)
                            i += 1
                @blk.vector
                def _(v):
                    i = 0
                    for part in range(2):
                        dstb = (qsb, ksb)[part]
                        for t5 in range(2):
                            for ec in range(8):
                                v.wait_ge(s_fp.h, FP + i + 1)
                                v.tensor_copy(dstb[:, ec, t5 * 512:(t5 + 1) * 512],
                                              pf[:, i % 2, :]).then_inc(s_fe.bump(), 1)
                                i += 1
                    for tc in range(8):
                        for vh in range(2):
                            v.wait_ge(s_fp.h, FP + i + 1)
                            v.tensor_copy(vsb[:, tc, vh * 512:(vh + 1) * 512],
                                          pf[:, i % 2, :]).then_inc(s_fe.bump(), 1)
                            i += 1
        if SIM_SKIP_CC:
            ctx.sem("cc1").bump()
            return
        s_cc1 = ctx.sem("cc1")
        with nc.Block() as blk:
            @blk.gpsimd
            def _(gp_):
                gp_.collective_compute(
                    "AllGather", OP.bypass, replica_groups=[list(range(NCORE))],
                    ins=[ctx.dram["cc1src"].ap().opt()],
                    outs=[ctx.dram["cc1dst"].ap().opt()]).then_inc(s_cc1.bump(), 1)



def emit_attn(ctx):
    nc = ctx.nc
    g = ctx.glob
    qf_d = ctx.dtmp("qf", [2, 128, 8, SEG], BF16)
    cc1d = ctx.dtmp("cc1dst", [NCORE, 2, 2, 128, 8, SEG], BF16)
    g1_d = ctx.dtmp("g1", [128, 16, D], BF16)
    x1_d = ctx.dtmp("x1", [TOK, D], F32)
    out_d = ctx.dout("out", [TOK, D], F32)
    idb = g["idb"]
    with contextlib.ExitStack() as _es:
        qseg = _es.enter_context(nc.sbuf_tensor("at_q", [128, 8, SEG], BF16))
        kbuf = _es.enter_context(nc.sbuf_tensor("at_k", [128, 2, 8, SEG], BF16))
        vbuf = _es.enter_context(nc.sbuf_tensor("at_v", [128, 2, 8, SEG], BF16))
        Oall = _es.enter_context(nc.sbuf_tensor("at_o", [128, 8, SEG], F32))
        Mall = _es.enter_context(nc.sbuf_tensor("at_m", [128, 8], F32))
        Dall = _es.enter_context(nc.sbuf_tensor("at_d", [128, 8], F32))
        mnew = _es.enter_context(nc.sbuf_tensor("at_mn", [128, 8], F32))
        nmn = _es.enter_context(nc.sbuf_tensor("at_nn", [128, 8], F32))
        resc = _es.enter_context(nc.sbuf_tensor("at_rc", [128, 8], F32))
        mxc = _es.enter_context(nc.sbuf_tensor("at_mx", [128, 2], F32))
        psc_ = _es.enter_context(nc.sbuf_tensor("at_ps", [128, 2], F32))
        P = _es.enter_context(nc.sbuf_tensor("at_P", [128, 2, SEG], BF16))
        PT = _es.enter_context(nc.sbuf_tensor("at_PT", [128, 8, 128], BF16))
        osb = _es.enter_context(nc.sbuf_tensor("at_os", [128, 2, D], F32))
        x1t = _es.enter_context(nc.sbuf_tensor("at_x1", [128, 2, D], F32))
        g2t = _es.enter_context(nc.sbuf_tensor("at_g2", [128, 2, D], F32))
        scr = _es.enter_context(nc.sbuf_tensor("at_sc", [128, D], F32))
        psc = _es.enter_context(nc.psum_tensor("at_psc", [128, 2, SEG], F32))
        ppt = _es.enter_context(nc.psum_tensor("at_ppt", [128, 8, 128], BF16))
        pav = _es.enter_context(nc.psum_tensor("at_pav", [128, 2, 512], F32))
        for s in range(2):
            s_kv = [ctx.sem("at_kvA"), ctx.sem("at_kvB")]
            s_q = ctx.sem("at_q"); s_sc = ctx.sem("at_sc")
            s_mx = ctx.sem("at_mx"); s_ax = ctx.sem("at_ax"); s_msk = ctx.sem("at_mk")
            s_pt = ctx.sem("at_pt"); s_pte = ctx.sem("at_pe"); s_av = ctx.sem("at_av")
            s_ou = ctx.sem("at_ou"); s_rs = ctx.sem("at_rs")
            s_xo = [ctx.sem("at_xoA"), ctx.sem("at_xoB")]
            s_x1 = [ctx.sem("at_x1A"), ctx.sem("at_x1B")]
            s_gt = [ctx.sem("at_gtA"), ctx.sem("at_gtB")]
            b = s
            N = 64
            with nc.Block() as blk:
                Q0, SC, MX, AX0, MSK = s_q.n, s_sc.n, s_mx.n, s_ax.n, s_msk.n
                PT0, PTE, AV, OU, RS = s_pt.n, s_pte.n, s_av.n, s_ou.n, s_rs.n
                KV = [x.n for x in s_kv]
                XO = [x.n for x in s_xo]
                X1 = [x.n for x in s_x1]
                GT = [x.n for x in s_gt]
                @blk.sync
                def _(sy):
                    if not SIM_SKIP_CC:
                        sy.wait_ge(ctx.sem("cc1").h, 1)
                    sy.dma_start(qseg[:], qf_d.ap()[s]).then_inc(s_q.bump(16), 16)
                    for t in range(8):
                        qq = 8 * b + t
                        rk = min(qq, 15 - qq)
                        sp = 0 if rk == qq else 1
                        if t >= 2:
                            sy.wait_ge(s_sc.h, SC + 8 * (t - 1))
                            sy.wait_ge(s_av.h, AV + 8 * (t - 1))
                        sy.dma_start(kbuf[:, t % 2, :, :],
                                     cc1d.ap()[rk, sp, 0]).then_inc(s_kv[t % 2].bump(16), 16)
                        sy.dma_start(vbuf[:, t % 2, :, :],
                                     cc1d.ap()[rk, sp, 1]).then_inc(s_kv[t % 2].bump(16), 16)
                    for k in range(8):
                        if k >= 2:
                            sy.wait_ge(s_rs.h, RS + k - 1)
                        sy.dma_start(x1t[:, k % 2, :],
                                     x1_d.ap()[s * SEG + k * 128:s * SEG + (k + 1) * 128, :]
                                     ).then_inc(s_x1[k % 2].bump(16), 16)
                        sy.wait_ge(s_rs.h, RS + k + 1)
                        sy.dma_start(out_d.ap()[s * SEG + k * 128:s * SEG + (k + 1) * 128, :],
                                     osb[:, k % 2, :]).then_inc(s_xo[k % 2].bump(16), 16)
                    sy.wait_ge(s_xo[0].h, XO[0] + 64)
                    sy.wait_ge(s_xo[1].h, XO[1] + 64)
                @blk.gpsimd
                def _(gp_):
                    for k in range(8):
                        if k >= 2:
                            gp_.wait_ge(s_rs.h, RS + k - 1)
                        gp_.dma_start(g2t[:, k % 2, :], g1_d.ap()[:, s * 8 + k, :]).then_inc(s_gt[k % 2].bump(16), 16)
                @blk.tensor
                def _(t_):
                    def scores(i):
                        t, k = divmod(i, 8)
                        t_.wait_ge(s_kv[t % 2].h, KV[t % 2] + 32 * (t // 2 + 1))
                        if i == 0:
                            t_.wait_ge(s_q.h, Q0 + 16)
                        if i >= 2:
                            t_.wait_ge(s_ax.h, AX0 + i - 1)
                        last = None
                        for ch in range(2):
                            for dc in range(8):
                                last = t_.matmul(psc[:, i % 2, ch * 512:(ch + 1) * 512],
                                          qseg[:, dc, k * 128:(k + 1) * 128],
                                          kbuf[:, (i // 8) % 2, dc, ch * 512:(ch + 1) * 512],
                                          start=(dc == 0), stop=(dc == 7))
                        last.then_inc(s_sc.bump(), 1)
                    scores(0)
                    scores(1)
                    for i in range(N):
                        if i + 2 < N:
                            scores(i + 2)
                        t_.wait_ge(s_msk.h, MSK + i + 1)
                        if i >= 1:
                            t_.wait_ge(s_pte.h, PTE + i)
                        last = None
                        for jb in range(8):
                            last = t_.transpose(ppt[:, jb, :], P[:, i % 2, jb * 128:(jb + 1) * 128], idb[:])
                        last.then_inc(s_pt.bump(), 1)
                        t_.wait_ge(s_pte.h, PTE + i + 1)
                        if i >= 1:
                            t_.wait_ge(s_ou.h, OU + i)
                        last = None
                        for ot in range(2):
                            for jb in range(8):
                                last = t_.matmul(pav[:, ot, :], PT[:, jb, :],
                                          vbuf[:, (i // 8) % 2, jb, ot * 512:(ot + 1) * 512],
                                          start=(jb == 0), stop=(jb == 7))
                        last.then_inc(s_av.bump(), 1)
                @blk.vector
                def _(v):
                    for k in range(8):
                        v.memset(Oall[:, k, :], 0.0)
                    v.memset(Mall[:], NEGBIG)
                    v.memset(Dall[:], 0.0)
                    def mxblock(i):
                        k = i % 8
                        v.wait_ge(s_sc.h, SC + i + 1)
                        if i >= 8:
                            v.wait_ge(s_ax.h, AX0 + i - 7)
                        v.reduce_max(mxc[:, i % 2:i % 2 + 1], psc[:, i % 2, :], axis=AX)
                        v.drain()
                        v.tensor_scalar(mxc[:, i % 2:i % 2 + 1], mxc[:, i % 2:i % 2 + 1],
                                        1.0 / 32.0, None, OP.mult)
                        v.drain()
                        v.tensor_tensor(mnew[:, k:k + 1], Mall[:, k:k + 1],
                                        mxc[:, i % 2:i % 2 + 1], OP.max)
                        v.drain()
                        v.tensor_scalar(nmn[:, k:k + 1], mnew[:, k:k + 1], -1.0,
                                        None, OP.mult).then_inc(s_mx.bump(), 1)
                    mxblock(0)
                    for i in range(N):
                        t, k = divmod(i, 8)
                        if i + 1 < N:
                            mxblock(i + 1)
                        v.wait_ge(s_ax.h, AX0 + i + 1)
                        v.scalar_tensor_tensor(P[:, i % 2, :], g["iota_pj"][:],
                                               g["thresh"][:, s * 64 + k * 8 + t:s * 64 + k * 8 + t + 1],
                                               P[:, i % 2, :], OP.is_ge, OP.mult)
                        v.drain()
                        v.reduce_sum(psc_[:, i % 2:i % 2 + 1], P[:, i % 2, :], axis=AX)
                        v.drain()
                        v.scalar_tensor_tensor(Dall[:, k:k + 1], Dall[:, k:k + 1],
                                               resc[:, k:k + 1], psc_[:, i % 2:i % 2 + 1],
                                               OP.mult, OP.add).then_inc(s_msk.bump(), 1)
                        v.wait_ge(s_av.h, AV + i + 1)
                        for ot in range(2):
                            v.scalar_tensor_tensor(Oall[:, k, ot * 512:(ot + 1) * 512],
                                                   Oall[:, k, ot * 512:(ot + 1) * 512],
                                                   resc[:, k:k + 1], pav[:, ot, :],
                                                   OP.mult, OP.add)
                        v.drain()
                        v.tensor_copy(Mall[:, k:k + 1], mnew[:, k:k + 1]).then_inc(s_ou.bump(), 1)
                    for k in range(8):
                        v.wait_ge(s_gt[k % 2].h, GT[k % 2] + 16 * (k // 2 + 1))
                        v.wait_ge(s_x1[k % 2].h, X1[k % 2] + 16 * (k // 2 + 1))
                        if k >= 2:
                            v.wait_ge(s_xo[k % 2].h, XO[k % 2] + 16 * (k // 2))
                        v.reciprocal(mxc[:, 0:1], Dall[:, k:k + 1])
                        v.drain()
                        v.tensor_scalar(scr[:], Oall[:, k, :], mxc[:, 0:1], None, OP.mult)
                        v.drain()
                        v.tensor_tensor(scr[:], scr[:], g2t[:, k % 2, :], OP.mult)
                        v.drain()
                        v.tensor_tensor(osb[:, k % 2, :], scr[:], x1t[:, k % 2, :],
                                        OP.add).then_inc(s_rs.bump(), 1)
                @blk.scalar
                def _(sc):
                    for i in range(N):
                        t, k = divmod(i, 8)
                        sc.wait_ge(s_mx.h, MX + i + 1)
                        sc.activation(resc[:, k:k + 1], Mall[:, k:k + 1], AF.Exp,
                                      bias=nmn[:, k:k + 1])
                        sc.activation(P[:, i % 2, :], psc[:, i % 2, :], AF.Exp, scale=1.0 / 32.0,
                                      bias=nmn[:, k:k + 1]).then_inc(s_ax.bump(), 1)
                        if i >= 1:
                            sc.wait_ge(s_pt.h, PT0 + i)
                            if i >= 2:
                                sc.wait_ge(s_av.h, AV + i - 1)
                            sc.copy(PT[:], ppt[:]).then_inc(s_pte.bump(), 1)
                    sc.wait_ge(s_pt.h, PT0 + N)
                    sc.wait_ge(s_av.h, AV + N - 1)
                    sc.copy(PT[:], ppt[:]).then_inc(s_pte.bump(), 1)



_BUILD_CACHE = {}


def build_program(upto="attn", taps=()):
    key = (upto, tuple(sorted(taps)))
    if key in _BUILD_CACHE:
        return _BUILD_CACHE[key]
    nc = bass.Bass("TRN2", target_bir_lowering=False)
    es = contextlib.ExitStack()
    ctx = Ctx(nc, es, taps)
    with es:
        x_in = ctx.din("x", [TOK, D])
        emit_setup(ctx)
        phases = ["ln1", "mixa", "moe0", "mixb", "ln2", "qkv", "moe1", "attn"]
        stop = phases.index({"mixp": "mixa", "mixn": "mixa", "mixb1": "mixb", "mixb2": "mixb", "mixb3": "mixb", "mixb4": "mixb", "mixb5": "mixb", "mixb6": "mixb"}.get(upto, upto))
        xn_fm = ctx.dtmp("xn_fm", [8, 128, TOK], BF16)
        emit_ln(ctx, x_in, xn_fm)
        if stop >= 1:
            emit_mixer_a(ctx, parts={"mixp": "prod", "mixn": "nocc"}.get(upto, "all"))
        if stop >= 2:
            emit_moe(ctx, 0)
        if stop >= 3:
            emit_mixer_b(ctx, parts={"mixb1": "m1", "mixb2": "m2", "mixb3": "m3rg", "mixb4": "m3ch", "mixb5": "m3c1", "mixb6": "m3c2"}.get(upto, "all"))
        if stop >= 4:
            emit_ln(ctx, ctx.dtmp("x1", [TOK, D], F32), xn_fm)
        if stop >= 5:
            emit_qkv(ctx)
        if stop >= 6:
            emit_moe(ctx, 1)
        if stop >= 7:
            emit_attn(ctx)
        if 1 <= stop < 3 and upto not in ("mixp", "mixn") and not SIM_SKIP_CC:
            with nc.Block() as blk:
                @blk.gpsimd
                def _(gp_):
                    gp_.wait_ge(ctx.sem("cc0").h, 1)
        if 5 <= stop < 7 and not SIM_SKIP_CC:
            with nc.Block() as blk:
                @blk.gpsimd
                def _(gp_):
                    gp_.wait_ge(ctx.sem("cc1").h, 1)
    innames = [k for k, v in ctx.kinds.items() if v == "in"]
    _BUILD_CACHE[key] = (nc, innames)
    return nc, innames


def kernel(**inputs):
    nc, innames = build_program("attn")
    in_maps = prep_in_maps(inputs)
    in_maps = [{k: m[k] for k in innames} for m in in_maps]
    res = run_bass_kernel_spmd(nc, in_maps, core_ids=list(range(NCORE)))
    return scatter_out(res.results)


# version 06 (final)
# speedup vs baseline: 1.1052x; 1.0286x over previous
import contextlib
import numpy as np
import concourse.bass as bass
import concourse.mybir as mybir
from concourse.bass_utils import run_bass_kernel_spmd

F32 = mybir.dt.float32
BF16 = mybir.dt.bfloat16
AF = mybir.ActivationFunctionType
OP = mybir.AluOpType
AX = mybir.AxisListType.X

D = 1024
TOK = 2048
SEG = 1024
H, HD = 16, 64
CH = 32
NCHUNK = SEG // CH
NCORE = 8
NEGBIG = -1.0e30
SIM_SKIP_CC = False
DEBUG_NCH = 8


def _prep_consts():
    c = {}
    c["iden"] = np.eye(128, dtype=np.float32)
    m = np.triu(np.ones((32, 32), np.float32))
    c["mask64"] = np.ascontiguousarray(np.tile(np.tile(m, (1, 16)), (2, 1)))
    c["ivec"] = np.ascontiguousarray(
        np.tile(np.arange(1, CH + 1, dtype=np.float32)[None, :], (128, 1)))
    c["revseg"] = (SEG - 1.0 - np.arange(SEG, dtype=np.float32))[None, :]
    c["w31col2"] = (31.0 - (np.arange(64) % 32)).astype(np.float32)[None, :]
    c["hsel"] = np.ascontiguousarray(np.repeat(np.eye(16, dtype=np.float32), HD, axis=1))
    c["iota_pj"] = np.ascontiguousarray(
        np.arange(128, dtype=np.float32)[:, None] - np.arange(1024, dtype=np.float32)[None, :])
    c["ones_1x"] = np.ones((1, 128), np.float32)
    return c


def _prep_core_consts(core):
    qA, qB = core, 15 - core
    segs = [(qA // 8, qA % 8), (qB // 8, qB % 8)]
    d = {}
    expo = np.full((2, 16), 1.0e6, np.float32)
    for s, (b, kpos) in enumerate(segs):
        for rank in range(8):
            for sp in range(2):
                q = rank if sp == 0 else 15 - rank
                bp, kp = q // 8, q % 8
                if bp == b and kp < kpos:
                    expo[s, 2 * rank + sp] = SEG * (kpos - 1 - kp)
    d["sin_nexpo"] = np.ascontiguousarray(np.tile(-expo.reshape(1, 32), (128, 1)))
    th = np.zeros((2, 8, 8), np.float32)
    for s, (b, kpos) in enumerate(segs):
        for k in range(8):
            g0 = kpos * SEG + k * 128
            for t in range(8):
                th[s, k, t] = t * 1024 - g0
    d["attn_thresh"] = np.ascontiguousarray(np.tile(th.reshape(1, 128), (128, 1)))
    return d


def _prep_shared(inputs):
    g = lambda k: np.asarray(inputs[k], dtype=np.float32)
    s = {}
    s["wk"] = np.ascontiguousarray(g("L0_attn_Wk").T)
    s["wv"] = np.ascontiguousarray(g("L0_attn_Wv").T)
    s["wr"] = np.ascontiguousarray(g("L0_attn_Wr").T)
    s["wgm"] = np.ascontiguousarray(g("L0_attn_Wg").T)
    s["wo"] = np.ascontiguousarray(g("L0_attn_Wo").T)
    tm = g("L0_attn_time_mix").reshape(D)
    s["tm_col"] = np.ascontiguousarray(tm.reshape(8, 128).T)
    td = g("L0_attn_time_decay")
    s["td_col"] = np.ascontiguousarray(td.reshape(D).reshape(8, 128).T)
    s["td_row"] = np.ascontiguousarray(td.reshape(1, D))
    s["td_nat"] = np.ascontiguousarray(td)
    tf = g("L0_attn_time_first")
    s["tf_col"] = np.ascontiguousarray(tf.reshape(D).reshape(8, 128).T)
    s["m0_w1t"] = np.ascontiguousarray(g("L0_ffn_W1").transpose(0, 2, 1))
    s["m0_w2t"] = np.ascontiguousarray(g("L0_ffn_W2").transpose(0, 2, 1))
    s["m0_wg"] = np.ascontiguousarray(g("L0_ffn_Wg").T)
    s["m0_bg"] = np.ascontiguousarray(g("L0_ffn_bg").reshape(1, 8))
    s["m0_b1"] = np.ascontiguousarray(g("L0_ffn_b1").reshape(8, 16, 128).transpose(0, 2, 1))
    s["m0_b2"] = np.ascontiguousarray(g("L0_ffn_b2"))
    s["wqkv"] = np.ascontiguousarray(g("L1_attn_W_qkv"))
    s["rw"] = np.ascontiguousarray(g("L1_ffn_router_w"))
    s["m1_w1"] = np.ascontiguousarray(g("L1_ffn_w1"))
    s["m1_w2"] = np.ascontiguousarray(g("L1_ffn_w2"))
    s["m1_b1"] = np.ascontiguousarray(g("L1_ffn_b1").reshape(16, 16, 128).transpose(0, 2, 1))
    s["m1_b2"] = np.ascontiguousarray(g("L1_ffn_b2"))
    return s


def prep_in_maps(inputs):
    x = np.asarray(inputs["L0_attn_x"], dtype=np.float32)
    shared = _prep_shared(inputs)
    consts = _prep_consts()
    in_maps = []
    for c in range(NCORE):
        qA, qB = c, 15 - c
        shard = np.concatenate(
            [x[qA // 8, (qA % 8) * SEG:(qA % 8 + 1) * SEG],
             x[qB // 8, (qB % 8) * SEG:(qB % 8 + 1) * SEG]], axis=0)
        m = {"x": np.ascontiguousarray(shard)}
        m.update(shared)
        m.update(consts)
        m.update(_prep_core_consts(c))
        in_maps.append(m)
    return in_maps


def scatter_out(results):
    out = np.empty((2, 8192, D), dtype=np.float32)
    for c in range(NCORE):
        y = np.asarray(results[c]["out"])
        qA, qB = c, 15 - c
        out[qA // 8, (qA % 8) * SEG:(qA % 8 + 1) * SEG] = y[:SEG]
        out[qB // 8, (qB % 8) * SEG:(qB % 8 + 1) * SEG] = y[SEG:]
    return out



class SemCt:

    def __init__(self, es, nc, name):
        self.h = es.enter_context(nc.semaphore(name))
        self.n = 0

    def bump(self, k=1):
        self.n += k
        return self.h


class Ctx:
    def __init__(self, nc, es, taps=()):
        self.nc = nc
        self.es = es
        self.dram = {}
        self.kinds = {}
        self.taps = set(taps)
        self.uid = 0
        self.glob = {}
        self.sems = {}

    def name(self, stem):
        self.uid += 1
        return f"{stem}{self.uid}"

    def sem(self, key):
        if key not in self.sems:
            self.sems[key] = SemCt(self.es, self.nc, "s_" + key)
        return self.sems[key]

    def din(self, key, shape, dtype=F32):
        if key not in self.dram:
            self.dram[key] = self.nc.dram_tensor(key, shape, dtype, kind="ExternalInput")
            self.kinds[key] = "in"
        return self.dram[key]

    def dout(self, key, shape, dtype=F32):
        if key not in self.dram:
            self.dram[key] = self.nc.dram_tensor(key, shape, dtype, kind="ExternalOutput")
            self.kinds[key] = "out"
        return self.dram[key]

    def dtmp(self, key, shape, dtype):
        if key not in self.dram:
            if key in self.taps:
                self.dram[key] = self.nc.dram_tensor(key, shape, dtype, kind="ExternalOutput")
                self.kinds[key] = "out"
            else:
                self.dram[key] = self.nc.dram_tensor(key, shape, dtype)
                self.kinds[key] = "tmp"
        return self.dram[key]



def emit_setup(ctx):
    nc = ctx.nc
    g = ctx.glob
    mk = lambda n, sh, dt: ctx.es.enter_context(nc.sbuf_tensor(n, sh, dt))
    g["idb"] = mk("g_idb", [128, 128], BF16)
    g["td_col"] = mk("g_tdcol", [128, 8], F32)
    g["tm_col"] = mk("g_tmcol", [128, 8], F32)
    g["tf_col"] = mk("g_tfcol", [128, 8], F32)
    g["td_nat"] = mk("g_tdnat", [16, 64], F32)
    g["dhat"] = mk("g_dhat", [128, 8, CH], BF16)
    g["dneg"] = mk("g_dneg", [128, 8, CH], BF16)
    g["tfmap"] = mk("g_tfmap", [128, 8, CH], F32)
    g["dc32"] = mk("g_dc32", [128, D], F32)
    g["dbar32"] = mk("g_dbar32", [64, D], BF16)
    g["mask64"] = mk("g_mask", [64, 512], BF16)
    g["iota_pj"] = mk("g_iota", [128, 1024], F32)
    g["thresh"] = mk("g_thr", [128, 128], F32)
    g["nexpo"] = mk("g_nexpo", [128, 32], F32)
    g["hsel"] = mk("g_hsel", [16, D], F32)
    g["ones1b"] = mk("g_ones1b", [1, 128], BF16)
    g["zrow"] = mk("g_zrow", [1, 512], BF16)

    with contextlib.ExitStack() as _es:
        idf = _es.enter_context(nc.sbuf_tensor("su_idf", [128, 128], F32))
        ivec = _es.enter_context(nc.sbuf_tensor("su_ivec", [128, CH], F32))
        tmp = _es.enter_context(nc.sbuf_tensor("su_t", [128, D], F32))
        mskf = _es.enter_context(nc.sbuf_tensor("su_m", [64, 512], F32))
        w31 = _es.enter_context(nc.sbuf_tensor("su_w31", [1, 64], F32))
        tdrow_l = _es.enter_context(nc.sbuf_tensor("su_tdrow", [1, D], F32))
        ones1 = _es.enter_context(nc.sbuf_tensor("su_ones", [1, 128], F32))
        ps = _es.enter_context(nc.psum_tensor("su_ps", [128, D], F32))
        pb = _es.enter_context(nc.psum_tensor("su_pb", [128, D], F32))
        s_ld = _es.enter_context(nc.semaphore("su_ld"))
        w31_host = ctx.din("w31col2", [1, 64])
        with nc.Block() as blk:
            @blk.sync
            def _(sy):
                n = [0]
                def ld(dst, key, shape):
                    sy.dma_start(dst, ctx.din(key, shape).ap()).then_inc(s_ld, 16)
                    n[0] += 16
                ld(idf[:], "iden", [128, 128])
                ld(g["td_col"][:], "td_col", [128, 8])
                ld(g["tm_col"][:], "tm_col", [128, 8])
                ld(g["tf_col"][:], "tf_col", [128, 8])
                ld(g["td_nat"][:], "td_nat", [16, 64])
                ld(tdrow_l[:], "td_row", [1, D])
                ld(g["iota_pj"][:], "iota_pj", [128, 1024])
                ld(g["thresh"][:], "attn_thresh", [128, 128])
                ld(g["nexpo"][:], "sin_nexpo", [128, 32])
                ld(g["hsel"][:], "hsel", [16, D])
                ld(ivec[:], "ivec", [128, CH])
                ld(mskf[:], "mask64", [64, 512])
                ld(w31[:], "w31col2", [1, 64])
                ld(ones1[:], "ones_1x", [1, 128])
                sy.wait_ge(s_ld, n[0])
        with nc.Block() as blk:
            @blk.vector
            def _(v):
                v.memset(g["zrow"][:], 0.0)
                v.tensor_copy(g["idb"][:], idf[:])
                v.tensor_copy(g["ones1b"][:], ones1[:])
                v.tensor_copy(g["mask64"][:], mskf[:])
                for c in range(8):
                    v.tensor_scalar(tmp[:, c * CH:(c + 1) * CH], ivec[:],
                                    g["td_col"][:, c:c + 1], None, OP.mult)
                    v.tensor_scalar(g["tfmap"][:, c, :], ivec[:],
                                    0.0, g["tf_col"][:, c:c + 1], OP.mult, OP.add)
        with nc.Block() as blk:
            @blk.scalar
            def _(sc):
                for c in range(8):
                    sc.activation(g["dhat"][:, c, :], tmp[:, c * CH:(c + 1) * CH],
                                  AF.Exp, scale=-1.0)
                    sc.activation(g["dneg"][:, c, :], tmp[:, c * CH:(c + 1) * CH],
                                  AF.Exp, scale=1.0)
                sc.drain()
                sc.activation(tmp[0:16, 0:64], g["td_nat"][:], AF.Exp, scale=-32.0)
        with nc.Block() as blk:
            @blk.tensor
            def _(t):
                for hh in (0, 64):
                    for cc in (0, 512):
                        t.matmul(ps[hh:hh + 64, cc:cc + 512], tmp[0:16, 0:64],
                                 g["hsel"][:, cc:cc + 512], start=True, stop=True)
                for cc in (0, 512):
                    t.matmul(pb[0:64, cc:cc + 512], w31[:], tdrow_l[:, cc:cc + 512],
                             start=True, stop=True)
        with nc.Block() as blk:
            @blk.vector
            def _(v):
                v.tensor_copy(g["dc32"][:], ps[:])
            @blk.scalar
            def _(sc):
                sc.activation(g["dbar32"][:], pb[0:64, :], AF.Exp, scale=-1.0)



def emit_ln(ctx, src_hdl, dst_hdl):
    nc = ctx.nc
    idb = ctx.glob["idb"]
    with contextlib.ExitStack() as _es:
        xs = _es.enter_context(nc.sbuf_tensor(ctx.name("lx"), [128, 16, D], F32))
        sq = _es.enter_context(nc.sbuf_tensor(ctx.name("lq"), [128, D], F32))
        s1 = _es.enter_context(nc.sbuf_tensor(ctx.name("ls1"), [128, 16], F32))
        s2 = _es.enter_context(nc.sbuf_tensor(ctx.name("ls2"), [128, 16], F32))
        nmu = _es.enter_context(nc.sbuf_tensor(ctx.name("lmu"), [128, 16], F32))
        var = _es.enter_context(nc.sbuf_tensor(ctx.name("lva"), [128, 16], F32))
        lnv = _es.enter_context(nc.sbuf_tensor(ctx.name("llv"), [128, 16], F32))
        epsb = _es.enter_context(nc.sbuf_tensor(ctx.name("lep"), [128, 1], F32))
        rstd = _es.enter_context(nc.sbuf_tensor(ctx.name("lrs"), [128, 16], F32))
        xn = _es.enter_context(nc.sbuf_tensor(ctx.name("lxn"), [128, D], BF16))
        xnT = _es.enter_context(nc.sbuf_tensor(ctx.name("lxT"), [128, 2, 8, 128], BF16))
        psT = _es.enter_context(nc.psum_tensor(ctx.name("lp"), [128, 8, 128], BF16))
        s_ld = ctx.sem("ln_ld"); s_v = ctx.sem("ln_v")
        s_pe = ctx.sem("ln_pe"); s_ev = ctx.sem("ln_ev")
        s_o2 = [ctx.sem("ln_oA"), ctx.sem("ln_oB")]
        L, V, P, E = s_ld.n, s_v.n, s_pe.n, s_ev.n
        O2 = [x.n for x in s_o2]
        src = src_hdl.ap()
        dst = dst_hdl.ap()
        with nc.Block() as blk:
            @blk.sync
            def _(sy):
                for t in range(16):
                    sy.dma_start(xs[:, t, :], src[t * 128:(t + 1) * 128, :]).then_inc(s_ld.bump(16), 16)
            @blk.vector
            def _(v):
                v.memset(epsb[:], 1e-5)
                v.wait_ge(s_ld.h, L + 256)
                for t in range(16):
                    v.tensor_tensor(sq[:], xs[:, t, :], xs[:, t, :], OP.mult)
                    v.drain()
                    v.reduce_sum(s1[:, t:t + 1], xs[:, t, :], axis=AX)
                    v.reduce_sum(s2[:, t:t + 1], sq[:], axis=AX)
                    v.drain()
                v.tensor_scalar(nmu[:], s1[:], -1.0 / D, None, OP.mult)
                v.tensor_scalar(var[:], s2[:], 1.0 / D, None, OP.mult)
                v.drain()
                v.tensor_tensor(s2[:], nmu[:], nmu[:], OP.mult)
                v.drain()
                v.tensor_tensor(var[:], var[:], s2[:], OP.subtract).then_inc(s_v.bump(), 1)
            @blk.scalar
            def _(sc):
                sc.wait_ge(s_v.h, V + 1)
                sc.activation(lnv[:], var[:], AF.Ln, bias=epsb[:])
                sc.drain()
                sc.activation(rstd[:], lnv[:], AF.Exp, scale=-0.5).then_inc(s_v.bump(), 1)
        with nc.Block() as blk:
            @blk.vector
            def _(v):
                for t in range(16):
                    v.tensor_scalar(xn[:], xs[:, t, :], nmu[:, t:t + 1],
                                    rstd[:, t:t + 1], OP.add, OP.mult).then_inc(s_v.bump(), 1)
                    v.wait_ge(s_pe.h, P + 8 * (t + 1))
                    if t >= 2:
                        v.wait_ge(s_o2[t % 2].h, O2[t % 2] + 16 * (t // 2))
                    v.tensor_copy(xnT[:, t % 2, :, :], psT[:]).then_inc(s_ev.bump(), 1)
            @blk.tensor
            def _(t_):
                for t in range(16):
                    t_.wait_ge(s_v.h, V + 2 + t + 1)
                    if t > 0:
                        t_.wait_ge(s_ev.h, E + t)
                    for cc in range(8):
                        t_.transpose(psT[:, cc, :], xn[:, cc * 128:(cc + 1) * 128],
                                     idb[:]).then_inc(s_pe.bump(), 1)
            @blk.sync
            def _(sy):
                for t in range(16):
                    sy.wait_ge(s_ev.h, E + t + 1)
                    sy.dma_start(dst[:, :, t * 128:(t + 1) * 128].rearrange("c p j -> p c j"),
                                 xnT[:, t % 2, :, :]).then_inc(s_o2[t % 2].bump(16), 16)
                sy.wait_ge(s_o2[0].h, O2[0] + 128)
                sy.wait_ge(s_o2[1].h, O2[1] + 128)



def emit_mixer_a(ctx, parts="all"):
    nc = ctx.nc
    g = ctx.glob
    xn_fm = ctx.dtmp("xn_fm", [8, 128, TOK], BF16)
    ktm_d = ctx.dtmp("ktm", [2, 64, 16, D], BF16)
    vtm_d = ctx.dtmp("vtm", [2, 64, 16, D], BF16)
    cc0s = ctx.dtmp("cc0src", [128, D], F32)
    cc0d = ctx.dtmp("cc0dst", [NCORE * 128, D], F32)
    with contextlib.ExitStack() as _es:
        wsb = _es.enter_context(nc.sbuf_tensor("ma_w", [128, 8, 2048], BF16))
        ktm = _es.enter_context(nc.sbuf_tensor("ma_ktm", [64, 16, D], BF16))
        vtm = _es.enter_context(nc.sbuf_tensor("ma_vtm", [64, 16, D], BF16))
        xnt = _es.enter_context(nc.sbuf_tensor("ma_xn", [128, 2, 8, 128], BF16))
        dbu = _es.enter_context(nc.sbuf_tensor("ma_dbu", [64, D], BF16))
        tdrow_a = _es.enter_context(nc.sbuf_tensor("ma_tdrow", [1, D], F32))
        rev_a = _es.enter_context(nc.sbuf_tensor("ma_rev", [1, SEG], F32))
        kbu = _es.enter_context(nc.sbuf_tensor("ma_kbu", [64, D], BF16))
        usb = _es.enter_context(nc.sbuf_tensor("ma_u", [64, D], F32))
        pj0 = _es.enter_context(nc.psum_tensor("ma_p0", [128, D], F32))
        pj1 = _es.enter_context(nc.psum_tensor("ma_p1", [128, D], F32))
        pdb = _es.enter_context(nc.psum_tensor("ma_pd", [128, D], F32))
        pu = _es.enter_context(nc.psum_tensor("ma_pu", [128, D], F32))
        wk = ctx.din("wk", [D, D]); wv = ctx.din("wv", [D, D])
        s_lw = ctx.sem("ma_lw")
        with nc.Block() as blk:
            LW = s_lw.n
            @blk.gpsimd
            def _(gp):
                gp.dma_start(tdrow_a[:], ctx.din("td_row", [1, D]).ap()).then_inc(s_lw.bump(16), 16)
                gp.dma_start(rev_a[:], ctx.din("revseg", [1, SEG]).ap()).then_inc(s_lw.bump(16), 16)
                for dc in range(8):
                    gp.dma_start(wsb[:, dc, 0:D], wk.ap()[dc * 128:(dc + 1) * 128, :]).then_inc(s_lw.bump(16), 16)
                    gp.dma_start(wsb[:, dc, D:2 * D], wv.ap()[dc * 128:(dc + 1) * 128, :]).then_inc(s_lw.bump(16), 16)
            @blk.vector
            def _(v):
                v.wait_ge(s_lw.h, LW + 288)
                for dc in range(8):
                    v.tensor_scalar(wsb[:, dc, :], wsb[:, dc, :],
                                    g["tm_col"][:, dc:dc + 1], None, OP.mult)
        for s in range(2):
            s_lx = [ctx.sem("ma_lxA"), ctx.sem("ma_lxB")]
            s_pe = ctx.sem("ma_pe"); s_ev = ctx.sem("ma_ev")
            with nc.Block() as blk:
                LX = [x.n for x in s_lx]
                PE, EV = s_pe.n, s_ev.n
                @blk.sync
                def _(sy):
                    for tc in range(8):
                        if tc >= 2:
                            sy.wait_ge(s_pe.h, PE + 4 * (tc - 1))
                        sy.dma_start(
                            xnt[:, tc % 2, :, :],
                            xn_fm.ap()[:, :, s * SEG + tc * 128: s * SEG + (tc + 1) * 128]
                            .rearrange("c p j -> p c j")).then_inc(s_lx[tc % 2].bump(16), 16)
                @blk.tensor
                def _(t_):
                    i = 0
                    for tc in range(8):
                        t_.wait_ge(s_lx[tc % 2].h, LX[tc % 2] + 16 * (tc // 2 + 1))
                        for w in range(2):
                            for jg in range(2):
                                ps = (pj0, pj1)[i % 2]
                                if i >= 2:
                                    t_.wait_ge(s_ev.h, EV + i - 1)
                                last = None
                                for cc2 in range(2):
                                    for dc in range(8):
                                        last = t_.matmul(ps[0:64, cc2 * 512:(cc2 + 1) * 512],
                                                  xnt[:, tc % 2, dc, jg * 64:(jg + 1) * 64],
                                                  wsb[:, dc, w * D + cc2 * 512:w * D + (cc2 + 1) * 512],
                                                  start=(dc == 0), stop=(dc == 7))
                                last.then_inc(s_pe.bump(), 1)
                                i += 1
                @blk.vector
                def _(v):
                    i = 0
                    for tc in range(8):
                        for w in range(2):
                            dstb = (ktm, vtm)[w]
                            for jg in range(2):
                                ps = (pj0, pj1)[i % 2]
                                v.wait_ge(s_pe.h, PE + i + 1)
                                v.tensor_copy(dstb[:, 2 * tc + jg, :], ps[0:64, :]).then_inc(s_ev.bump(), 1)
                                i += 1
            if parts == "prod":
                continue
            s_db = ctx.sem("ma_db"); s_ex = ctx.sem("ma_ex"); s_kb = ctx.sem("ma_kb")
            s_up = ctx.sem("ma_up"); s_sp = ctx.sem("ma_sp")
            with nc.Block() as blk:
                DB, EX, KB, UP, SP = s_db.n, s_ex.n, s_kb.n, s_up.n, s_sp.n
                @blk.tensor
                def _(t_):
                    for gr in range(16):
                        if gr > 0:
                            t_.wait_ge(s_ex.h, EX + gr)
                        last = None
                        for cc2 in range(2):
                            last = t_.matmul(pdb[0:64, cc2 * 512:(cc2 + 1) * 512],
                                      rev_a[:, 64 * gr:64 * gr + 64],
                                      tdrow_a[:, cc2 * 512:(cc2 + 1) * 512],
                                      start=True, stop=True)
                        last.then_inc(s_db.bump(), 1)
                        t_.wait_ge(s_kb.h, KB + gr + 1)
                        last = None
                        for n2 in range(2):
                            n = 2 * gr + n2
                            base = 32 * n2
                            for h in range(H):
                                last = t_.matmul(pu[0:64, h * 64:(h + 1) * 64],
                                          kbu[base:base + 32, h * 64:(h + 1) * 64],
                                          vtm[base:base + 32, gr, h * 64:(h + 1) * 64],
                                          start=(n == 0 and h in (0, 8)),
                                          stop=(n == NCHUNK - 1 and h in (7, 15)))
                        last.then_inc(s_up.bump(), 1)
                @blk.scalar
                def _(sc):
                    for gr in range(16):
                        sc.wait_ge(s_db.h, DB + gr + 1)
                        if gr > 0:
                            sc.wait_ge(s_kb.h, KB + gr)
                        sc.activation(dbu[:], pdb[0:64, :], AF.Exp, scale=-1.0).then_inc(s_ex.bump(), 1)
                @blk.vector
                def _(v):
                    for gr in range(16):
                        v.wait_ge(s_ex.h, EX + gr + 1)
                        if gr > 0:
                            v.wait_ge(s_up.h, UP + gr)
                        v.tensor_tensor(kbu[:], ktm[:, gr, :], dbu[:], OP.mult).then_inc(s_kb.bump(), 1)
                    v.wait_ge(s_up.h, UP + 16)
                    v.tensor_copy(usb[:], pu[0:64, :]).then_inc(s_kb.bump(), 1)
                @blk.sync
                def _(sy):
                    sy.wait_ge(s_kb.h, KB + 17)
                    sy.dma_start(cc0s.ap()[s * 64:(s + 1) * 64, :], usb[:]).then_inc(s_sp.bump(16), 16)
                    if "cc0dbg" in ctx.taps:
                        dbg = ctx.dtmp("cc0dbg", [128, D], F32)
                        sy.dma_start(dbg.ap()[s * 64:(s + 1) * 64, :], usb[:]).then_inc(s_sp.bump(16), 16)
                    sy.dma_start(ktm_d.ap()[s], ktm[:]).then_inc(s_sp.bump(16), 16)
                    sy.dma_start(vtm_d.ap()[s], vtm[:]).then_inc(s_sp.bump(16), 16)
                    sy.wait_ge(s_sp.h, s_sp.n)
        if parts == "nocc" or SIM_SKIP_CC:
            ctx.sem("cc0").bump()
            return
        s_cc0 = ctx.sem("cc0")
        with nc.Block() as blk:
            @blk.gpsimd
            def _(gp):
                gp.collective_compute(
                    "AllGather", OP.bypass, replica_groups=[list(range(NCORE))],
                    ins=[ctx.dram["cc0src"].ap().opt()],
                    outs=[ctx.dram["cc0dst"].ap().opt()]).then_inc(s_cc0.bump(), 1)



def emit_moe(ctx, layer):
    nc = ctx.nc
    g = ctx.glob
    E = 8 if layer == 0 else 16
    act = AF.Relu if layer == 0 else AF.Gelu
    xn_fm = ctx.dtmp("xn_fm", [8, 128, TOK], BF16)
    g_d = ctx.dtmp(f"g{layer}", [128, 16, D], BF16)
    if layer == 0:
        w1 = ctx.din("m0_w1t", [E, D, 2048]); w2 = ctx.din("m0_w2t", [E, 2048, D])
        b1 = ctx.din("m0_b1", [E, 128, 16]); b2 = ctx.din("m0_b2", [E, D])
        wg_h = ctx.din("m0_wg", [D, E])
    else:
        w1 = ctx.din("m1_w1", [E, D, 2048]); w2 = ctx.din("m1_w2", [E, 2048, D])
        b1 = ctx.din("m1_b1", [E, 128, 16]); b2 = ctx.din("m1_b2", [E, D])
        wg_h = ctx.din("rw", [D, E])
    with contextlib.ExitStack() as _es:
        comb = _es.enter_context(nc.sbuf_tensor(ctx.name("mcb"), [128, 16, E], F32))
        lgall = _es.enter_context(nc.sbuf_tensor(ctx.name("mlg"), [128, 16, E], F32))
        m1n = _es.enter_context(nc.sbuf_tensor(ctx.name("mm1"), [128, 16], F32))
        m2a = _es.enter_context(nc.sbuf_tensor(ctx.name("mm2"), [128, 16], F32))
        em2 = _es.enter_context(nc.sbuf_tensor(ctx.name("mem"), [128, 16], F32))
        aux = _es.enter_context(nc.sbuf_tensor(ctx.name("mau"), [128, 16], F32))
        bgb = _es.enter_context(nc.sbuf_tensor(ctx.name("mgb"), [128, E], F32))
        xng = _es.enter_context(nc.sbuf_tensor(ctx.name("mxg"), [128, 2, 8, 128], BF16))
        bgsb = _es.enter_context(nc.sbuf_tensor(ctx.name("mbg16"), [1, 8], BF16))
        wgsb = _es.enter_context(nc.sbuf_tensor(ctx.name("mwg"), [128, 8, E], BF16))
        gacc = _es.enter_context(nc.sbuf_tensor(ctx.name("mga"), [128, 16, D], F32))
        pga = _es.enter_context(nc.psum_tensor(ctx.name("mpga"), [128, 512], F32))
        pgb = _es.enter_context(nc.psum_tensor(ctx.name("mpgb"), [128, 512], F32))
        s_l = ctx.sem("mo_l"); s_gp = ctx.sem("mo_gp"); s_gv = ctx.sem("mo_gv")
        s_gx = [ctx.sem("mo_gxA"), ctx.sem("mo_gxB")]
        with nc.Block() as blk:
            L0_, GP, GV = s_l.n, s_gp.n, s_gv.n
            GX = [x.n for x in s_gx]
            @blk.gpsimd
            def _(gp_):
                for dc in range(8):
                    gp_.dma_start(wgsb[:, dc, :], wg_h.ap()[dc * 128:(dc + 1) * 128, :]).then_inc(s_l.bump(16), 16)
                if layer == 0:
                    gp_.dma_start(bgsb[:], ctx.din("m0_bg", [1, 8]).ap()).then_inc(s_l.bump(16), 16)
            @blk.sync
            def _(sy):
                for tc in range(16):
                    if tc >= 2:
                        sy.wait_ge(s_gp.h, GP + tc - 1)
                    sy.dma_start(
                        xng[:, tc % 2, :, :],
                        xn_fm.ap()[:, :, tc * 128:(tc + 1) * 128].rearrange("c p j -> p c j")
                    ).then_inc(s_gx[tc % 2].bump(16), 16)
            @blk.tensor
            def _(t_):
                t_.wait_ge(s_l.h, L0_ + 16 * (9 if layer == 0 else 8))
                for tc in range(16):
                    t_.wait_ge(s_gx[tc % 2].h, GX[tc % 2] + 16 * (tc // 2 + 1))
                    if tc >= 2:
                        t_.wait_ge(s_gv.h, GV + tc - 1)
                    pgx = (pga, pgb)[tc % 2]
                    last = None
                    for dc in range(8):
                        last = t_.matmul(pgx[:, 0:E], xng[:, tc % 2, dc, :], wgsb[:, dc, :],
                                  start=(dc == 0), stop=(dc == 7))
                    last.then_inc(s_gp.bump(), 1)
            @blk.vector
            def _(v):
                for tc in range(16):
                    v.wait_ge(s_gp.h, GP + tc + 1)
                    v.tensor_copy(lgall[:, tc, :], (pga, pgb)[tc % 2][:, 0:E]).then_inc(s_gv.bump(), 1)
                for t in range(16):
                    v.memset(gacc[:, t, :], 0.0)
        if layer == 0:
            with nc.Block() as blk:
                @blk.tensor
                def _(t_):
                    t_.matmul(pga[:, 0:E], g["ones1b"][:], bgsb[:], start=True, stop=True)
            with nc.Block() as blk:
                @blk.vector
                def _(v):
                    v.tensor_copy(bgb[:], pga[:, 0:E])
                    v.drain()
                    for tc in range(16):
                        v.tensor_tensor(lgall[:, tc, :], lgall[:, tc, :], bgb[:], OP.add)
        with nc.Block() as blk:
            @blk.vector
            def _(v):
                for tc in range(16):
                    v.reduce_max(m1n[:, tc:tc + 1], lgall[:, tc, :], axis=AX, negate=True)
                v.drain()
                if layer == 0:
                    for tc in range(16):
                        v.tensor_scalar(aux[:, tc:tc + 1], m1n[:, tc:tc + 1], -1.0, None, OP.mult)
                    v.drain()
                    for tc in range(16):
                        v.tensor_scalar(comb[:, tc, :], lgall[:, tc, :], aux[:, tc:tc + 1], None, OP.is_ge)
                        v.drain()
                        v.scalar_tensor_tensor(comb[:, tc, :], comb[:, tc, :], NEGBIG,
                                               lgall[:, tc, :], OP.mult, OP.add)
                        v.drain()
                        v.reduce_max(m2a[:, tc:tc + 1], comb[:, tc, :], axis=AX)
                        v.drain()
                        v.tensor_scalar(comb[:, tc, :], lgall[:, tc, :], m2a[:, tc:tc + 1],
                                        None, OP.is_ge)
                else:
                    for tc in range(16):
                        v.tensor_scalar(aux[:, tc:tc + 1], m1n[:, tc:tc + 1], -1.0, None, OP.mult)
                        v.drain()
                        v.tensor_scalar(comb[:, tc, :], lgall[:, tc, :], aux[:, tc:tc + 1],
                                        None, OP.is_ge)
        with nc.Block() as blk:
            @blk.scalar
            def _(sc):
                if layer == 0:
                    for tc in range(16):
                        sc.activation(lgall[:, tc, :], lgall[:, tc, :], AF.Exp,
                                      bias=m1n[:, tc:tc + 1])
                        sc.activation(em2[:, tc:tc + 1], m2a[:, tc:tc + 1], AF.Exp,
                                      bias=m1n[:, tc:tc + 1])
                else:
                    for tc in range(16):
                        sc.activation(em2[:, tc:tc + 1], m1n[:, tc:tc + 1], AF.Exp)
        with nc.Block() as blk:
            @blk.vector
            def _(v):
                v.tensor_scalar(aux[:], em2[:], 1.0, None, OP.add)
                v.drain()
                v.reciprocal(em2[:], aux[:])
                v.drain()
                if layer == 0:
                    for tc in range(16):
                        v.tensor_tensor(comb[:, tc, :], comb[:, tc, :], lgall[:, tc, :], OP.mult)
                        v.drain()
                        v.tensor_scalar(comb[:, tc, :], comb[:, tc, :], em2[:, tc:tc + 1],
                                        None, OP.mult)
                        v.drain()
                else:
                    for tc in range(16):
                        v.tensor_scalar(comb[:, tc, :], comb[:, tc, :], em2[:, tc:tc + 1],
                                        None, OP.mult)
        with contextlib.ExitStack() as _es:
            w1sb = _es.enter_context(nc.sbuf_tensor(ctx.name("mw1"), [128, 8, 2048], BF16))
            w2sb = _es.enter_context(nc.sbuf_tensor(ctx.name("mw2"), [128, 16, D], BF16))
            hsb = _es.enter_context(nc.sbuf_tensor(ctx.name("mh"), [128, 16, 512], BF16))
            xnt = _es.enter_context(nc.sbuf_tensor(ctx.name("mx5"), [128, 2, 8, 512], BF16))
            b1sb = _es.enter_context(nc.sbuf_tensor(ctx.name("mb1"), [128, 2, 16], F32))
            b2sb = _es.enter_context(nc.sbuf_tensor(ctx.name("mb2"), [1, 2, D], BF16))
            ph = _es.enter_context(nc.psum_tensor(ctx.name("mph"), [128, 2, 512], F32))
            po = _es.enter_context(nc.psum_tensor(ctx.name("mpo"), [128, 2, 512], F32))
            s_w1 = ctx.sem("mo_w1"); s_w2 = ctx.sem("mo_w2")
            s_b1s = ctx.sem("mo_b1"); s_b2s = ctx.sem("mo_b2")
            s_xn = [ctx.sem("mo_xnA"), ctx.sem("mo_xnB")]
            s_hp = ctx.sem("mo_hp"); s_ha = ctx.sem("mo_ha")
            s_ht = ctx.sem("mo_ht"); s_op = ctx.sem("mo_op"); s_mac = ctx.sem("mo_mac")
            s_ot = ctx.sem("mo_ot")
            with nc.Block() as blk:
                W1, W2, HP, HA, OPM, MAC = (
                    s_w1.n, s_w2.n, s_hp.n, s_ha.n, s_op.n, s_mac.n)
                B1, B2 = s_b1s.n, s_b2s.n
                XN = [x.n for x in s_xn]
                @blk.gpsimd
                def _(gp_):
                    for e in range(E):
                        if e >= 1:
                            gp_.wait_ge(s_hp.h, HP + 64 * e)
                        for dc in range(8):
                            gp_.dma_start(w1sb[:, dc, :], w1.ap()[e, dc * 128:(dc + 1) * 128, :]
                                          ).then_inc(s_w1.bump(16), 16)
                        if e >= 2:
                            gp_.wait_ge(s_op.h, OPM + 32 * (e - 1))
                        gp_.dma_start(b1sb[:, e % 2, :], b1.ap()[e]).then_inc(s_b1s.bump(16), 16)
                        gp_.dma_start(b2sb[0:1, e % 2, :], b2.ap()[e:e + 1, :]).then_inc(s_b2s.bump(16), 16)
                        if e >= 1:
                            gp_.wait_ge(s_op.h, OPM + 32 * e)
                        for fc in range(16):
                            gp_.dma_start(w2sb[:, fc, :], w2.ap()[e, fc * 128:(fc + 1) * 128, :]
                                          ).then_inc(s_w2.bump(16), 16)
                @blk.sync
                def _(sy):
                    i = 0
                    for e in range(E):
                        for hf in range(2):
                            for t5 in range(2):
                                if i >= 2:
                                    sy.wait_ge(s_hp.h, HP + 16 * (i - 1))
                                sy.dma_start(
                                    xnt[:, i % 2, :, :],
                                    xn_fm.ap()[:, :, hf * 1024 + t5 * 512:hf * 1024 + (t5 + 1) * 512]
                                    .rearrange("c p j -> p c j")).then_inc(s_xn[i % 2].bump(16), 16)
                                i += 1
                @blk.tensor
                def _(t_):
                    i = 0
                    for e in range(E):
                        for hf in range(2):
                            for t5 in range(2):
                                t_.wait_ge(s_w1.h, W1 + 128 * (e + 1))
                                t_.wait_ge(s_xn[i % 2].h, XN[i % 2] + 16 * (i // 2 + 1))
                                for fc in range(16):
                                    gi = (i * 16 + fc)
                                    if gi >= 2:
                                        t_.wait_ge(s_ha.h, HA + gi - 1)
                                    last = None
                                    for dc in range(8):
                                        last = t_.matmul(ph[:, gi % 2, :], w1sb[:, dc, fc * 128:(fc + 1) * 128],
                                                  xnt[:, i % 2, dc, :], start=(dc == 0), stop=(dc == 7))
                                    last.then_inc(s_hp.bump(), 1)
                                t_.wait_ge(s_w2.h, W2 + 256 * (e + 1))
                                t_.wait_ge(s_b2s.h, B2 + 16 * (e + 1))
                                t_.wait_ge(s_ha.h, HA + 16 * (i + 1))
                                for tloc in range(4):
                                    for ot in range(2):
                                        oi = i * 8 + tloc * 2 + ot
                                        if oi >= 2:
                                            t_.wait_ge(s_mac.h, MAC + oi - 1)
                                        for fc in range(16):
                                            t_.matmul(po[:, oi % 2, :], hsb[:, fc, tloc * 128:(tloc + 1) * 128],
                                                      w2sb[:, fc, ot * 512:(ot + 1) * 512],
                                                      start=(fc == 0), stop=False)
                                        t_.matmul(po[:, oi % 2, :], g["ones1b"][:],
                                                  b2sb[0:1, e % 2, ot * 512:(ot + 1) * 512],
                                                  start=False, stop=True).then_inc(s_op.bump(), 1)
                                i += 1
                @blk.scalar
                def _(sc):
                    i = 0
                    for e in range(E):
                        for hf in range(2):
                            for t5 in range(2):
                                for fc in range(16):
                                    gi = i * 16 + fc
                                    sc.wait_ge(s_hp.h, HP + gi + 1)
                                    if hf == 0 and t5 == 0 and fc == 0:
                                        sc.wait_ge(s_b1s.h, B1 + 16 * (e + 1))
                                    if i >= 1 and fc == 0:
                                        sc.wait_ge(s_op.h, OPM + 8 * i)
                                    sc.activation(hsb[:, fc, :],
                                                  ph[:, gi % 2, :], act,
                                                  bias=b1sb[:, e % 2, fc:fc + 1]).then_inc(s_ha.bump(), 1)
                                i += 1
                @blk.vector
                def _(v):
                    i = 0
                    for e in range(E):
                        for hf in range(2):
                            for t5 in range(2):
                                for tloc in range(4):
                                    for ot in range(2):
                                        oi = i * 8 + tloc * 2 + ot
                                        tg = hf * 8 + t5 * 4 + tloc
                                        v.wait_ge(s_op.h, OPM + oi + 1)
                                        v.scalar_tensor_tensor(
                                            gacc[:, tg, ot * 512:(ot + 1) * 512],
                                            po[:, oi % 2, :], comb[:, tg, e:e + 1],
                                            gacc[:, tg, ot * 512:(ot + 1) * 512],
                                            OP.mult, OP.add).then_inc(s_mac.bump(), 1)
                                i += 1
            with (nc.sbuf_tensor(ctx.name("mgs"), [128, 2, D], BF16) as gsb,):
                s_sg = ctx.sem("mo_sg")
                s_go = [ctx.sem("mo_goA"), ctx.sem("mo_goB")]
                with nc.Block() as blk:
                    SG = s_sg.n
                    GO = [x.n for x in s_go]
                    @blk.scalar
                    def _(sc):
                        for t in range(16):
                            if t >= 2:
                                sc.wait_ge(s_go[t % 2].h, GO[t % 2] + 16 * (t // 2))
                            sc.activation(gsb[:, t % 2, :], gacc[:, t, :], AF.Silu).then_inc(s_sg.bump(), 1)
                    @blk.sync
                    def _(sy):
                        for t in range(16):
                            sy.wait_ge(s_sg.h, SG + t + 1)
                            sy.dma_start(g_d.ap()[:, t, :], gsb[:, t % 2, :]).then_inc(s_go[t % 2].bump(16), 16)
                        sy.wait_ge(s_go[0].h, GO[0] + 128)
                        sy.wait_ge(s_go[1].h, GO[1] + 128)


def emit_mixer_b(ctx, parts="all"):
    nc = ctx.nc
    g = ctx.glob
    x_in = ctx.din("x", [TOK, D])
    xn_fm = ctx.dtmp("xn_fm", [8, 128, TOK], BF16)
    ktm_d = ctx.dtmp("ktm", [2, 64, 16, D], BF16)
    vtm_d = ctx.dtmp("vtm", [2, 64, 16, D], BF16)
    cc0d = ctx.dtmp("cc0dst", [NCORE * 128, D], F32)
    g0_d = ctx.dtmp("g0", [128, 16, D], BF16)
    x1_d = ctx.dtmp("x1", [TOK, D], F32)
    wk = ctx.din("wk", [D, D]); wv = ctx.din("wv", [D, D])
    wr = ctx.din("wr", [D, D]); wgm = ctx.din("wgm", [D, D])
    wo = ctx.din("wo", [D, D])
    with contextlib.ExitStack() as mbes:
        ec_ = mbes.enter_context
        ktmq = ec_(nc.sbuf_tensor("mb_ktq", [64, 2, 4, D], BF16))
        vtmq = ec_(nc.sbuf_tensor("mb_vtq", [64, 2, 4, D], BF16))
        kfm = ec_(nc.sbuf_tensor("mb_kfm", [128, 8, SEG], BF16))
        vfm = ec_(nc.sbuf_tensor("mb_vfm", [128, 8, SEG], BF16))
        wbuf = ec_(nc.sbuf_tensor("mb_w", [128, 8, 2048], BF16))
        wosb = ec_(nc.sbuf_tensor("mb_wo", [128, 8, D], BF16))
        rq = ec_(nc.sbuf_tensor("mb_rq", [128, 8, 256], BF16))
        gq = ec_(nc.sbuf_tensor("mb_gq", [128, 8, 256], BF16))
        xnq = ec_(nc.sbuf_tensor("mb_xq", [128, 2, 8, 256], BF16))
        khb = ec_(nc.sbuf_tensor("mb_kh", [128, 2, 8, CH], BF16))
        ktb = ec_(nc.sbuf_tensor("mb_kt", [128, 2, 8, CH], BF16))
        kb = ec_(nc.sbuf_tensor("mb_kb", [64, D], BF16))
        am = ec_(nc.sbuf_tensor("mb_am", [64, 512], BF16))
        S = ec_(nc.sbuf_tensor("mb_S", [128, D], F32))
        S2 = ec_(nc.sbuf_tensor("mb_S2", [128, D], BF16))
        St = ec_(nc.sbuf_tensor("mb_St", [128, D], F32))
        tw = ec_(nc.sbuf_tensor("mb_tw", [128, 8, CH], F32))
        ws = ec_(nc.sbuf_tensor("mb_ws", [128, 8, CH], F32))
        rp = ec_(nc.sbuf_tensor("mb_rp", [128, 8, 256], BF16))
        usl = ec_(nc.sbuf_tensor("mb_us", [128, D], F32))
        xres = ec_(nc.sbuf_tensor("mb_xr", [128, 2, D], F32))
        gres = ec_(nc.sbuf_tensor("mb_gr", [128, 2, D], BF16))
        x1sb = ec_(nc.sbuf_tensor("mb_x1", [128, 2, D], F32))
        cfn = ec_(nc.sbuf_tensor("mb_cn", [16, 2, 64], F32))
        for s in range(2):
            s_lw = ctx.sem("mb_lw")
            with nc.Block() as blk:
                LW = s_lw.n
                @blk.gpsimd
                def _(gp_):
                    for dc in range(8):
                        gp_.dma_start(wbuf[:, dc, 0:D], wk.ap()[dc * 128:(dc + 1) * 128, :]).then_inc(s_lw.bump(16), 16)
                        gp_.dma_start(wbuf[:, dc, D:2 * D], wv.ap()[dc * 128:(dc + 1) * 128, :]).then_inc(s_lw.bump(16), 16)
                        gp_.dma_start(wosb[:, dc, :], wo.ap()[dc * 128:(dc + 1) * 128, :]).then_inc(s_lw.bump(16), 16)
                @blk.vector
                def _(v):
                    v.wait_ge(s_lw.h, LW + 16 * 24)
                    for dc in range(8):
                        v.tensor_scalar(wbuf[:, dc, :], wbuf[:, dc, :],
                                        g["tm_col"][:, dc:dc + 1], None, OP.mult)
            s_xf = [ctx.sem("mb_xfA"), ctx.sem("mb_xfB")]
            s_fp = ctx.sem("mb_fp"); s_fe = ctx.sem("mb_fe")
            with contextlib.ExitStack() as pes1:
                pfa = pes1.enter_context(nc.psum_tensor(ctx.name("mbpfa"), [128, 512], F32))
                pfb = pes1.enter_context(nc.psum_tensor(ctx.name("mbpfb"), [128, 512], F32))
                with nc.Block() as blk:
                    FP, FE = s_fp.n, s_fe.n
                    XF = [x.n for x in s_xf]
                    @blk.sync
                    def _(sy):
                        for qq in range(4):
                            if qq >= 2:
                                sy.wait_ge(s_fp.h, FP + 16 * (qq - 1))
                            sy.dma_start(
                                xnq[:, qq % 2, :, :],
                                xn_fm.ap()[:, :, s * SEG + qq * 256:s * SEG + (qq + 1) * 256]
                                .rearrange("c p j -> p c j")).then_inc(s_xf[qq % 2].bump(16), 16)
                    @blk.tensor
                    def _(t_):
                        i = 0
                        for qq in range(4):
                            t_.wait_ge(s_xf[qq % 2].h, XF[qq % 2] + 16 * (qq // 2 + 1))
                            for w in range(2):
                                for ec in range(8):
                                    if i >= 2:
                                        t_.wait_ge(s_fe.h, FE + i - 1)
                                    last = None
                                    for dc in range(8):
                                        last = t_.matmul((pfa, pfb)[i % 2][:, 0:256],
                                                  wbuf[:, dc, w * D + ec * 128:w * D + (ec + 1) * 128],
                                                  xnq[:, qq % 2, dc, :],
                                                  start=(dc == 0), stop=(dc == 7))
                                    last.then_inc(s_fp.bump(), 1)
                                    i += 1
                    @blk.vector
                    def _(v):
                        i = 0
                        for qq in range(4):
                            for w in range(2):
                                dstb = (kfm, vfm)[w]
                                for ec in range(8):
                                    v.wait_ge(s_fp.h, FP + i + 1)
                                    v.tensor_copy(dstb[:, ec, qq * 256:(qq + 1) * 256],
                                                  (pfa, pfb)[i % 2][:, 0:256]).then_inc(s_fe.bump(), 1)
                                    i += 1
            if parts == "m1":
                continue
            s_cs = [ctx.sem("mb_csA"), ctx.sem("mb_csB")]
            s_cp = ctx.sem("mb_cp"); s_cv = ctx.sem("mb_cv")
            s_ca = ctx.sem("mb_ca")
            with contextlib.ExitStack() as pes2:
                pco = pes2.enter_context(nc.psum_tensor(ctx.name("mbpc"), [128, D], F32))
                with nc.Block() as blk:
                    CP, CV, CA = s_cp.n, s_cv.n, s_ca.n
                    CS = [x.n for x in s_cs]
                    s_cc0 = ctx.sem("cc0")
                    @blk.vector
                    def _(v):
                        v.memset(S[:], 0.0)
                        for sl in range(16):
                            v.wait_ge(s_cs[0].h, CS[0] + 32 * (sl + 1))
                            v.wait_ge(s_cp.h, CP + sl + 1)
                            v.tensor_tensor(St[:], pco[:], usl[:], OP.mult)
                            v.drain()
                            v.tensor_tensor(S[:], S[:], St[:], OP.add).then_inc(s_cv.bump(), 1)
                            v.drain()
                        v.tensor_copy(S2[:], S[:])
                    @blk.sync
                    def _(sy):
                        if not SIM_SKIP_CC:
                            sy.wait_ge(s_cc0.h, 1)
                        for sl in range(16):
                            if sl >= 1:
                                sy.wait_ge(s_cv.h, CV + sl)
                            sy.dma_start(usl[0:64, :],
                                         cc0d.ap()[sl * 64:(sl + 1) * 64, :]).then_inc(s_cs[0].bump(16), 16)
                            sy.dma_start(usl[64:128, :],
                                         cc0d.ap()[sl * 64:(sl + 1) * 64, :]).then_inc(s_cs[0].bump(16), 16)
                    @blk.scalar
                    def _(sc):
                        for sl in range(16):
                            if sl >= 1:
                                sc.wait_ge(s_cp.h, CP + sl)
                            sc.activation(cfn[:, sl % 2, :], g["td_nat"][:], AF.Exp,
                                          scale=g["nexpo"][0:16, s * 16 + sl:s * 16 + sl + 1]
                                          ).then_inc(s_ca.bump(), 1)
                    @blk.tensor
                    def _(t_):
                        for sl in range(16):
                            t_.wait_ge(s_ca.h, CA + sl + 1)
                            if sl >= 1:
                                t_.wait_ge(s_cv.h, CV + sl)
                            last = None
                            for hh in (0, 64):
                                for cc in (0, 512):
                                    last = t_.matmul(pco[hh:hh + 64, cc:cc + 512],
                                              cfn[:, sl % 2, :], g["hsel"][:, cc:cc + 512],
                                              start=True, stop=True)
                            last.then_inc(s_cp.bump(), 1)
            if parts == "m2":
                continue
            m3rg = parts == "m3rg"
            m3ch = parts in ("m3rg", "m3ch", "m3c1", "m3c2")
            c_wkv = parts not in ("m3c1",)
            c_u = parts not in ("m3c1", "m3c2")
            s_fold = ctx.sem("mb_fo")
            s_xq2 = [ctx.sem("mb_x2A"), ctx.sem("mb_x2B")]
            s_kv2 = [ctx.sem("mb_kvA"), ctx.sem("mb_kvB")]
            s_rgp = ctx.sem("mb_rg")
            s_rge = ctx.sem("mb_re"); s_ge = ctx.sem("mb_gse")
            s_pA = ctx.sem("mb_pA"); s_pw = ctx.sem("mb_pw"); s_pU = ctx.sem("mb_pU")
            s_wo2 = ctx.sem("mb_wo"); s_dkh = ctx.sem("mb_d1"); s_dam = ctx.sem("mb_d2")
            s_drg = ctx.sem("mb_d3"); s_dkb = ctx.sem("mb_d4"); s_dsu = ctx.sem("mb_d5")
            s_rs = ctx.sem("mb_rs")
            s_xt = [ctx.sem("mb_xtA"), ctx.sem("mb_xtB")]
            s_gt = [ctx.sem("mb_gtA"), ctx.sem("mb_gtB")]
            s_xo = [ctx.sem("mb_xoA"), ctx.sem("mb_xoB")]
            s_lw2 = ctx.sem("mb_l2")
            with contextlib.ExitStack() as pes3:
                pA = pes3.enter_context(nc.psum_tensor(ctx.name("mbpa"), [128, 512], F32))
                pw = pes3.enter_context(nc.psum_tensor(ctx.name("mbpw"), [128, 16, CH], F32))
                pU = pes3.enter_context(nc.psum_tensor(ctx.name("mbpu"), [128, D], F32))
                pRGa = pes3.enter_context(nc.psum_tensor(ctx.name("mbra"), [128, 512], F32))
                pRGb = pes3.enter_context(nc.psum_tensor(ctx.name("mbrb"), [128, 512], F32))
                ps_s = pes3.enter_context(nc.psum_tensor(ctx.name("mbps"), [128, 2, 512], F32))
                with nc.Block() as blk:
                    FO, RGP, RGE, GE = s_fold.n, s_rgp.n, s_rge.n, s_ge.n
                    PA, PW, PU, WO2 = s_pA.n, s_pw.n, s_pU.n, s_wo2.n
                    KH, AM, RG3, KB4, SU = s_dkh.n, s_dam.n, s_drg.n, s_dkb.n, s_dsu.n
                    RS, LW2 = s_rs.n, s_lw2.n
                    XQ = [x.n for x in s_xq2]
                    KV2 = [x.n for x in s_kv2]
                    XT = [x.n for x in s_xt]
                    GT = [x.n for x in s_gt]
                    XO = [x.n for x in s_xo]
                    @blk.gpsimd
                    def _(gp_):
                        for dc in range(8):
                            gp_.dma_start(wbuf[:, dc, 0:D], wr.ap()[dc * 128:(dc + 1) * 128, :]).then_inc(s_lw2.bump(16), 16)
                            gp_.dma_start(wbuf[:, dc, D:2 * D], wgm.ap()[dc * 128:(dc + 1) * 128, :]).then_inc(s_lw2.bump(16), 16)
                        for tc in range(8):
                            if m3ch:
                                continue
                            if tc >= 2:
                                gp_.wait_ge(s_rs.h, RS + tc - 1)
                            gp_.dma_start(gres[:, tc % 2, :],
                                          g0_d.ap()[:, s * 8 + tc, :]).then_inc(s_gt[tc % 2].bump(16), 16)
                    @blk.sync
                    def _(sy):
                        for q in range(4):
                            if q >= 2:
                                sy.wait_ge(s_rgp.h, RGP + 16 * (q - 1))
                                if not m3rg:
                                    sy.wait_ge(s_pU.h, PU + DEBUG_NCH * (q - 1))
                            sy.dma_start(
                                xnq[:, q % 2, :, :],
                                xn_fm.ap()[:, :, s * SEG + q * 256:s * SEG + (q + 1) * 256]
                                .rearrange("c p j -> p c j")).then_inc(s_xq2[q % 2].bump(16), 16)
                            if not m3rg:
                                sy.dma_start(ktmq[:, q % 2, :, :],
                                             ktm_d.ap()[s, :, 4 * q:4 * q + 4, :]).then_inc(s_kv2[q % 2].bump(16), 16)
                                sy.dma_start(vtmq[:, q % 2, :, :],
                                             vtm_d.ap()[s, :, 4 * q:4 * q + 4, :]).then_inc(s_kv2[q % 2].bump(16), 16)
                            if m3ch:
                                continue
                            for tc in (2 * q, 2 * q + 1):
                                if tc >= 2:
                                    sy.wait_ge(s_rs.h, RS + tc - 1)
                                sy.dma_start(xres[:, tc % 2, :],
                                             x_in.ap()[s * SEG + tc * 128:s * SEG + (tc + 1) * 128, :]
                                             ).then_inc(s_xt[tc % 2].bump(16), 16)
                                if tc >= 2:
                                    to = tc - 2
                                    sy.wait_ge(s_rs.h, RS + to + 1)
                                    sy.dma_start(x1_d.ap()[s * SEG + to * 128:s * SEG + (to + 1) * 128, :],
                                                 x1sb[:, to % 2, :]).then_inc(s_xo[to % 2].bump(16), 16)
                        if not m3ch:
                            for tc in (6, 7):
                                sy.wait_ge(s_rs.h, RS + tc + 1)
                                sy.dma_start(x1_d.ap()[s * SEG + tc * 128:s * SEG + (tc + 1) * 128, :],
                                             x1sb[:, tc % 2, :]).then_inc(s_xo[tc % 2].bump(16), 16)
                            sy.wait_ge(s_xo[0].h, XO[0] + 64)
                            sy.wait_ge(s_xo[1].h, XO[1] + 64)
                    @blk.vector
                    def _(v):
                        v.wait_ge(s_lw2.h, LW2 + 256)
                        last = None
                        for dc in range(8):
                            last = v.tensor_scalar(wbuf[:, dc, :], wbuf[:, dc, :],
                                            g["tm_col"][:, dc:dc + 1], None, OP.mult)
                        last.then_inc(s_fold.bump(), 1)
                        for q in range(4):
                            for ec in range(8):
                                v.wait_ge(s_rgp.h, RGP + 16 * q + ec + 1)
                                v.tensor_copy(rq[:, ec, :], (pRGa, pRGb)[ec % 2][:, 0:256]).then_inc(s_rge.bump(), 1)
                            if m3rg:
                                continue
                            v.wait_ge(s_kv2[q % 2].h, KV2[q % 2] + 32 * (q // 2 + 1))
                            for m in range(8 * q, 8 * q + DEBUG_NCH):
                                sl = slice(m * CH, (m + 1) * CH)
                                par = m % 2
                                if m >= 2 and c_wkv:
                                    v.wait_ge(s_pw.h, PW + m - 1)
                                v.tensor_tensor(khb[:, par, :, :], kfm[:, :, sl], g["dhat"][:], OP.mult)
                                v.tensor_tensor(ktb[:, par, :, :], kfm[:, :, sl], g["dneg"][:], OP.mult).then_inc(s_dkh.bump(), 1)
                                v.wait_ge(s_pA.h, PA + m + 1)
                                v.tensor_tensor(am[32 * par:32 * par + 32, :],
                                                pA[32 * par:32 * par + 32, :],
                                                g["mask64"][32 * par:32 * par + 32, :],
                                                OP.mult).then_inc(s_dam.bump(), 1)
                                if not c_wkv:
                                    v.nop().then_inc(s_drg.bump(), 1)
                                    v.nop().then_inc(s_dkb.bump(), 1)
                                    v.nop().then_inc(s_dsu.bump(), 1)
                                    continue
                                v.tensor_tensor(tw[:], kfm[:, :, sl], vfm[:, :, sl], OP.mult)
                                v.drain()
                                v.tensor_tensor(tw[:], tw[:], g["tfmap"][:], OP.mult)
                                v.wait_ge(s_pw.h, PW + m + 1)
                                v.wait_ge(s_ge.h, GE + 8 * q + 8)
                                v.drain()
                                v.tensor_tensor(ws[:], pw[:, 0:8, :], tw[:], OP.add)
                                v.drain()
                                v.tensor_tensor(ws[:], ws[:], rq[:, :, (m % 8) * CH:(m % 8 + 1) * CH], OP.mult)
                                v.drain()
                                v.tensor_tensor(rp[:, :, (m % 8) * CH:(m % 8 + 1) * CH],
                                                ws[:], gq[:, :, (m % 8) * CH:(m % 8 + 1) * CH],
                                                OP.mult).then_inc(s_drg.bump(), 1)
                                if not c_u:
                                    v.nop().then_inc(s_dkb.bump(), 1)
                                    v.nop().then_inc(s_dsu.bump(), 1)
                                    continue
                                if m >= 2:
                                    v.wait_ge(s_pU.h, PU + m - 1)
                                v.tensor_tensor(kb[32 * par:32 * par + 32, :],
                                                ktmq[32 * par:32 * par + 32, q % 2, m // 2 - 4 * q, :],
                                                g["dbar32"][32 * par:32 * par + 32, :],
                                                OP.mult).then_inc(s_dkb.bump(), 1)
                                v.wait_ge(s_pU.h, PU + m + 1)
                                v.tensor_tensor(St[:], S[:], g["dc32"][:], OP.mult)
                                v.drain()
                                v.tensor_tensor(S2[:], St[:], pU[:], OP.add).then_inc(s_dsu.bump(), 1)
                                v.drain()
                                v.tensor_tensor(S[:], St[:], pU[:], OP.add)
                                v.drain()
                            for hh in range(2):
                                if m3ch:
                                    continue
                                tc = 2 * q + hh
                                v.wait_ge(s_wo2.h, WO2 + tc + 1)
                                v.wait_ge(s_xt[tc % 2].h, XT[tc % 2] + 16 * (tc // 2 + 1))
                                v.wait_ge(s_gt[tc % 2].h, GT[tc % 2] + 16 * (tc // 2 + 1))
                                if tc >= 2:
                                    v.wait_ge(s_xo[tc % 2].h, XO[tc % 2] + 16 * (tc // 2))
                                last = None
                                for ot in range(2):
                                    v.tensor_tensor(St[:, 0:512], ps_s[:, ot, :],
                                                    gres[:, tc % 2, ot * 512:(ot + 1) * 512], OP.mult)
                                    v.drain()
                                    last = v.tensor_tensor(x1sb[:, tc % 2, ot * 512:(ot + 1) * 512],
                                                    St[:, 0:512], xres[:, tc % 2, ot * 512:(ot + 1) * 512],
                                                    OP.add)
                                    v.drain()
                                last.then_inc(s_rs.bump(), 1)
                    @blk.scalar
                    def _(sc):
                        for q in range(4):
                            for ec in range(8):
                                sc.wait_ge(s_rgp.h, RGP + 16 * q + 8 + ec + 1)
                                sc.activation(gq[:, ec, :], (pRGa, pRGb)[ec % 2][:, 0:256],
                                              AF.Silu).then_inc(s_ge.bump(), 1)
                    @blk.tensor
                    def _(t_):
                        prevb = [None]
                        def bmm(base, *a, **kw):
                            if prevb[0] is not None and prevb[0] != base:
                                t_.drain()
                            prevb[0] = base
                            return t_.matmul(*a, **kw)
                        for q in range(4):
                            t_.wait_ge(s_fold.h, FO + 1)
                            t_.wait_ge(s_xq2[q % 2].h, XQ[q % 2] + 16 * (q // 2 + 1))
                            if q >= 1 and not m3ch:
                                t_.wait_ge(s_drg.h, RG3 + DEBUG_NCH * q)
                            for w in range(2):
                                for ec in range(8):
                                    gi = 16 * q + w * 8 + ec
                                    if gi >= 2:
                                        ndone = gi - 1
                                        nfullq, rem = divmod(ndone, 16)
                                        rdone = nfullq * 8 + min(rem, 8)
                                        gdone = nfullq * 8 + max(rem - 8, 0)
                                        if rdone > 0:
                                            t_.wait_ge(s_rge.h, RGE + rdone)
                                        if gdone > 0:
                                            t_.wait_ge(s_ge.h, GE + gdone)
                                    prg = (pRGa, pRGb)[ec % 2]
                                    last = None
                                    for dc in range(8):
                                        last = bmm(0, prg[:, 0:256], wbuf[:, dc, w * D + ec * 128:w * D + (ec + 1) * 128],
                                                  xnq[:, q % 2, dc, :],
                                                  start=(dc == 0), stop=(dc == 7))
                                    last.then_inc(s_rgp.bump(), 1)
                            if m3rg:
                                continue
                            t_.wait_ge(s_kv2[q % 2].h, KV2[q % 2] + 32 * (q // 2 + 1))
                            for m in range(8 * q, 8 * q + DEBUG_NCH):
                                par = m % 2
                                t_.wait_ge(s_dkh.h, KH + m + 1)
                                t_.wait_ge(s_dam.h, AM + m)
                                first = True
                                last = None
                                for h in [hh for hh in range(H) if hh % 2 == 0] + [hh for hh in range(H) if hh % 2 == 1]:
                                    hb = 64 * (h % 2)
                                    last = bmm(hb, pA[32 * par:32 * par + 32, h * 32:(h + 1) * 32],
                                              ktb[hb:hb + 64, par, h // 2, :],
                                              khb[hb:hb + 64, par, h // 2, :],
                                              start=first, stop=(h == H - 1))
                                    first = False
                                last.then_inc(s_pA.bump(), 1)
                                if not c_wkv:
                                    t_.nop().then_inc(s_pw.bump(), 1)
                                    t_.nop().then_inc(s_pU.bump(), 1)
                                    continue
                                t_.wait_ge(s_dam.h, AM + m + 1)
                                t_.wait_ge(s_dsu.h, SU + m)
                                t_.wait_ge(s_drg.h, RG3 + m)
                                pwflat = bass.AP(pw, 0, [[512, 128], [1, 512]])
                                bmm("z", pwflat, g["ones1b"][:], g["zrow"][:],
                                    start=True, stop=False)
                                for h in range(H):
                                    bmm(32 * par, pw[64 * (h % 2):64 * (h % 2) + 64, h // 2, :],
                                        vtmq[32 * par:32 * par + 32, q % 2, m // 2 - 4 * q, h * 64:(h + 1) * 64],
                                        am[32 * par:32 * par + 32, h * 32:(h + 1) * 32],
                                        start=False, stop=False)
                                for h in [hh for hh in range(H) if hh % 2 == 0] + [hh for hh in range(H) if hh % 2 == 1]:
                                    hb = 64 * (h % 2)
                                    bmm(hb, pw[hb:hb + 64, h // 2, :],
                                        S2[hb:hb + 64, h * 64:(h + 1) * 64],
                                        khb[hb:hb + 64, par, h // 2, :],
                                        start=False, stop=False)
                                bmm("z", pwflat, g["ones1b"][:], g["zrow"][:],
                                    start=False, stop=True).then_inc(s_pw.bump(), 1)
                                if not c_u:
                                    t_.nop().then_inc(s_pU.bump(), 1)
                                    if m % 4 == 3 and not m3ch:
                                        pass
                                    continue
                                t_.wait_ge(s_dkb.h, KB4 + m + 1)
                                t_.wait_ge(s_dsu.h, SU + m)
                                for cc in (0, 512):
                                    bmm("z", pU[:, cc:cc + 512], g["ones1b"][:], g["zrow"][:],
                                        start=True, stop=False)
                                for half in range(2):
                                    for h in range(H):
                                        bmm(32 * par, pU[64 * half:64 * half + 64, h * 64:(h + 1) * 64],
                                            kb[32 * par:32 * par + 32, h * 64:(h + 1) * 64],
                                            vtmq[32 * par:32 * par + 32, q % 2, m // 2 - 4 * q, h * 64:(h + 1) * 64],
                                            start=False, stop=False)
                                last = None
                                for cc in (0, 512):
                                    last = bmm("z", pU[:, cc:cc + 512], g["ones1b"][:], g["zrow"][:],
                                               start=False, stop=True)
                                last.then_inc(s_pU.bump(), 1)
                                if m % 4 == 3 and not m3ch:
                                    tc = m // 4
                                    t_.wait_ge(s_drg.h, RG3 + m + 1)
                                    t_.wait_ge(s_rs.h, RS + tc)
                                    last = None
                                    for ot in range(2):
                                        for dc in range(8):
                                            last = bmm(0, ps_s[:, ot, :],
                                                      rp[:, dc, (tc % 2) * 128:(tc % 2) * 128 + 128],
                                                      wosb[:, dc, ot * 512:(ot + 1) * 512],
                                                      start=(dc == 0), stop=(dc == 7))
                                    last.then_inc(s_wo2.bump(), 1)


def emit_qkv(ctx):
    nc = ctx.nc
    xn_fm = ctx.dtmp("xn_fm", [8, 128, TOK], BF16)
    qf_d = ctx.dtmp("qf", [2, 128, 8, SEG], BF16)
    cc1s = ctx.dtmp("cc1src", [2, 2, 128, 8, SEG], BF16)
    cc1d = ctx.dtmp("cc1dst", [NCORE, 2, 2, 128, 8, SEG], BF16)
    wqkv = ctx.din("wqkv", [D, 3 * D])
    with contextlib.ExitStack() as _es:
        w3 = _es.enter_context(nc.sbuf_tensor("qk_w", [128, 8, 3 * D], BF16))
        xnq = _es.enter_context(nc.sbuf_tensor("qk_x", [128, 2, 8, 512], BF16))
        qsb = _es.enter_context(nc.sbuf_tensor("qk_q", [128, 8, SEG], BF16))
        ksb = _es.enter_context(nc.sbuf_tensor("qk_k", [128, 8, SEG], BF16))
        vsb = _es.enter_context(nc.sbuf_tensor("qk_v", [128, 8, SEG], BF16))
        pf = _es.enter_context(nc.psum_tensor("qk_pf", [128, 2, 512], F32))
        s_lw = ctx.sem("qk_lw")
        with nc.Block() as blk:
            LW = s_lw.n
            @blk.gpsimd
            def _(gp_):
                for dc in range(8):
                    gp_.dma_start(w3[:, dc, :], wqkv.ap()[dc * 128:(dc + 1) * 128, :]).then_inc(s_lw.bump(16), 16)
                gp_.wait_ge(s_lw.h, LW + 128)
        for s in range(2):
            s_xf = ctx.sem("qk_xf"); s_fp = ctx.sem("qk_fp"); s_fe = ctx.sem("qk_fe")
            s_sp = ctx.sem("qk_sp")
            with nc.Block() as blk:
                XF, FP, FE, SP = s_xf.n, s_fp.n, s_fe.n, s_sp.n
                @blk.sync
                def _(sy):
                    for t5 in range(2):
                        sy.dma_start(
                            xnq[:, t5, :, :],
                            xn_fm.ap()[:, :, s * SEG + t5 * 512:s * SEG + (t5 + 1) * 512]
                            .rearrange("c p j -> p c j")).then_inc(s_xf.bump(16), 16)
                    sy.wait_ge(s_fe.h, FE + 48)
                    sy.dma_start(qf_d.ap()[s], qsb[:]).then_inc(s_sp.bump(16), 16)
                    sy.dma_start(cc1s.ap()[s, 0], ksb[:]).then_inc(s_sp.bump(16), 16)
                    sy.dma_start(cc1s.ap()[s, 1], vsb[:]).then_inc(s_sp.bump(16), 16)
                    sy.wait_ge(s_sp.h, SP + 48)
                @blk.tensor
                def _(t_):
                    i = 0
                    for part in range(2):
                        for t5 in range(2):
                            t_.wait_ge(s_xf.h, XF + 32)
                            for ec in range(8):
                                if i >= 2:
                                    t_.wait_ge(s_fe.h, FE + i - 1)
                                last = None
                                for dc in range(8):
                                    last = t_.matmul(pf[:, i % 2, :],
                                              w3[:, dc, part * D + ec * 128:part * D + (ec + 1) * 128],
                                              xnq[:, t5, dc, :], start=(dc == 0), stop=(dc == 7))
                                last.then_inc(s_fp.bump(), 1)
                                i += 1
                    t_.wait_ge(s_xf.h, XF + 32)
                    for tc in range(8):
                        for vh in range(2):
                            if i >= 2:
                                t_.wait_ge(s_fe.h, FE + i - 1)
                            last = None
                            for dc in range(8):
                                last = t_.matmul(pf[:, i % 2, :],
                                          xnq[:, tc // 4, dc, (tc % 4) * 128:(tc % 4 + 1) * 128],
                                          w3[:, dc, 2 * D + vh * 512:2 * D + (vh + 1) * 512],
                                          start=(dc == 0), stop=(dc == 7))
                            last.then_inc(s_fp.bump(), 1)
                            i += 1
                @blk.vector
                def _(v):
                    i = 0
                    for part in range(2):
                        dstb = (qsb, ksb)[part]
                        for t5 in range(2):
                            for ec in range(8):
                                v.wait_ge(s_fp.h, FP + i + 1)
                                v.tensor_copy(dstb[:, ec, t5 * 512:(t5 + 1) * 512],
                                              pf[:, i % 2, :]).then_inc(s_fe.bump(), 1)
                                i += 1
                    for tc in range(8):
                        for vh in range(2):
                            v.wait_ge(s_fp.h, FP + i + 1)
                            v.tensor_copy(vsb[:, tc, vh * 512:(vh + 1) * 512],
                                          pf[:, i % 2, :]).then_inc(s_fe.bump(), 1)
                            i += 1
        if SIM_SKIP_CC:
            ctx.sem("cc1").bump()
            return
        s_cc1 = ctx.sem("cc1")
        with nc.Block() as blk:
            @blk.gpsimd
            def _(gp_):
                gp_.collective_compute(
                    "AllGather", OP.bypass, replica_groups=[list(range(NCORE))],
                    ins=[ctx.dram["cc1src"].ap().opt()],
                    outs=[ctx.dram["cc1dst"].ap().opt()]).then_inc(s_cc1.bump(), 1)



def emit_attn(ctx):
    nc = ctx.nc
    g = ctx.glob
    qf_d = ctx.dtmp("qf", [2, 128, 8, SEG], BF16)
    cc1d = ctx.dtmp("cc1dst", [NCORE, 2, 2, 128, 8, SEG], BF16)
    g1_d = ctx.dtmp("g1", [128, 16, D], BF16)
    x1_d = ctx.dtmp("x1", [TOK, D], F32)
    out_d = ctx.dout("out", [TOK, D], F32)
    idb = g["idb"]
    with contextlib.ExitStack() as _es:
        qseg = _es.enter_context(nc.sbuf_tensor("at_q", [128, 8, SEG], BF16))
        kbuf = _es.enter_context(nc.sbuf_tensor("at_k", [128, 2, 8, SEG], BF16))
        vbuf = _es.enter_context(nc.sbuf_tensor("at_v", [128, 2, 8, SEG], BF16))
        Oall = _es.enter_context(nc.sbuf_tensor("at_o", [128, 8, SEG], F32))
        Mall = _es.enter_context(nc.sbuf_tensor("at_m", [128, 8], F32))
        Dall = _es.enter_context(nc.sbuf_tensor("at_d", [128, 8], F32))
        mnew = _es.enter_context(nc.sbuf_tensor("at_mn", [128, 8], F32))
        nmn = _es.enter_context(nc.sbuf_tensor("at_nn", [128, 8], F32))
        resc = _es.enter_context(nc.sbuf_tensor("at_rc", [128, 8], F32))
        mxc = _es.enter_context(nc.sbuf_tensor("at_mx", [128, 2], F32))
        psc_ = _es.enter_context(nc.sbuf_tensor("at_ps", [128, 2], F32))
        P = _es.enter_context(nc.sbuf_tensor("at_P", [128, 2, SEG], BF16))
        PT = _es.enter_context(nc.sbuf_tensor("at_PT", [128, 8, 128], BF16))
        osb = _es.enter_context(nc.sbuf_tensor("at_os", [128, 2, D], F32))
        x1t = _es.enter_context(nc.sbuf_tensor("at_x1", [128, 2, D], F32))
        g2t = _es.enter_context(nc.sbuf_tensor("at_g2", [128, 2, D], F32))
        scr = _es.enter_context(nc.sbuf_tensor("at_sc", [128, D], F32))
        psc = _es.enter_context(nc.psum_tensor("at_psc", [128, 2, SEG], F32))
        ppt = _es.enter_context(nc.psum_tensor("at_ppt", [128, 8, 128], BF16))
        pav = _es.enter_context(nc.psum_tensor("at_pav", [128, 2, 512], F32))
        for s in range(2):
            s_kv = [ctx.sem("at_kvA"), ctx.sem("at_kvB")]
            s_q = ctx.sem("at_q"); s_sc = ctx.sem("at_sc")
            s_mx = ctx.sem("at_mx"); s_ax = ctx.sem("at_ax"); s_msk = ctx.sem("at_mk")
            s_pt = ctx.sem("at_pt"); s_pte = ctx.sem("at_pe"); s_av = ctx.sem("at_av")
            s_ou = ctx.sem("at_ou"); s_rs = ctx.sem("at_rs")
            s_xo = [ctx.sem("at_xoA"), ctx.sem("at_xoB")]
            s_x1 = [ctx.sem("at_x1A"), ctx.sem("at_x1B")]
            s_gt = [ctx.sem("at_gtA"), ctx.sem("at_gtB")]
            b = s
            N = 64
            with nc.Block() as blk:
                Q0, SC, MX, AX0, MSK = s_q.n, s_sc.n, s_mx.n, s_ax.n, s_msk.n
                PT0, PTE, AV, OU, RS = s_pt.n, s_pte.n, s_av.n, s_ou.n, s_rs.n
                KV = [x.n for x in s_kv]
                XO = [x.n for x in s_xo]
                X1 = [x.n for x in s_x1]
                GT = [x.n for x in s_gt]
                @blk.sync
                def _(sy):
                    if not SIM_SKIP_CC:
                        sy.wait_ge(ctx.sem("cc1").h, 1)
                    sy.dma_start(qseg[:], qf_d.ap()[s]).then_inc(s_q.bump(16), 16)
                    for t in range(8):
                        qq = 8 * b + t
                        rk = min(qq, 15 - qq)
                        sp = 0 if rk == qq else 1
                        if t >= 2:
                            sy.wait_ge(s_sc.h, SC + 8 * (t - 1))
                            sy.wait_ge(s_av.h, AV + 8 * (t - 1))
                        sy.dma_start(kbuf[:, t % 2, :, :],
                                     cc1d.ap()[rk, sp, 0]).then_inc(s_kv[t % 2].bump(16), 16)
                        sy.dma_start(vbuf[:, t % 2, :, :],
                                     cc1d.ap()[rk, sp, 1]).then_inc(s_kv[t % 2].bump(16), 16)
                    for k in range(8):
                        if k >= 2:
                            sy.wait_ge(s_rs.h, RS + k - 1)
                        sy.dma_start(x1t[:, k % 2, :],
                                     x1_d.ap()[s * SEG + k * 128:s * SEG + (k + 1) * 128, :]
                                     ).then_inc(s_x1[k % 2].bump(16), 16)
                        sy.wait_ge(s_rs.h, RS + k + 1)
                        sy.dma_start(out_d.ap()[s * SEG + k * 128:s * SEG + (k + 1) * 128, :],
                                     osb[:, k % 2, :]).then_inc(s_xo[k % 2].bump(16), 16)
                    sy.wait_ge(s_xo[0].h, XO[0] + 64)
                    sy.wait_ge(s_xo[1].h, XO[1] + 64)
                @blk.gpsimd
                def _(gp_):
                    for k in range(8):
                        if k >= 2:
                            gp_.wait_ge(s_rs.h, RS + k - 1)
                        gp_.dma_start(g2t[:, k % 2, :], g1_d.ap()[:, s * 8 + k, :]).then_inc(s_gt[k % 2].bump(16), 16)
                @blk.tensor
                def _(t_):
                    def scores(i):
                        t, k = divmod(i, 8)
                        t_.wait_ge(s_kv[t % 2].h, KV[t % 2] + 32 * (t // 2 + 1))
                        if i == 0:
                            t_.wait_ge(s_q.h, Q0 + 16)
                        if i >= 2:
                            t_.wait_ge(s_ax.h, AX0 + i - 1)
                        last = None
                        for ch in range(2):
                            for dc in range(8):
                                last = t_.matmul(psc[:, i % 2, ch * 512:(ch + 1) * 512],
                                          qseg[:, dc, k * 128:(k + 1) * 128],
                                          kbuf[:, (i // 8) % 2, dc, ch * 512:(ch + 1) * 512],
                                          start=(dc == 0), stop=(dc == 7))
                        last.then_inc(s_sc.bump(), 1)
                    scores(0)
                    scores(1)
                    for i in range(N):
                        if i + 2 < N:
                            scores(i + 2)
                        t_.wait_ge(s_msk.h, MSK + i + 1)
                        if i >= 1:
                            t_.wait_ge(s_pte.h, PTE + i)
                        last = None
                        for jb in range(8):
                            last = t_.transpose(ppt[:, jb, :], P[:, i % 2, jb * 128:(jb + 1) * 128], idb[:])
                        last.then_inc(s_pt.bump(), 1)
                        t_.wait_ge(s_pte.h, PTE + i + 1)
                        if i >= 1:
                            t_.wait_ge(s_ou.h, OU + i)
                        last = None
                        for ot in range(2):
                            for jb in range(8):
                                last = t_.matmul(pav[:, ot, :], PT[:, jb, :],
                                          vbuf[:, (i // 8) % 2, jb, ot * 512:(ot + 1) * 512],
                                          start=(jb == 0), stop=(jb == 7))
                        last.then_inc(s_av.bump(), 1)
                @blk.vector
                def _(v):
                    for k in range(8):
                        v.memset(Oall[:, k, :], 0.0)
                    v.memset(Mall[:], NEGBIG)
                    v.memset(Dall[:], 0.0)
                    def mxblock(i):
                        k = i % 8
                        v.wait_ge(s_sc.h, SC + i + 1)
                        if i >= 8:
                            v.wait_ge(s_ax.h, AX0 + i - 7)
                        v.reduce_max(mxc[:, i % 2:i % 2 + 1], psc[:, i % 2, :], axis=AX)
                        v.drain()
                        v.tensor_scalar(mxc[:, i % 2:i % 2 + 1], mxc[:, i % 2:i % 2 + 1],
                                        1.0 / 32.0, None, OP.mult)
                        v.drain()
                        v.tensor_tensor(mnew[:, k:k + 1], Mall[:, k:k + 1],
                                        mxc[:, i % 2:i % 2 + 1], OP.max)
                        v.drain()
                        v.tensor_scalar(nmn[:, k:k + 1], mnew[:, k:k + 1], -1.0,
                                        None, OP.mult).then_inc(s_mx.bump(), 1)
                    mxblock(0)
                    for i in range(N):
                        t, k = divmod(i, 8)
                        if i + 1 < N:
                            mxblock(i + 1)
                        v.wait_ge(s_ax.h, AX0 + i + 1)
                        v.scalar_tensor_tensor(P[:, i % 2, :], g["iota_pj"][:],
                                               g["thresh"][:, s * 64 + k * 8 + t:s * 64 + k * 8 + t + 1],
                                               P[:, i % 2, :], OP.is_ge, OP.mult).then_inc(s_msk.bump(), 1)
                        v.drain()
                        v.reduce_sum(psc_[:, i % 2:i % 2 + 1], P[:, i % 2, :], axis=AX)
                        v.drain()
                        v.scalar_tensor_tensor(Dall[:, k:k + 1], Dall[:, k:k + 1],
                                               resc[:, k:k + 1], psc_[:, i % 2:i % 2 + 1],
                                               OP.mult, OP.add)
                        v.wait_ge(s_av.h, AV + i + 1)
                        for ot in range(2):
                            v.scalar_tensor_tensor(Oall[:, k, ot * 512:(ot + 1) * 512],
                                                   Oall[:, k, ot * 512:(ot + 1) * 512],
                                                   resc[:, k:k + 1], pav[:, ot, :],
                                                   OP.mult, OP.add)
                        v.drain()
                        v.tensor_copy(Mall[:, k:k + 1], mnew[:, k:k + 1]).then_inc(s_ou.bump(), 1)
                    for k in range(8):
                        v.wait_ge(s_gt[k % 2].h, GT[k % 2] + 16 * (k // 2 + 1))
                        v.wait_ge(s_x1[k % 2].h, X1[k % 2] + 16 * (k // 2 + 1))
                        if k >= 2:
                            v.wait_ge(s_xo[k % 2].h, XO[k % 2] + 16 * (k // 2))
                        v.reciprocal(mxc[:, 0:1], Dall[:, k:k + 1])
                        v.drain()
                        v.tensor_scalar(scr[:], Oall[:, k, :], mxc[:, 0:1], None, OP.mult)
                        v.drain()
                        v.tensor_tensor(scr[:], scr[:], g2t[:, k % 2, :], OP.mult)
                        v.drain()
                        v.tensor_tensor(osb[:, k % 2, :], scr[:], x1t[:, k % 2, :],
                                        OP.add).then_inc(s_rs.bump(), 1)
                @blk.scalar
                def _(sc):
                    for i in range(N):
                        t, k = divmod(i, 8)
                        sc.wait_ge(s_mx.h, MX + i + 1)
                        sc.activation(resc[:, k:k + 1], Mall[:, k:k + 1], AF.Exp,
                                      bias=nmn[:, k:k + 1])
                        sc.activation(P[:, i % 2, :], psc[:, i % 2, :], AF.Exp, scale=1.0 / 32.0,
                                      bias=nmn[:, k:k + 1]).then_inc(s_ax.bump(), 1)
                        if i >= 1:
                            sc.wait_ge(s_pt.h, PT0 + i)
                            if i >= 2:
                                sc.wait_ge(s_av.h, AV + i - 1)
                            sc.copy(PT[:], ppt[:]).then_inc(s_pte.bump(), 1)
                    sc.wait_ge(s_pt.h, PT0 + N)
                    sc.wait_ge(s_av.h, AV + N - 1)
                    sc.copy(PT[:], ppt[:]).then_inc(s_pte.bump(), 1)



_BUILD_CACHE = {}


def build_program(upto="attn", taps=()):
    key = (upto, tuple(sorted(taps)))
    if key in _BUILD_CACHE:
        return _BUILD_CACHE[key]
    nc = bass.Bass("TRN2", target_bir_lowering=False)
    es = contextlib.ExitStack()
    ctx = Ctx(nc, es, taps)
    with es:
        x_in = ctx.din("x", [TOK, D])
        emit_setup(ctx)
        phases = ["ln1", "mixa", "moe0", "mixb", "ln2", "qkv", "moe1", "attn"]
        stop = phases.index({"mixp": "mixa", "mixn": "mixa", "mixb1": "mixb", "mixb2": "mixb", "mixb3": "mixb", "mixb4": "mixb", "mixb5": "mixb", "mixb6": "mixb"}.get(upto, upto))
        xn_fm = ctx.dtmp("xn_fm", [8, 128, TOK], BF16)
        emit_ln(ctx, x_in, xn_fm)
        if stop >= 1:
            emit_mixer_a(ctx, parts={"mixp": "prod", "mixn": "nocc"}.get(upto, "all"))
        if stop >= 2:
            emit_moe(ctx, 0)
        if stop >= 3:
            emit_mixer_b(ctx, parts={"mixb1": "m1", "mixb2": "m2", "mixb3": "m3rg", "mixb4": "m3ch", "mixb5": "m3c1", "mixb6": "m3c2"}.get(upto, "all"))
        if stop >= 4:
            emit_ln(ctx, ctx.dtmp("x1", [TOK, D], F32), xn_fm)
        if stop >= 5:
            emit_qkv(ctx)
        if stop >= 6:
            emit_moe(ctx, 1)
        if stop >= 7:
            emit_attn(ctx)
        if 1 <= stop < 3 and upto not in ("mixp", "mixn") and not SIM_SKIP_CC:
            with nc.Block() as blk:
                @blk.gpsimd
                def _(gp_):
                    gp_.wait_ge(ctx.sem("cc0").h, 1)
        if 5 <= stop < 7 and not SIM_SKIP_CC:
            with nc.Block() as blk:
                @blk.gpsimd
                def _(gp_):
                    gp_.wait_ge(ctx.sem("cc1").h, 1)
    innames = [k for k, v in ctx.kinds.items() if v == "in"]
    _BUILD_CACHE[key] = (nc, innames)
    return nc, innames


def kernel(**inputs):
    nc, innames = build_program("attn")
    in_maps = prep_in_maps(inputs)
    in_maps = [{k: m[k] for k in innames} for m in in_maps]
    res = run_bass_kernel_spmd(nc, in_maps, core_ids=list(range(NCORE)))
    return scatter_out(res.results)
